# version 22
# baseline (speedup 1.0000x reference)
import contextlib
import numpy as np
import concourse.bass as bass
import concourse.mybir as mybir
from concourse.bass_utils import run_bass_kernel_spmd

F32 = mybir.dt.float32
BF16 = mybir.dt.bfloat16
AF = mybir.ActivationFunctionType
ALU = mybir.AluOpType
AX = mybir.AxisListType

NCORES = 8
T = 2048
D = 2048
NH = 16
HD = 128
BLK = 256
NB = T // BLK
NTT = T // 128
NFC = D // 128
NTG = T // 512
NBITS = 4

ENGS = ("T", "V", "S", "Q", "G")

_CO = {}
_off = 0
for _name, _w in [("id", 128), ("rampq", BLK), ("rampmn", 2 * BLK), ("rampk", 2),
                  ("normw", NH), ("slope", NH), ("pb2", NH * NBITS),
                  ("nb2", NH * NBITS), ("exps", NBITS), ("mw", NBITS),
                  ("bexps", NBITS), ("scale", 1), ("bscale", 1)]:
    _CO[_name] = (_off, _off + _w)
    _off += _w
CW = _off


class Sched:

    def __init__(self):
        self.ops = {e: [] for e in ENGS}
        self.cnt = {}
        self.sem_names = set(["T", "V", "S"])

    def op(self, eng, fn, waits=(), inc=False, sem=None, hard=()):
        if eng in ("Q", "G"):
            amt = 16
            assert sem is not None or not inc
        else:
            amt = 1
            sem = eng
        ws = []
        for w in waits:
            if w is None:
                continue
            pse, val = w
            if eng not in ("Q", "G") and pse == eng:
                continue
            ws.append((pse, val))
        for w in hard:
            if w is None:
                continue
            ws.append(tuple(w))
        self.ops[eng].append((ws, fn, inc, sem))
        if inc:
            self.sem_names.add(sem)
            self.cnt[sem] = self.cnt.get(sem, 0) + amt
            return (sem, self.cnt[sem])
        return None

    def replay(self, eng_name, eng_obj, sems):
        maxw = {}
        for ws, fn, inc, sem in self.ops[eng_name]:
            for pse, val in ws:
                if maxw.get(pse, 0) >= val:
                    continue
                maxw[pse] = val
                eng_obj.wait_ge(sems[pse], val)
            inst = fn(eng_obj)
            if inc:
                amt = 16 if eng_name in ("Q", "G") else 1
                if inst is None:
                    inst = eng_obj.nop()
                inst.then_inc(sems[sem], amt)


def build_program(upto="D", c_heads=NH):
    nc = bass.Bass("TRN2", target_bir_lowering=False)

    def din(name, shape):
        return nc.dram_tensor(name, list(shape), F32, kind="ExternalInput").ap()

    x_ap = din("x", (T, D))
    xT_ap = din("xT", (D, T))
    wqkvT_ap = din("wqkvT", (NH, D, 384))
    wgateT_ap = din("wgateT", (NH, D, HD))
    woutT_ap = din("woutT", (D, D))
    pwT_ap = din("pwT", (NBITS, D, D))
    nwT_ap = din("nwT", (NBITS, D, D))
    consts_ap = din("consts", (128, CW))

    outT_ap = nc.dram_tensor("outT", [D, T], F32, kind="ExternalOutput").ap()

    g_dram = nc.dram_tensor("g_scr", [NFC, 128, T], BF16, kind="Internal").ap()
    gate_dram = nc.dram_tensor("gate_scr", [NH, 128, T], BF16, kind="Internal").ap()
    o_dram = nc.dram_tensor("o_scr", [NH, 128, T], BF16, kind="Internal").ap()

    sch = Sched()
    es = contextlib.ExitStack()

    def sb(name, shape, dtype):
        return es.enter_context(nc.sbuf_tensor(name, list(shape), dtype))

    def ps(name, shape, dtype):
        return es.enter_context(nc.psum_tensor(name, list(shape), dtype))

    csb = sb("csb", (128, CW), F32)

    def cc(name):
        a, b = _CO[name]
        return csb[:, a:b]

    id_f32 = cc("id")
    rampq_sb = cc("rampq")
    rampmn_sb = cc("rampmn")
    rampk_sb = cc("rampk")
    normw_sb = cc("normw")
    slope_sb = cc("slope")
    pb2_sb = cc("pb2")
    nb2_sb = cc("nb2")

    id_bf = sb("id_bf", (128, 128), BF16)
    ones_bf = sb("ones_bf", (128, 1), BF16)
    ones1 = sb("ones1", (1, 128), F32)
    biasW = sb("biasW", (128, NFC), F32)
    sumsq = sb("sumsq", (1, T), F32)
    eps_sb = sb("eps_sb", (128, 2), F32)
    cpos = sb("cpos", (128, NBITS), F32)
    cneg = sb("cneg", (128, NBITS), F32)
    sgm = sb("sgm", (128, NBITS), F32)
    join_sb = sb("join_sb", (1, 8), F32)

    pfA = [ps(f"pfA{i}", (128, 512), F32) for i in range(2)]
    pS = [ps(f"pS{i}", (128, 512), F32) for i in range(2)]
    pO = [ps(f"pO{i}", (128, 512), F32) for i in range(2)]
    pT = [ps(f"pT{i}", (128, 1024), BF16) for i in range(2)]

    t_cst = sch.op("Q", lambda e: e.dma_start(out=csb[:], in_=consts_ap),
                   inc=True, sem="cst")
    t_idbf = sch.op("V", lambda e: e.tensor_copy(out=id_bf[:], in_=id_f32),
                    waits=[t_cst], inc=True)
    sch.op("V", lambda e: e.memset(ones_bf[:], 1.0))
    sch.op("V", lambda e: e.memset(ones1[:], 1.0))
    sch.op("V", lambda e: e.memset(eps_sb[:, 0:1], 1e-5))
    sch.op("V", lambda e: e.memset(eps_sb[:, 1:2], 1e-6))
    sch.op("V", lambda e: e.memset(sumsq[:], 0.0))

    esABC = contextlib.ExitStack()
    xnT = esABC.enter_context(nc.sbuf_tensor("xnT", [128, NFC * T], BF16))
    esPN = contextlib.ExitStack()
    pnbg = [esPN.enter_context(nc.sbuf_tensor(f"pnbg{i}", [128, D], BF16))
            for i in range(2)]
    pnq = [esPN.enter_context(nc.sbuf_tensor(f"pnq{i}", [128, D], F32))
           for i in range(2)]
    sgp = [esPN.enter_context(nc.sbuf_tensor(f"sgp{i}", [128, D], BF16))
           for i in range(2)]
    sgn = [esPN.enter_context(nc.sbuf_tensor(f"sgn{i}", [128, D], BF16))
           for i in range(2)]
    Wacc = esPN.enter_context(nc.sbuf_tensor("Wacc", [128, D], F32))
    with contextlib.ExitStack() as esA:
        def sbA(name, shape, dtype):
            return esA.enter_context(nc.sbuf_tensor(name, list(shape), dtype))

        xtile = [sbA(f"xtile{i}", (128, D), F32) for i in range(3)]
        xntok = [sbA(f"xntok{i}", (128, D), BF16) for i in range(2)]
        junkA = sbA("junkA", (128, D), BF16)
        mneg_c = sbA("mneg_c", (128, NTT), F32)
        vss_c = sbA("vss_c", (128, NTT), F32)
        y_c = sbA("y_c", (128, NTT), F32)
        s_c = sbA("s_c", (128, NTT), F32)
        r_c = sbA("r_c", (128, NTT), F32)
        t_c = sbA("t_c", (128, NTT), F32)

        tload = [None, None, None]
        tsq = {}
        txn = {}
        tev = {}

        def a_stage1(tt):
            b = tt % 3
            tload[b] = sch.op("Q", (lambda tt, b: lambda e: e.dma_start(
                out=xtile[b][:], in_=x_ap[tt * 128:(tt + 1) * 128, :]))(tt, b),
                waits=[txn.get(tt - 3)], inc=True, sem=f"xa{b}")
            tred = sch.op("V", (lambda tt, b: lambda e: e.tensor_reduce(
                out=mneg_c[:, tt:tt + 1], in_=xtile[b][:], axis=AX.X, op=ALU.add))(tt, b),
                waits=[tload[b]], inc=True)
            tr2 = sch.op("V", (lambda tt: lambda e: e.tensor_scalar_mul(
                out=mneg_c[:, tt:tt + 1], in0=mneg_c[:, tt:tt + 1],
                scalar1=-1.0 / D))(tt), hard=[tred], inc=True)
            tsq[tt] = sch.op("S", (lambda tt, b: lambda e: e.activation(
                out=junkA[:], in_=xtile[b][:], func=AF.Square,
                bias=mneg_c[:, tt:tt + 1], accum_out=vss_c[:, tt:tt + 1]))(tt, b),
                waits=[tr2], hard=[tsq.get(tt - 1)], inc=True)

        def a_stage2(tt):
            b = tt % 3
            b2 = tt % 2
            tyv = sch.op("V", (lambda tt: lambda e: e.tensor_scalar(
                out=y_c[:, tt:tt + 1], in0=vss_c[:, tt:tt + 1],
                scalar1=1.0 / D, scalar2=1e-5,
                op0=ALU.mult, op1=ALU.add))(tt), waits=[tsq[tt]], inc=True)
            tsr = sch.op("S", (lambda tt: lambda e: e.activation(
                out=s_c[:, tt:tt + 1], in_=y_c[:, tt:tt + 1],
                func=AF.Sqrt))(tt), waits=[tyv], inc=True)
            tn = sch.op("V", (lambda tt: lambda e: e.reciprocal(
                out=r_c[:, tt:tt + 1], in_=s_c[:, tt:tt + 1]))(tt),
                waits=[tsr], inc=True)
            tn = sch.op("V", (lambda tt: lambda e: e.tensor_mul(
                out=t_c[:, tt:tt + 1], in0=r_c[:, tt:tt + 1],
                in1=r_c[:, tt:tt + 1]))(tt), hard=[tn], inc=True)
            tn = sch.op("V", (lambda tt: lambda e: e.tensor_mul(
                out=t_c[:, tt:tt + 1], in0=t_c[:, tt:tt + 1],
                in1=y_c[:, tt:tt + 1]))(tt), hard=[tn], inc=True)
            tn = sch.op("V", (lambda tt: lambda e: e.tensor_scalar(
                out=t_c[:, tt:tt + 1], in0=t_c[:, tt:tt + 1],
                scalar1=-0.5, scalar2=1.5,
                op0=ALU.mult, op1=ALU.add))(tt), hard=[tn], inc=True)
            tn = sch.op("V", (lambda tt: lambda e: e.tensor_mul(
                out=r_c[:, tt:tt + 1], in0=r_c[:, tt:tt + 1],
                in1=t_c[:, tt:tt + 1]))(tt), hard=[tn], inc=True)
            txn[tt] = sch.op("V", (lambda tt, b, b2: lambda e: e.tensor_scalar(
                out=xntok[b2][:], in0=xtile[b][:],
                scalar1=mneg_c[:, tt:tt + 1], scalar2=r_c[:, tt:tt + 1],
                op0=ALU.add, op1=ALU.mult))(tt, b, b2),
                hard=[tn, tev.get((tt - 2, 1))], inc=True)
            for half in range(2):
                def trfnA(b2, half, tt):
                    def f(e):
                        r = None
                        for s in range(8):
                            fc = half * 8 + s
                            r = e.transpose(pT[half][:, s * 128:(s + 1) * 128],
                                            xntok[b2][:, fc * 128:(fc + 1) * 128],
                                            id_bf[:])
                        return r
                    return f
                tk = sch.op("T", trfnA(b2, half, tt),
                    waits=[txn[tt], t_idbf, tev.get((tt - 1, half))], inc=True)
                xv = xnT[:].rearrange("p (c t) -> p c t", t=T)
                tev[(tt, half)] = sch.op("V",
                    (lambda tt, half, xv: lambda e: e.tensor_copy(
                        out=xv[:, half * 8:(half + 1) * 8, tt * 128:(tt + 1) * 128],
                        in_=pT[half][:].rearrange("p (s t) -> p s t", t=128)))(tt, half, xv),
                    waits=[tk], inc=True)

        for tt in range(NTT + 1):
            if tt < NTT:
                a_stage1(tt)
            if tt >= 1:
                a_stage2(tt - 1)
        tok_xn = sch.op("V", lambda e: e.tensor_copy(
            out=join_sb[0:1, 3:4], in_=ones1[0:1, 0:1]),
            hard=[tev[(NTT - 1, 0)], tev[(NTT - 1, 1)]], inc=True)

    tok_gsd = {}
    with contextlib.ExitStack() as esB:
        def sbB(name, shape, dtype):
            return esB.enter_context(nc.sbuf_tensor(name, list(shape), dtype))

        WffnT = sbB("WffnT", (128, NFC * D), BF16)
        wg_sb = [sbB(f"wg_sb{i}", (128, NFC * HD), BF16) for i in range(2)]
        gate_st = [sbB("gate_stA", (128, T), BF16)]
        g_st = [sbB(f"g_st{i}", (128, 512), BF16) for i in range(2)]
        btmp = sbB("btmp", (128, NH * NBITS), F32)
        bacc = sbB("bacc", (128, NFC), F32)
        br = sbB("br", (128, NFC), F32)

        tcs = sch.op("S", lambda e: e.activation(
            out=sgm[:], in_=cc("mw"), func=AF.Sigmoid), waits=[t_cst], inc=True)
        tc1 = sch.op("V", lambda e: e.tensor_mul(
            out=cpos[:], in0=sgm[:], in1=cc("exps")), waits=[tcs], inc=True)
        tc2 = sch.op("V", lambda e: e.tensor_scalar_mul(
            out=cpos[:], in0=cpos[:], scalar1=cc("scale")), hard=[tc1], inc=True)
        tcneg = sch.op("V", lambda e: e.tensor_scalar_mul(
            out=cneg[:], in0=cpos[:], scalar1=-1.0), hard=[tc2], inc=True)

        tb0 = sch.op("V", lambda e: e.tensor_sub(
            out=btmp[:], in0=pb2_sb, in1=nb2_sb),
            waits=[t_cst, tok_xn], inc=True)
        btr = btmp[:].rearrange("p (g n) -> p n g", n=NBITS)
        tb1 = tb0
        for n in range(NBITS):
            bx = cc("bexps")
            if n == 0:
                tb1 = sch.op("V", (lambda n, bx: lambda e: e.tensor_scalar_mul(
                    out=bacc[:], in0=btr[:, n, :], scalar1=bx[:, n:n + 1]))(n, bx),
                    hard=[tb1], inc=True)
            else:
                tb1 = sch.op("V", (lambda n, bx: lambda e: e.scalar_tensor_tensor(
                    out=bacc[:], in0=btr[:, n, :], scalar=bx[:, n:n + 1],
                    in1=bacc[:], op0=ALU.mult, op1=ALU.add))(n, bx),
                    hard=[tb1], inc=True)
        tb1 = sch.op("V", lambda e: e.tensor_scalar(
            out=bacc[:], in0=bacc[:], scalar1=1.0, scalar2=-1.0,
            op0=ALU.min, op1=ALU.max), hard=[tb1], inc=True)
        tb2 = sch.op("S", lambda e: e.activation(
            out=br[:], in_=bacc[:], func=AF.Abs, scale=15.0), waits=[tb1], inc=True)
        tb3 = sch.op("V", lambda e: e.tensor_single_scalar(
            out=btmp[:, 0:NFC], in_=br[:], scalar=0.5, op=ALU.is_ge),
            waits=[tb2], inc=True)
        for k in range(1, 15):
            tb3 = sch.op("V", (lambda k: lambda e: e.scalar_tensor_tensor(
                out=btmp[:, 0:NFC], in0=br[:], scalar=k + 0.5, in1=btmp[:, 0:NFC],
                op0=ALU.is_ge, op1=ALU.add))(k), hard=[tb3], inc=True)
        tb4 = sch.op("S", lambda e: e.activation(
            out=br[:], in_=bacc[:], func=AF.Sign), waits=[tb3], hard=[tb2], inc=True)
        tbm = sch.op("V", lambda e: e.tensor_mul(
            out=btmp[:, 0:NFC], in0=btmp[:, 0:NFC], in1=br[:]),
            waits=[tb4], hard=[tb3], inc=True)
        tbias = sch.op("V", lambda e: e.tensor_scalar(
            out=biasW[:], in0=btmp[:, 0:NFC], scalar1=1.0 / 15.0,
            scalar2=cc("bscale"), op0=ALU.mult, op1=ALU.mult),
            hard=[tbm], inc=True)

        tok_wg = {}
        tok_gmm = {}
        tok_gst = {}
        tok_evac = {}
        tok_pn = {}
        tok_sig = {}
        tok_comb = {}
        tok_wob = {}
        tok_gev = {}
        tok_gse = {}

        def emit_wg_load(h):
            b = h % 2
            tok_wg[h] = sch.op("G", (lambda h, b: lambda e: e.dma_start(
                out=wg_sb[b][:].rearrange("p (c j) -> p c j", j=HD),
                in_=wgateT_ap[h].rearrange("(c p) j -> p c j", p=128)))(h, b),
                waits=[tok_gmm.get((h - 2, NTG - 1)),
                       tok_xn if h < 2 else None], inc=True, sem=f"wg{b}")

        def emit_gate_head(h):
            b = h % 2
            gb = 0
            for tg in range(NTG):
                cidx = h * 8 + tg
                pbank = (pfA + pS)[cidx % 4]

                def mmfn(h, b, tg, pbank):
                    def f(e):
                        for fc in range(NFC):
                            r = e.matmul(pbank[:],
                                     wg_sb[b][:, fc * HD:(fc + 1) * HD],
                                     xnT[:, fc * T + tg * 512: fc * T + tg * 512 + 512],
                                     start=(fc == 0), stop=(fc == NFC - 1))
                        return r
                    return f
                tok_gmm[(h, tg)] = sch.op("T", mmfn(h, b, tg, pbank),
                    waits=[tok_wg[h], tok_xn,
                           tok_evac.get(cidx - 4)], inc=True)
                tok_gst[h * NTG + tg] = sch.op("S",
                    (lambda h, gb, tg, pbank: lambda e: e.activation(
                        out=gate_st[gb][:, tg * 512:(tg + 1) * 512], in_=pbank[:],
                        func=AF.Sigmoid))(h, gb, tg, pbank),
                    waits=[tok_gmm[(h, tg)], tok_gsd.get(h - 1)], inc=True)
                tok_evac[cidx] = tok_gst[h * NTG + tg]
            tok_gsd[h] = sch.op("Q", (lambda h, gb: lambda e: e.dma_start(
                out=gate_dram[h], in_=gate_st[gb][:]))(h, gb),
                waits=[tok_gst[h * NTG + NTG - 1]], inc=True, sem="gsd0")

        def emit_pn_ob(ob):
            prev_cast = tok_wob.get(ob - 1)
            for n in range(NBITS):
                u = ob * NBITS + n
                sl = u % 2
                sp = u % 2
                tok_pn[(u, 0)] = sch.op("G",
                    (lambda ob, n, sl: lambda e: e.dma_start(
                        out=pnbg[sl][:].rearrange("p (c o) -> p c o", o=128),
                        in_=pwT_ap[n][:, ob * 128:(ob + 1) * 128].rearrange(
                            "(c p) o -> p c o", p=128)))(ob, n, sl),
                    waits=[tok_sig.get((u - 2, 0))],
                    inc=True, sem=f"pn{sl}")
                tok_pn[(u, 1)] = sch.op("Q",
                    (lambda ob, n, sl: lambda e: e.dma_start(
                        out=pnq[sl][:].rearrange("p (c o) -> p c o", o=128),
                        in_=nwT_ap[n][:, ob * 128:(ob + 1) * 128].rearrange(
                            "(c p) o -> p c o", p=128)))(ob, n, sl),
                    waits=[tok_sig.get((u - 2, 1))],
                    inc=True, sem=f"pq{sl}")
                tok_sig[(u, 0)] = sch.op("S",
                    (lambda sl, sp: lambda e: e.activation(
                        out=sgp[sp][:], in_=pnbg[sl][:], func=AF.Sigmoid))(sl, sp),
                    waits=[tok_pn[(u, 0)], tok_comb.get((u - 2, 1))], inc=True)
                tok_sig[(u, 1)] = sch.op("S",
                    (lambda sl, sp: lambda e: e.activation(
                        out=sgn[sp][:], in_=pnq[sl][:], func=AF.Sigmoid))(sl, sp),
                    waits=[tok_pn[(u, 1)], tok_comb.get((u - 2, 0))], inc=True)
                tok_comb[(u, 0)] = sch.op("V",
                    (lambda sp: lambda e: e.tensor_sub(
                        out=sgp[sp][:], in0=sgp[sp][:], in1=sgn[sp][:]))(sp),
                    waits=[tok_sig[(u, 0)], tok_sig[(u, 1)]],
                    hard=[tok_comb.get((u - 2, 1))], inc=True)
                if n == 0:
                    tok_comb[(u, 1)] = sch.op("V",
                        (lambda sp: lambda e: e.tensor_scalar_mul(
                            out=Wacc[:], in0=sgp[sp][:],
                            scalar1=cpos[:, 0:1]))(sp),
                        hard=[tok_comb[(u, 0)], tcneg, prev_cast], inc=True)
                else:
                    tok_comb[(u, 1)] = sch.op("V",
                        (lambda n, sp: lambda e: e.scalar_tensor_tensor(
                            out=Wacc[:], in0=sgp[sp][:],
                            scalar=cpos[:, n:n + 1],
                            in1=Wacc[:],
                            op0=ALU.mult, op1=ALU.add))(n, sp),
                        hard=[tok_comb[(u, 0)], tok_comb.get((u - 1, 1))], inc=True)
            tok_wob[ob] = sch.op("V", (lambda ob: lambda e: e.tensor_copy(
                out=WffnT[:, ob * D:(ob + 1) * D], in_=Wacc[:]))(ob),
                waits=[tok_xn],
                hard=[tok_comb[(ob * NBITS + NBITS - 1, 1)]], inc=True)

        def emit_ffn_ob(fc):
            for tg in range(NTG):
                cidx = fc * 8 + 4 + tg
                gidx = fc * NTG + tg
                pbank = (pfA + pS)[cidx % 4]

                def mmfn(fc, tg, pbank):
                    def f(e):
                        for ic in range(NFC):
                            r = e.matmul(pbank[:],
                                     WffnT[:, fc * D + ic * 128: fc * D + ic * 128 + 128],
                                     xnT[:, ic * T + tg * 512: ic * T + tg * 512 + 512],
                                     start=(ic == 0), stop=(ic == NFC - 1))
                        return r
                    return f
                tmm_ = sch.op("T", mmfn(fc, tg, pbank),
                    waits=[tok_wob[fc], tok_xn, tok_evac.get(cidx - 4)], inc=True)
                bse = gidx % 2
                tok_gse[gidx] = sch.op("V",
                    (lambda fc, tg, pbank, bse: lambda e: e.tensor_scalar(
                        out=g_st[bse][:], in0=pbank[:],
                        scalar1=biasW[:, fc:fc + 1], scalar2=1.0,
                        op0=ALU.add, op1=ALU.mult))(fc, tg, pbank, bse),
                    waits=[tmm_, tok_gev.get(gidx - 2)],
                    hard=[tbias], inc=True)
                tok_evac[cidx] = tok_gse[gidx]
                tok_gev[gidx] = sch.op("Q",
                    (lambda fc, tg, bse: lambda e: e.dma_start(
                        out=g_dram[fc, :, tg * 512:(tg + 1) * 512], in_=g_st[bse][:]))(fc, tg, bse),
                    waits=[tok_gse[gidx]], inc=True, sem=f"gev{bse}")

        emit_pn_ob(0)
        emit_pn_ob(1)
        emit_wg_load(0)
        emit_wg_load(1)
        for step in range(NH):
            emit_gate_head(step)
            if step + 2 < NH:
                emit_wg_load(step + 2)
            if step + 2 < NFC:
                emit_pn_ob(step + 2)
            emit_ffn_ob(step)
        tok_ffn_done = [tok_gev[NFC * NTG - 1], tok_gev[NFC * NTG - 2],
                        tok_gse[NFC * NTG - 1]]

    if upto == "BG":
        fin = sch.op("G", lambda e: e.dma_start(
            out=outT_ap[0:128, 0:512], in_=xnT[:, 0:512]),
            waits=tok_ffn_done + [tok_gsd[NH - 1]],
            inc=True, sem="fin")
        sch.op("V", lambda e: e.nop(), waits=[fin])
        return _finish(nc, sch, es, esABC)

    esPN.close()

    with contextlib.ExitStack() as esC:
        def sbC(name, shape, dtype):
            return esC.enter_context(nc.sbuf_tensor(name, list(shape), dtype))

        wq_sb = [sbC(f"wq_sb{i}", (128, NFC * 384), BF16) for i in range(2)]
        gate_h = [sbC(f"gate_h{i}", (128, T), BF16) for i in range(2)]
        q_dim = [sbC(f"q_dim{i}", (128, T), BF16) for i in range(2)]
        k_dim = [sbC(f"k_dim{i}", (128, T), BF16) for i in range(2)]
        v_dimb = sbC("v_dim", (128, T), BF16)
        ktok_raw = sbC("ktok_raw", (128, T), BF16)
        k_td = [sbC(f"k_td{i}", (128, T), BF16) for i in range(2)]
        v_tok = [sbC(f"v_tok{i}", (128, T), BF16) for i in range(2)]
        qdec_b = sbC("qdec_b", (128, 512), BF16)
        sT_b = sbC("sT_b", (128, 1024), BF16)
        sq_b = sbC("sq_b", (128, 512), BF16)
        kv_sb = sbC("kv_sb", (128, 128), F32)
        kv_bf = sbC("kv_bf", (128, 128), BF16)
        O_st = [sbC(f"O_st{i}", (128, T), BF16) for i in range(2)]
        qdec_all = sbC("qdec_all", (128, NH * BLK), BF16)
        decT_all = sbC("decT_all", (128, NH * 2 * BLK), BF16)
        kdec_all = sbC("kdec_all", (128, 2 * NH), F32)
        bd_all = sbC("bd_all", (128, NH), F32)
        dtmp = sbC("dtmp", (128, 2 * BLK), F32)

        tok_wq = {}
        tok_gld = {}
        tok_pev = {}
        tok_tr = {}
        tok_ktd = {}
        tok_vev = {}
        tok_qdecb = {}
        tok_og = {}
        tok_kvstt = {}
        tok_kvcast = {}
        tok_ssacc = {}
        tok_sq = {}
        tok_ost = {}
        tok_prod = {}
        tok_tst = {}

        tvjoinC = sch.op("V", lambda e: e.tensor_copy(
            out=join_sb[0:1, 0:1], in_=ones1[0:1, 0:1]),
            waits=tok_ffn_done + [tok_gsd[NH - 1]],
            inc=True)

        tok_decay = None
        tv_ = None
        ts3 = None
        for h in range(NH):
            w0 = [t_cst, tvjoinC, ts3,
                  tok_gse[NFC * NTG - 1] if h == 0 else None]
            tv_ = sch.op("V", (lambda h: lambda e: e.tensor_scalar_mul(
                out=dtmp[:, 0:BLK], in0=rampq_sb, scalar1=slope_sb[:, h:h + 1]))(h),
                waits=w0, hard=[tv_], inc=True)
            ts1 = sch.op("S", (lambda h: lambda e: e.activation(
                out=qdec_all[:, h * BLK:(h + 1) * BLK], in_=dtmp[:, 0:BLK],
                func=AF.Exp, scale=-1.0))(h), waits=[tv_], inc=True)
            tv_ = sch.op("V", (lambda h: lambda e: e.tensor_scalar_mul(
                out=dtmp[:, 0:2 * BLK], in0=rampmn_sb,
                scalar1=slope_sb[:, h:h + 1]))(h), waits=[ts1], inc=True)
            ts2 = sch.op("S", (lambda h: lambda e: e.activation(
                out=decT_all[:, h * 2 * BLK:(h + 1) * 2 * BLK], in_=dtmp[:, 0:2 * BLK],
                func=AF.Exp, scale=-1.0))(h), waits=[tv_], inc=True)
            tv_ = sch.op("V", (lambda h: lambda e: e.tensor_scalar_mul(
                out=dtmp[:, 0:2], in0=rampk_sb, scalar1=slope_sb[:, h:h + 1]))(h),
                waits=[ts2], inc=True)
            ts3 = sch.op("S", (lambda h: lambda e: e.activation(
                out=kdec_all[:, 2 * h:2 * h + 2], in_=dtmp[:, 0:2],
                func=AF.Exp, scale=-1.0))(h), waits=[tv_], inc=True)
            tok_decay = sch.op("S", (lambda h: lambda e: e.activation(
                out=bd_all[:, h:h + 1], in_=slope_sb[:, h:h + 1],
                func=AF.Exp, scale=-256.0))(h), inc=True)

        tok_wq[0] = sch.op("G", lambda e: e.dma_start(
            out=wq_sb[0][:].rearrange("p (c j) -> p c j", j=384),
            in_=wqkvT_ap[0].rearrange("(c p) j -> p c j", p=128)),
            waits=tok_ffn_done + [tok_gsd[NH - 1]], inc=True, sem="wq0")
        tok_gld[0] = sch.op("Q", lambda e: e.dma_start(
            out=gate_h[0][:], in_=gate_dram[0]),
            waits=tok_ffn_done + [tok_gsd[0], tok_gsd[NH - 1]],
            inc=True, sem="gld0")
        if c_heads > 1:
            tok_gld[1] = sch.op("Q", lambda e: e.dma_start(
                out=gate_h[1][:], in_=gate_dram[1]),
                waits=tok_ffn_done + [tok_gsd[1], tok_gsd[NH - 1]],
                inc=True, sem="gld1")
        tkv0 = {}

        def emit_prod_group(h, sel, tg):
            hb = h % 2
            dests = (q_dim[hb], k_dim[hb], v_dimb)
            gidx = (h * 3 + sel) * NTG + tg
            par = gidx % 2

            def mmfn(hb, sel, tg, par):
                def f(e):
                    r = None
                    for fc in range(NFC):
                        r = e.matmul(pfA[par][:],
                                     wq_sb[hb][:, fc * 384 + sel * 128: fc * 384 + (sel + 1) * 128],
                                     xnT[:, fc * T + tg * 512: fc * T + tg * 512 + 512],
                                     start=(fc == 0), stop=(fc == NFC - 1))
                    return r
                return f
            wmm = [tok_wq[h], tok_pev.get((gidx - 2, "g"))]
            wmm.extend(tok_ffn_done)
            if gidx < 2:
                wmm.append(tok_gse.get(NFC * NTG - 1))
            if sel == 0:
                wmm.append(tok_qdecb.get((h - 2, NB - 1)))
            if sel == 2:
                wmm.append(tok_tr.get(h - 1))
            tmm_ = sch.op("T", mmfn(hb, sel, tg, par), waits=wmm, inc=True)
            tok_pev[(gidx, "g")] = sch.op("S",
                (lambda d, tg, par: lambda e: e.activation(
                    out=d[:, tg * 512:(tg + 1) * 512], in_=pfA[par][:],
                    func=AF.Silu))(dests[sel], tg, par),
                waits=[tmm_], inc=True)
            if tg == NTG - 1:
                tok_pev[(h, sel)] = tok_pev[(gidx, "g")]
                if sel == 2:
                    tok_prod[h] = tok_pev[(h, 2)]

        def emit_transposes(h):
            hb = h % 2

            def trfn(srcbuf, b, base):
                def f(e):
                    r = None
                    for s in range(8):
                        r = e.transpose(pT[b][:, s * 128:(s + 1) * 128],
                                        srcbuf[:, (base + s) * 128:(base + s + 1) * 128],
                                        id_bf[:])
                    return r
                return f
            tev_prev = None
            for half in range(2):
                tk = sch.op("T", trfn(k_dim[hb], half, half * 8),
                    waits=[tok_pev[(h, 1)], t_idbf, tok_ktd.get((h - 1, half)),
                           tok_vev.get((h - 1, half)), tev_prev], inc=True)
                tev_prev = sch.op("V", (lambda half: lambda e: e.tensor_copy(
                    out=ktok_raw[:, half * 1024:(half + 1) * 1024], in_=pT[half][:]))(half),
                    waits=[tk, tvjoinC], inc=True)
            for half in range(2):
                tk = sch.op("T", trfn(v_dimb, half, half * 8),
                    waits=[tok_pev[(h, 2)], tev_prev], inc=True)
                tok_vev[(h, half)] = sch.op("V",
                    (lambda half, hb: lambda e: e.tensor_copy(
                        out=v_tok[hb][:, half * 1024:(half + 1) * 1024], in_=pT[half][:]))(half, hb),
                    waits=[tk], hard=[tok_kvstt.get((h - 2, NB - 1))], inc=True)
                tev_prev = tok_vev[(h, half)]
            tok_tr[h] = tk

            kr = ktok_raw[:].rearrange("p (t e) -> p t e", e=128)
            kdr = k_td[hb][:].rearrange("p (t e) -> p t e", e=128)
            for par_t in range(2):
                vi = kr[:, par_t:NTT:2, :]
                vo = kdr[:, par_t:NTT:2, :]
                tok_ktd[(h, par_t)] = sch.op("V",
                    (lambda vi, vo, h, par_t: lambda e: e.tensor_scalar_mul(
                        out=vo, in0=vi,
                        scalar1=kdec_all[:, 2 * h + par_t:2 * h + par_t + 1]))(vi, vo, h, par_t),
                    waits=[tok_decay],
                    hard=[tev_prev, tok_kvstt.get((h - 2, NB - 1))], inc=True)

        def emit_scan_a(h, j):
            hb = h % 2
            jp = j % 2
            if j == 0:
                tkv0[h] = sch.op("V", lambda e: e.memset(kv_sb[:], 0.0),
                    hard=[tok_kvcast.get((h - 1, NB - 1))], inc=True)
            tok_qdecb[(h, j)] = sch.op("V",
                (lambda h, j, jp, hb: lambda e: e.tensor_mul(
                    out=qdec_b[:, jp * 256:(jp + 1) * 256],
                    in0=q_dim[hb][:, j * 256:(j + 1) * 256],
                    in1=qdec_all[:, h * BLK:(h + 1) * BLK]))(h, j, jp, hb),
                waits=[tok_pev[(h, 0)], tok_decay], inc=True)

            def scfn(hb, j, jp):
                def f(e):
                    r = None
                    for nt in range(2):
                        r = e.matmul(pS[jp][:, nt * 256:(nt + 1) * 256],
                                     k_dim[hb][:, (2 * j + nt) * 128:(2 * j + nt + 1) * 128],
                                     q_dim[hb][:, j * 256:(j + 1) * 256],
                                     start=True, stop=True)
                    return r
                return f
            tsc = sch.op("T", scfn(hb, j, jp),
                waits=[tok_pev[(h, 1)], tok_pev[(h, 0)],
                       tok_kvstt.get((h, j - 2)), tok_ssacc.get((h, j - 2)),
                       tok_kvstt.get((h - 1, NB - 1)) if j < 2 else None,
                       tok_ssacc.get((h - 1, NB - 1)) if j < 2 else None],
                inc=True)
            tst = sch.op("V", (lambda h, jp: lambda e: e.tensor_tensor(
                out=sT_b[:, jp * 512:(jp + 1) * 512], in0=pS[jp][:],
                in1=decT_all[:, h * 512:(h + 1) * 512], op=ALU.mult))(h, jp),
                waits=[tsc], inc=True)
            tok_tst[(h, j)] = tst

        def emit_scan_b(h, j):
            hb = h % 2
            jp = j % 2
            tst = tok_tst[(h, j)]

            def ofn(h, j, jp, hb):
                def f(e):
                    r = None
                    first = True
                    if j > 0:
                        r = e.matmul(pO[jp][:, 0:256], kv_bf[:],
                                     qdec_b[:, jp * 256:(jp + 1) * 256],
                                     start=True, stop=False)
                        first = False
                    for nt in range(2):
                        r = e.matmul(pO[jp][:, 0:256],
                                     v_tok[hb][:, (2 * j + nt) * 128:(2 * j + nt + 1) * 128],
                                     sT_b[:, jp * 512 + nt * 256: jp * 512 + (nt + 1) * 256],
                                     start=first, stop=(nt == 1))
                        first = False
                    return r
                return f
            tog = sch.op("T", ofn(h, j, jp, hb),
                waits=[tst, tok_qdecb[(h, j)],
                       tok_kvcast.get((h, j - 1)),
                       tok_vev.get((h, 1)),
                       tok_sq.get((h, j - 2)), tok_og.get((h, j - 2)),
                       tok_sq.get((h - 1, NB - 1)) if j < 2 else None,
                       tok_og.get((h - 1, NB - 1)) if j < 2 else None],
                inc=True)

            def kvfn(h, j, jp, hb):
                def f(e):
                    r = None
                    for nt in range(2):
                        r = e.matmul(pS[jp][:, 0:128],
                                     k_td[hb][:, (2 * j + nt) * 128:(2 * j + nt + 1) * 128],
                                     v_tok[hb][:, (2 * j + nt) * 128:(2 * j + nt + 1) * 128],
                                     start=(nt == 0), stop=(nt == 1))
                    return r
                return f
            tkv = sch.op("T", kvfn(h, j, jp, hb),
                waits=[tok_ktd.get((h, 1)), tok_ktd.get((h, 0))], inc=True)
            tok_sq[(h, j)] = sch.op("S", (lambda jp: lambda e: e.activation(
                out=sq_b[:, jp * 256:(jp + 1) * 256], in_=pO[jp][:, 0:256],
                func=AF.Square))(jp), waits=[tog], inc=True)
            tss = sch.op("T", (lambda jp: lambda e: e.matmul(
                pS[jp][0:1, 128:384], ones_bf[:],
                sq_b[:, jp * 256:(jp + 1) * 256], start=True, stop=True))(jp),
                waits=[tok_sq[(h, j)], tkv], inc=True)
            tok_og[(h, j)] = sch.op("V",
                (lambda h, j, jp, hb: lambda e: e.scalar_tensor_tensor(
                    out=O_st[hb][:, j * 256:(j + 1) * 256], in0=pO[jp][:, 0:256],
                    scalar=normw_sb[:, h:h + 1],
                    in1=gate_h[hb][:, j * 256:(j + 1) * 256],
                    op0=ALU.mult, op1=ALU.mult))(h, j, jp, hb),
                waits=[tog, t_cst, tok_ost.get(h - 2), tok_sq[(h, j)],
                       tok_gld[h]], inc=True)
            tok_kvstt[(h, j)] = sch.op("V",
                (lambda h, jp: lambda e: e.scalar_tensor_tensor(
                    out=kv_sb[:], in0=kv_sb[:], scalar=bd_all[:, h:h + 1],
                    in1=pS[jp][:, 0:128], op0=ALU.mult, op1=ALU.add))(h, jp),
                waits=[tss], hard=[tkv0[h]], inc=True)
            tok_kvcast[(h, j)] = sch.op("V", lambda e: e.tensor_copy(
                out=kv_bf[:], in_=kv_sb[:]),
                hard=[tok_kvstt[(h, j)]], inc=True)
            tok_ssacc[(h, j)] = sch.op("V",
                (lambda j, jp: lambda e: e.tensor_tensor(
                    out=sumsq[0:1, j * 256:(j + 1) * 256],
                    in0=sumsq[0:1, j * 256:(j + 1) * 256],
                    in1=pS[jp][0:1, 128:384], op=ALU.add))(j, jp),
                waits=[tss], inc=True)

        for it in range(c_heads + 1):
            ph = it if it < c_heads else None
            sh = it - 1
            if ph is not None and ph + 1 < c_heads:
                tok_wq[ph + 1] = sch.op("G", (lambda ph: lambda e: e.dma_start(
                    out=wq_sb[(ph + 1) % 2][:].rearrange("p (c j) -> p c j", j=384),
                    in_=wqkvT_ap[ph + 1].rearrange("(c p) j -> p c j", p=128)))(ph),
                    waits=[tok_prod.get(ph - 1),
                           tok_ffn_done[0] if ph == 0 else None],
                    inc=True, sem=f"wq{(ph + 1) % 2}")
            gitems = ([("p", s, t) for s in range(3) for t in range(NTG)]
                      if ph is not None else [])
            bitems = ([x for j in range(NB) for x in (("a", j), ("b", j))]
                      if sh >= 0 else [])
            merged = []
            gi = bi = 0
            while gi < len(gitems) or bi < len(bitems):
                if bi >= len(bitems) or (gi < len(gitems) and
                                         gi * (len(bitems) + 1) <= bi * (len(gitems) + 1)):
                    merged.append(gitems[gi]); gi += 1
                else:
                    merged.append(bitems[bi]); bi += 1
            for item in merged:
                if item[0] == "p":
                    emit_prod_group(ph, item[1], item[2])
                elif item[0] == "a":
                    emit_scan_a(sh, item[1])
                else:
                    emit_scan_b(sh, item[1])
            if sh >= 0:
                hb = sh % 2
                tok_ost[sh] = sch.op("Q", (lambda sh, hb: lambda e: e.dma_start(
                    out=o_dram[sh], in_=O_st[hb][:]))(sh, hb),
                    waits=[tok_og[(sh, NB - 1)]], inc=True, sem=f"ost{hb}")
                if sh + 2 < c_heads:
                    tok_gld[sh + 2] = sch.op("Q", (lambda sh: lambda e: e.dma_start(
                        out=gate_h[sh % 2][:], in_=gate_dram[sh + 2]))(sh),
                        waits=[tok_gsd[sh + 2], tok_og[(sh, NB - 1)]],
                        inc=True, sem=f"gld{sh % 2}")
            if ph is not None:
                emit_transposes(ph)

        tok_c_done = [tok_ost.get(c_heads - 1), tok_ost.get(c_heads - 2),
                      tok_decay]
        tok_c_v = tok_ssacc.get((NH - 1, NB - 1))
        tok_c_s = tok_sq.get((NH - 1, NB - 1))

    if upto == "C":
        fin = sch.op("G", lambda e: e.dma_start(
            out=outT_ap[0:128, 0:512], in_=xnT[:, 0:512]),
            waits=tok_c_done + [tok_c_v], inc=True, sem="fin")
        sch.op("V", lambda e: e.nop(), waits=[fin])
        return _finish(nc, sch, es, esABC)

    esABC.close()

    with contextlib.ExitStack() as esD:
        def sbD(name, shape, dtype):
            return esD.enter_context(nc.sbuf_tensor(name, list(shape), dtype))

        wout_sb = sbD("wout_sb", (128, NFC * D), BF16)
        O_sb = sbD("O_sb", (128, NH * T), BF16)
        rsq_rep = sbD("rsq_rep", (128, T), F32)
        g_fc = [sbD(f"g_fc{i}", (128, T), BF16) for i in range(2)]
        xT_fc = [sbD(f"xT_fc{i}", (128, T), F32) for i in range(2)]
        s1 = [sbD(f"s1_{i}", (128, 512), F32) for i in range(2)]
        outb = [sbD(f"outb{i}", (128, 512), F32) for i in range(2)]
        rtmp = sbD("rtmp", (1, T), F32)
        rvec = sbD("rvec", (1, T), F32)
        rtmp2 = sumsq

        tjoin = sch.op("V", lambda e: e.tensor_copy(
            out=join_sb[0:1, 1:2], in_=ones1[0:1, 0:1]), waits=tok_c_done, inc=True)
        tsjoin = sch.op("S", lambda e: e.copy(
            out=join_sb[0:1, 2:3], in_=ones1[0:1, 0:1]), waits=[tok_c_s], inc=True)

        tok_wout = {}
        for fc in range(NFC):
            tok_wout[fc] = sch.op("G", (lambda fc: lambda e: e.dma_start(
                out=wout_sb[:, fc * NH * 128:(fc + 1) * NH * 128].rearrange(
                    "p (c f) -> p c f", f=128),
                in_=woutT_ap[:, fc * 128:(fc + 1) * 128].rearrange(
                    "(c p) f -> p c f", p=128)))(fc),
                waits=[tok_prod[NH - 1], tjoin, tsjoin] if fc == 0 else [],
                inc=True, sem=f"wou{fc}")
        tok_oload = sch.op("Q", lambda e: e.dma_start(
            out=O_sb[:].rearrange("p (c t) -> p c t", t=T),
            in_=o_dram.rearrange("c p t -> p c t")),
            waits=tok_c_done + [tjoin, tsjoin], inc=True, sem="olo")

        tsr2 = sch.op("S", lambda e: e.activation(
            out=rtmp[:], in_=sumsq[:], func=AF.Sqrt, scale=1.0 / D,
            bias=eps_sb[0:1, 1:2]), waits=[tok_c_v, tsjoin], inc=True)
        tn = sch.op("V", lambda e: e.reciprocal(out=rvec[:], in_=rtmp[:]),
                    waits=[tsr2], hard=[tjoin], inc=True)
        tn = sch.op("V", lambda e: e.tensor_scalar(
            out=rtmp2[:], in0=sumsq[:], scalar1=1.0 / D, scalar2=1e-6,
            op0=ALU.mult, op1=ALU.add), waits=[tsr2], hard=[tn], inc=True)
        tn = sch.op("V", lambda e: e.tensor_mul(
            out=rtmp[:], in0=rvec[:], in1=rvec[:]), hard=[tn], inc=True)
        tn = sch.op("V", lambda e: e.tensor_mul(
            out=rtmp[:], in0=rtmp[:], in1=rtmp2[:]), hard=[tn], inc=True)
        tn = sch.op("V", lambda e: e.tensor_scalar(
            out=rtmp[:], in0=rtmp[:], scalar1=-0.5, scalar2=1.5,
            op0=ALU.mult, op1=ALU.add), hard=[tn], inc=True)
        trv = sch.op("V", lambda e: e.tensor_mul(
            out=rvec[:], in0=rvec[:], in1=rtmp[:]), hard=[tn], inc=True)
        trep2 = {}
        for g in range(NTG):
            par = g % 2
            ta = sch.op("T", (lambda g, par: lambda e: e.matmul(
                pO[par][:, 0:512], ones1[:], rvec[0:1, g * 512:(g + 1) * 512],
                start=True, stop=True))(g, par),
                waits=[trv, tok_c_v, trep2.get(g - 2)], inc=True)
            trep2[g] = sch.op("V", (lambda g, par: lambda e: e.tensor_copy(
                out=rsq_rep[:, g * 512:(g + 1) * 512], in_=pO[par][:, 0:512]))(g, par),
                waits=[ta], inc=True)
        tok_rsq = trep2[NTG - 1]

        tok_gl = {}
        tok_xl = {}
        tok_out = {}
        tok_ev3 = {}
        for fc in range(NFC):
            fb = fc % 2
            tok_gl[fc] = sch.op("Q", (lambda fc, fb: lambda e: e.dma_start(
                out=g_fc[fb][:], in_=g_dram[fc]))(fc, fb),
                waits=[tok_ev3.get((fc - 2, NTG - 1)), tjoin, tsjoin] + tok_ffn_done,
                inc=True, sem=f"gl{fb}")
            tok_gl[fc] = sch.op("S", (lambda fc, fb: lambda e: e.activation(
                out=g_fc[fb][:], in_=g_fc[fb][:], func=AF.Silu))(fc, fb),
                waits=[tok_gl[fc]], inc=True)
            tok_xl[fc] = sch.op("Q", (lambda fc, fb: lambda e: e.dma_start(
                out=xT_fc[fb][:], in_=xT_ap[fc * 128:(fc + 1) * 128, :]))(fc, fb),
                waits=[tok_ev3.get((fc - 2, NTG - 1)), tjoin, tsjoin],
                inc=True, sem=f"xl{fb}")
            for tg in range(NTG):
                gidx = fc * NTG + tg
                pbank = (pfA + pS)[gidx % 4]
                par = gidx % 2

                def mmfn(fc, tg, pbank):
                    def f(e):
                        base = fc * NH * 128
                        for oc in range(NH):
                            r = e.matmul(pbank[:],
                                     wout_sb[:, base + oc * 128: base + (oc + 1) * 128],
                                     O_sb[:, oc * T + tg * 512: oc * T + tg * 512 + 512],
                                     start=(oc == 0), stop=(oc == NH - 1))
                        return r
                    return f
                tmm_ = sch.op("T", mmfn(fc, tg, pbank),
                    waits=[tok_wout[fc], tok_oload, tok_ev3.get((gidx - 4, "s"))], inc=True)
                tv1 = sch.op("V", (lambda par, tg, pbank: lambda e: e.tensor_tensor(
                    out=s1[par][:], in0=pbank[:],
                    in1=rsq_rep[:, tg * 512:(tg + 1) * 512], op=ALU.mult))(par, tg, pbank),
                    waits=[tmm_],
                    hard=[tok_rsq, tok_ev3.get((gidx - 2, "v3"))], inc=True)
                tok_ev3[(gidx, "s")] = tv1
                tv2 = sch.op("V", (lambda par, fb, tg: lambda e: e.tensor_tensor(
                    out=s1[par][:], in0=s1[par][:],
                    in1=g_fc[fb][:, tg * 512:(tg + 1) * 512], op=ALU.mult))(par, fb, tg),
                    waits=[tok_gl[fc]], hard=[tv1], inc=True)
                tv3 = sch.op("V", (lambda par, fb, tg: lambda e: e.tensor_tensor(
                    out=outb[par][:], in0=s1[par][:],
                    in1=xT_fc[fb][:, tg * 512:(tg + 1) * 512], op=ALU.add))(par, fb, tg),
                    waits=[tok_xl[fc], tok_out.get(gidx - 2)], hard=[tv2], inc=True)
                tok_ev3[(fc, tg)] = tv3
                tok_ev3[(gidx, "v3")] = tv3
                tok_out[gidx] = sch.op("Q",
                    (lambda fc, tg, par: lambda e: e.dma_start(
                        out=outT_ap[fc * 128:(fc + 1) * 128, tg * 512:(tg + 1) * 512],
                        in_=outb[par][:]))(fc, tg, par),
                    waits=[tv3], inc=True, sem=f"out{par}")

        finals = [tok_out[NFC * NTG - 1], tok_out[NFC * NTG - 2]]
        sch.op("Q", lambda e: e.nop(), waits=finals)
        sch.op("V", lambda e: e.nop(), waits=finals)

    return _finish(nc, sch, es, esABC)


def _finish(nc, sch, es, esABC):
    semstack = contextlib.ExitStack()
    sems = {}
    for name in sorted(sch.sem_names):
        sems[name] = semstack.enter_context(nc.semaphore(f"sem_{name}"))
    with nc.Block() as block:
        @block.tensor
        def _(eng):
            sch.replay("T", eng, sems)

        @block.vector
        def _(eng):
            sch.replay("V", eng, sems)

        @block.scalar
        def _(eng):
            sch.replay("S", eng, sems)

        @block.sync
        def _(eng):
            sch.replay("Q", eng, sems)

        @block.gpsimd
        def _(eng):
            sch.replay("G", eng, sems)

    semstack.close()
    try:
        esABC.close()
    except Exception:
        pass
    es.close()
    return nc


def host_inputs(inputs):
    x = np.asarray(inputs["L0_attn_hidden_states"], dtype=np.float32)
    slope = np.asarray(inputs["L0_attn_slope_rate"], dtype=np.float32).reshape(NH)
    w_qkv = np.asarray(inputs["L0_attn_w_qkv"], dtype=np.float32)
    w_gate = np.asarray(inputs["L0_attn_w_gate"], dtype=np.float32)
    w_out = np.asarray(inputs["L0_attn_w_out"], dtype=np.float32)
    norm_w = np.asarray(inputs["L0_attn_norm_w"], dtype=np.float32)
    pweight = np.asarray(inputs["L0_ffn_pweight"], dtype=np.float32)
    nweight = np.asarray(inputs["L0_ffn_nweight"], dtype=np.float32)
    exps = np.asarray(inputs["L0_ffn_exps"], dtype=np.float32)
    bexps = np.asarray(inputs["L0_ffn_bexps"], dtype=np.float32)
    mask_w = np.asarray(inputs["L0_ffn_mask_weight"], dtype=np.float32)
    scale = np.asarray(inputs["L0_ffn_scale"], dtype=np.float32)
    pbias = np.asarray(inputs["L0_ffn_pbias"], dtype=np.float32)
    nbias = np.asarray(inputs["L0_ffn_nbias"], dtype=np.float32)
    bscale = np.asarray(inputs["L0_ffn_biasscale"], dtype=np.float32)

    C = np.ascontiguousarray
    consts = np.zeros((128, CW), dtype=np.float32)

    def setc(name, val):
        a, b = _CO[name]
        consts[:, a:b] = val

    p = np.arange(128, dtype=np.float32)
    m = np.arange(BLK, dtype=np.float32).reshape(1, BLK)
    setc("id", np.eye(128, dtype=np.float32))
    setc("rampq", np.arange(1, BLK + 1, dtype=np.float32).reshape(1, BLK))
    grids = []
    for tile in range(2):
        n = (p + 128 * tile).reshape(128, 1)
        g = m - n
        grids.append(np.where(g >= 0, g, np.float32(1e30)).astype(np.float32))
    setc("rampmn", np.concatenate(grids, axis=1))
    setc("rampk", np.stack([255.0 - p, 127.0 - p], axis=1))
    setc("normw", norm_w.reshape(NH, 128).T)
    setc("slope", slope.reshape(1, NH))
    setc("pb2", pbias.reshape(NH, 128, NBITS).transpose(1, 0, 2).reshape(128, NH * NBITS))
    setc("nb2", nbias.reshape(NH, 128, NBITS).transpose(1, 0, 2).reshape(128, NH * NBITS))
    setc("exps", exps.reshape(1, NBITS))
    setc("mw", mask_w.reshape(1, NBITS))
    setc("bexps", bexps.reshape(1, NBITS))
    setc("scale", scale.reshape(1, 1))
    setc("bscale", bscale.reshape(1, 1))

    shared = {
        "wqkvT": C(w_qkv.reshape(NH, 384, D).transpose(0, 2, 1)),
        "wgateT": C(w_gate.reshape(NH, HD, D).transpose(0, 2, 1)),
        "woutT": C(w_out.T),
        "pwT": C(pweight.transpose(2, 1, 0)),
        "nwT": C(nweight.transpose(2, 1, 0)),
        "consts": consts,
    }
    in_maps = []
    for c in range(NCORES):
        im = dict(shared)
        im["x"] = C(x[c])
        im["xT"] = C(x[c].T)
        in_maps.append(im)
    return in_maps


def kernel(**inputs):
    in_maps = host_inputs(inputs)
    nc = build_program()
    res = run_bass_kernel_spmd(nc, in_maps, core_ids=list(range(NCORES)))
    out = np.stack([np.ascontiguousarray(r["outT"].T) for r in res.results])
    return out


# version 23
# speedup vs baseline: 1.0361x; 1.0361x over previous
import contextlib
import numpy as np
import concourse.bass as bass
import concourse.mybir as mybir
from concourse.bass_utils import run_bass_kernel_spmd

F32 = mybir.dt.float32
BF16 = mybir.dt.bfloat16
AF = mybir.ActivationFunctionType
ALU = mybir.AluOpType
AX = mybir.AxisListType

NCORES = 8
T = 2048
D = 2048
NH = 16
HD = 128
BLK = 256
NB = T // BLK
NTT = T // 128
NFC = D // 128
NTG = T // 512
NBITS = 4

ENGS = ("T", "V", "S", "Q", "G")

_CO = {}
_off = 0
for _name, _w in [("id", 128), ("rampq", BLK), ("rampmn", 2 * BLK), ("rampk", 2),
                  ("normw", NH), ("slope", NH), ("pb2", NH * NBITS),
                  ("nb2", NH * NBITS), ("exps", NBITS), ("mw", NBITS),
                  ("bexps", NBITS), ("scale", 1), ("bscale", 1)]:
    _CO[_name] = (_off, _off + _w)
    _off += _w
CW = _off


class Sched:

    def __init__(self):
        self.ops = {e: [] for e in ENGS}
        self.cnt = {}
        self.sem_names = set(["T", "V", "S"])

    def op(self, eng, fn, waits=(), inc=False, sem=None, hard=()):
        if eng in ("Q", "G"):
            amt = 16
            assert sem is not None or not inc
        else:
            amt = 1
            sem = eng
        ws = []
        for w in waits:
            if w is None:
                continue
            pse, val = w
            if eng not in ("Q", "G") and pse == eng:
                continue
            ws.append((pse, val))
        for w in hard:
            if w is None:
                continue
            ws.append(tuple(w))
        self.ops[eng].append((ws, fn, inc, sem))
        if inc:
            self.sem_names.add(sem)
            self.cnt[sem] = self.cnt.get(sem, 0) + amt
            return (sem, self.cnt[sem])
        return None

    def replay(self, eng_name, eng_obj, sems):
        maxw = {}
        for ws, fn, inc, sem in self.ops[eng_name]:
            for pse, val in ws:
                if maxw.get(pse, 0) >= val:
                    continue
                maxw[pse] = val
                eng_obj.wait_ge(sems[pse], val)
            inst = fn(eng_obj)
            if inc:
                amt = 16 if eng_name in ("Q", "G") else 1
                if inst is None:
                    inst = eng_obj.nop()
                inst.then_inc(sems[sem], amt)


def build_program(upto="D", c_heads=NH):
    nc = bass.Bass("TRN2", target_bir_lowering=False)

    def din(name, shape):
        return nc.dram_tensor(name, list(shape), F32, kind="ExternalInput").ap()

    x_ap = din("x", (T, D))
    xT_ap = din("xT", (D, T))
    wqkvT_ap = din("wqkvT", (NH, D, 384))
    wgateT_ap = din("wgateT", (NH, D, HD))
    woutT_ap = din("woutT", (D, D))
    pwT_ap = din("pwT", (NBITS, D, D))
    nwT_ap = din("nwT", (NBITS, D, D))
    consts_ap = din("consts", (128, CW))

    outT_ap = nc.dram_tensor("outT", [D, T], F32, kind="ExternalOutput").ap()

    g_dram = nc.dram_tensor("g_scr", [NFC, 128, T], BF16, kind="Internal").ap()
    gate_dram = nc.dram_tensor("gate_scr", [NH, 128, T], BF16, kind="Internal").ap()
    o_dram = nc.dram_tensor("o_scr", [NH, 128, T], BF16, kind="Internal").ap()

    sch = Sched()
    es = contextlib.ExitStack()

    def sb(name, shape, dtype):
        return es.enter_context(nc.sbuf_tensor(name, list(shape), dtype))

    def ps(name, shape, dtype):
        return es.enter_context(nc.psum_tensor(name, list(shape), dtype))

    csb = sb("csb", (128, CW), F32)

    def cc(name):
        a, b = _CO[name]
        return csb[:, a:b]

    id_f32 = cc("id")
    rampq_sb = cc("rampq")
    rampmn_sb = cc("rampmn")
    rampk_sb = cc("rampk")
    normw_sb = cc("normw")
    slope_sb = cc("slope")
    pb2_sb = cc("pb2")
    nb2_sb = cc("nb2")

    id_bf = sb("id_bf", (128, 128), BF16)
    ones_bf = sb("ones_bf", (128, 1), BF16)
    ones1 = sb("ones1", (1, 128), F32)
    biasW = sb("biasW", (128, NFC), F32)
    sumsq = sb("sumsq", (1, T), F32)
    eps_sb = sb("eps_sb", (128, 2), F32)
    cpos = sb("cpos", (128, NBITS), F32)
    cneg = sb("cneg", (128, NBITS), F32)
    sgm = sb("sgm", (128, NBITS), F32)
    join_sb = sb("join_sb", (1, 8), F32)

    pfA = [ps(f"pfA{i}", (128, 512), F32) for i in range(2)]
    pS = [ps(f"pS{i}", (128, 512), F32) for i in range(2)]
    pO = [ps(f"pO{i}", (128, 512), F32) for i in range(2)]
    pT = [ps(f"pT{i}", (128, 1024), BF16) for i in range(2)]

    t_cst = sch.op("Q", lambda e: e.dma_start(out=csb[:], in_=consts_ap),
                   inc=True, sem="cst")
    t_idbf = sch.op("V", lambda e: e.tensor_copy(out=id_bf[:], in_=id_f32),
                    waits=[t_cst], inc=True)
    sch.op("V", lambda e: e.memset(ones_bf[:], 1.0))
    sch.op("V", lambda e: e.memset(ones1[:], 1.0))
    sch.op("V", lambda e: e.memset(eps_sb[:, 0:1], 1e-5))
    sch.op("V", lambda e: e.memset(eps_sb[:, 1:2], 1e-6))
    sch.op("V", lambda e: e.memset(sumsq[:], 0.0))

    esABC = contextlib.ExitStack()
    xnT = esABC.enter_context(nc.sbuf_tensor("xnT", [128, NFC * T], BF16))
    esPN = contextlib.ExitStack()
    NSLP = 4
    NSLQ = 3
    pnbg = [esPN.enter_context(nc.sbuf_tensor(f"pnbg{i}", [128, D], BF16))
            for i in range(NSLP)]
    pnq = [esPN.enter_context(nc.sbuf_tensor(f"pnq{i}", [128, D], F32))
           for i in range(NSLQ)]
    Wacc = esPN.enter_context(nc.sbuf_tensor("Wacc", [128, D], F32))
    with contextlib.ExitStack() as esA:
        def sbA(name, shape, dtype):
            return esA.enter_context(nc.sbuf_tensor(name, list(shape), dtype))

        xtile = [sbA(f"xtile{i}", (128, D), F32) for i in range(3)]
        xntok = [sbA(f"xntok{i}", (128, D), BF16) for i in range(2)]
        junkA = sbA("junkA", (128, D), BF16)
        mneg_c = sbA("mneg_c", (128, NTT), F32)
        vss_c = sbA("vss_c", (128, NTT), F32)
        y_c = sbA("y_c", (128, NTT), F32)
        s_c = sbA("s_c", (128, NTT), F32)
        r_c = sbA("r_c", (128, NTT), F32)
        t_c = sbA("t_c", (128, NTT), F32)

        tload = [None, None, None]
        tsq = {}
        txn = {}
        tev = {}

        def a_stage1(tt):
            b = tt % 3
            tload[b] = sch.op("Q", (lambda tt, b: lambda e: e.dma_start(
                out=xtile[b][:], in_=x_ap[tt * 128:(tt + 1) * 128, :]))(tt, b),
                waits=[txn.get(tt - 3)], inc=True, sem=f"xa{b}")
            tred = sch.op("V", (lambda tt, b: lambda e: e.tensor_reduce(
                out=mneg_c[:, tt:tt + 1], in_=xtile[b][:], axis=AX.X, op=ALU.add))(tt, b),
                waits=[tload[b]], inc=True)
            tr2 = sch.op("V", (lambda tt: lambda e: e.tensor_scalar_mul(
                out=mneg_c[:, tt:tt + 1], in0=mneg_c[:, tt:tt + 1],
                scalar1=-1.0 / D))(tt), hard=[tred], inc=True)
            tsq[tt] = sch.op("S", (lambda tt, b: lambda e: e.activation(
                out=junkA[:], in_=xtile[b][:], func=AF.Square,
                bias=mneg_c[:, tt:tt + 1], accum_out=vss_c[:, tt:tt + 1]))(tt, b),
                waits=[tr2], hard=[tsq.get(tt - 1)], inc=True)

        def a_stage2(tt):
            b = tt % 3
            b2 = tt % 2
            tyv = sch.op("V", (lambda tt: lambda e: e.tensor_scalar(
                out=y_c[:, tt:tt + 1], in0=vss_c[:, tt:tt + 1],
                scalar1=1.0 / D, scalar2=1e-5,
                op0=ALU.mult, op1=ALU.add))(tt), waits=[tsq[tt]], inc=True)
            tsr = sch.op("S", (lambda tt: lambda e: e.activation(
                out=s_c[:, tt:tt + 1], in_=y_c[:, tt:tt + 1],
                func=AF.Sqrt))(tt), waits=[tyv], inc=True)
            tn = sch.op("V", (lambda tt: lambda e: e.reciprocal(
                out=r_c[:, tt:tt + 1], in_=s_c[:, tt:tt + 1]))(tt),
                waits=[tsr], inc=True)
            tn = sch.op("V", (lambda tt: lambda e: e.tensor_mul(
                out=t_c[:, tt:tt + 1], in0=r_c[:, tt:tt + 1],
                in1=r_c[:, tt:tt + 1]))(tt), hard=[tn], inc=True)
            tn = sch.op("V", (lambda tt: lambda e: e.tensor_mul(
                out=t_c[:, tt:tt + 1], in0=t_c[:, tt:tt + 1],
                in1=y_c[:, tt:tt + 1]))(tt), hard=[tn], inc=True)
            tn = sch.op("V", (lambda tt: lambda e: e.tensor_scalar(
                out=t_c[:, tt:tt + 1], in0=t_c[:, tt:tt + 1],
                scalar1=-0.5, scalar2=1.5,
                op0=ALU.mult, op1=ALU.add))(tt), hard=[tn], inc=True)
            tn = sch.op("V", (lambda tt: lambda e: e.tensor_mul(
                out=r_c[:, tt:tt + 1], in0=r_c[:, tt:tt + 1],
                in1=t_c[:, tt:tt + 1]))(tt), hard=[tn], inc=True)
            txn[tt] = sch.op("V", (lambda tt, b, b2: lambda e: e.tensor_scalar(
                out=xntok[b2][:], in0=xtile[b][:],
                scalar1=mneg_c[:, tt:tt + 1], scalar2=r_c[:, tt:tt + 1],
                op0=ALU.add, op1=ALU.mult))(tt, b, b2),
                hard=[tn, tev.get((tt - 2, 1))], inc=True)
            for half in range(2):
                def trfnA(b2, half, tt):
                    def f(e):
                        r = None
                        for s in range(8):
                            fc = half * 8 + s
                            r = e.transpose(pT[half][:, s * 128:(s + 1) * 128],
                                            xntok[b2][:, fc * 128:(fc + 1) * 128],
                                            id_bf[:])
                        return r
                    return f
                tk = sch.op("T", trfnA(b2, half, tt),
                    waits=[txn[tt], t_idbf, tev.get((tt - 1, half))], inc=True)
                xv = xnT[:].rearrange("p (c t) -> p c t", t=T)
                tev[(tt, half)] = sch.op("V",
                    (lambda tt, half, xv: lambda e: e.tensor_copy(
                        out=xv[:, half * 8:(half + 1) * 8, tt * 128:(tt + 1) * 128],
                        in_=pT[half][:].rearrange("p (s t) -> p s t", t=128)))(tt, half, xv),
                    waits=[tk], inc=True)

        for tt in range(NTT + 1):
            if tt < NTT:
                a_stage1(tt)
            if tt >= 1:
                a_stage2(tt - 1)
        tok_xn = sch.op("V", lambda e: e.tensor_copy(
            out=join_sb[0:1, 3:4], in_=ones1[0:1, 0:1]),
            hard=[tev[(NTT - 1, 0)], tev[(NTT - 1, 1)]], inc=True)

    tok_gsd = {}
    with contextlib.ExitStack() as esB:
        def sbB(name, shape, dtype):
            return esB.enter_context(nc.sbuf_tensor(name, list(shape), dtype))

        WffnT = sbB("WffnT", (128, NFC * D), BF16)
        wg_sb = [sbB(f"wg_sb{i}", (128, NFC * HD), BF16) for i in range(2)]
        gate_st = [sbB("gate_stA", (128, T), BF16)]
        g_st = [sbB(f"g_st{i}", (128, 512), BF16) for i in range(2)]
        btmp = sbB("btmp", (128, NH * NBITS), F32)
        bacc = sbB("bacc", (128, NFC), F32)
        br = sbB("br", (128, NFC), F32)

        tcs = sch.op("S", lambda e: e.activation(
            out=sgm[:], in_=cc("mw"), func=AF.Sigmoid), waits=[t_cst], inc=True)
        tc1 = sch.op("V", lambda e: e.tensor_mul(
            out=cpos[:], in0=sgm[:], in1=cc("exps")), waits=[tcs], inc=True)
        tc2 = sch.op("V", lambda e: e.tensor_scalar_mul(
            out=cpos[:], in0=cpos[:], scalar1=cc("scale")), hard=[tc1], inc=True)
        tcneg = sch.op("V", lambda e: e.tensor_scalar_mul(
            out=cneg[:], in0=cpos[:], scalar1=-1.0), hard=[tc2], inc=True)

        tb0 = sch.op("V", lambda e: e.tensor_sub(
            out=btmp[:], in0=pb2_sb, in1=nb2_sb),
            waits=[t_cst, tok_xn], inc=True)
        btr = btmp[:].rearrange("p (g n) -> p n g", n=NBITS)
        tb1 = tb0
        for n in range(NBITS):
            bx = cc("bexps")
            if n == 0:
                tb1 = sch.op("V", (lambda n, bx: lambda e: e.tensor_scalar_mul(
                    out=bacc[:], in0=btr[:, n, :], scalar1=bx[:, n:n + 1]))(n, bx),
                    hard=[tb1], inc=True)
            else:
                tb1 = sch.op("V", (lambda n, bx: lambda e: e.scalar_tensor_tensor(
                    out=bacc[:], in0=btr[:, n, :], scalar=bx[:, n:n + 1],
                    in1=bacc[:], op0=ALU.mult, op1=ALU.add))(n, bx),
                    hard=[tb1], inc=True)
        tb1 = sch.op("V", lambda e: e.tensor_scalar(
            out=bacc[:], in0=bacc[:], scalar1=1.0, scalar2=-1.0,
            op0=ALU.min, op1=ALU.max), hard=[tb1], inc=True)
        tb2 = sch.op("S", lambda e: e.activation(
            out=br[:], in_=bacc[:], func=AF.Abs, scale=15.0), waits=[tb1], inc=True)
        tb3 = sch.op("V", lambda e: e.tensor_single_scalar(
            out=btmp[:, 0:NFC], in_=br[:], scalar=0.5, op=ALU.is_ge),
            waits=[tb2], inc=True)
        for k in range(1, 15):
            tb3 = sch.op("V", (lambda k: lambda e: e.scalar_tensor_tensor(
                out=btmp[:, 0:NFC], in0=br[:], scalar=k + 0.5, in1=btmp[:, 0:NFC],
                op0=ALU.is_ge, op1=ALU.add))(k), hard=[tb3], inc=True)
        tb4 = sch.op("S", lambda e: e.activation(
            out=br[:], in_=bacc[:], func=AF.Sign), waits=[tb3], hard=[tb2], inc=True)
        tbm = sch.op("V", lambda e: e.tensor_mul(
            out=btmp[:, 0:NFC], in0=btmp[:, 0:NFC], in1=br[:]),
            waits=[tb4], hard=[tb3], inc=True)
        tbias = sch.op("V", lambda e: e.tensor_scalar(
            out=biasW[:], in0=btmp[:, 0:NFC], scalar1=1.0 / 15.0,
            scalar2=cc("bscale"), op0=ALU.mult, op1=ALU.mult),
            hard=[tbm], inc=True)

        tok_wg = {}
        tok_gmm = {}
        tok_gst = {}
        tok_evac = {}
        tok_pn = {}
        tok_sig = {}
        tok_comb = {}
        tok_wob = {}
        tok_gev = {}
        tok_gse = {}

        def emit_wg_load(h):
            b = h % 2
            tok_wg[h] = sch.op("G", (lambda h, b: lambda e: e.dma_start(
                out=wg_sb[b][:].rearrange("p (c j) -> p c j", j=HD),
                in_=wgateT_ap[h].rearrange("(c p) j -> p c j", p=128)))(h, b),
                waits=[tok_gmm.get((h - 2, NTG - 1)),
                       tok_xn if h < 2 else None], inc=True, sem=f"wg{b}")

        def emit_gate_head(h):
            b = h % 2
            gb = 0
            for tg in range(NTG):
                cidx = h * 8 + tg
                pbank = (pfA + pS)[cidx % 4]

                def mmfn(h, b, tg, pbank):
                    def f(e):
                        for fc in range(NFC):
                            r = e.matmul(pbank[:],
                                     wg_sb[b][:, fc * HD:(fc + 1) * HD],
                                     xnT[:, fc * T + tg * 512: fc * T + tg * 512 + 512],
                                     start=(fc == 0), stop=(fc == NFC - 1))
                        return r
                    return f
                tok_gmm[(h, tg)] = sch.op("T", mmfn(h, b, tg, pbank),
                    waits=[tok_wg[h], tok_xn,
                           tok_evac.get(cidx - 4)], inc=True)
                tok_gst[h * NTG + tg] = sch.op("S",
                    (lambda h, gb, tg, pbank: lambda e: e.activation(
                        out=gate_st[gb][:, tg * 512:(tg + 1) * 512], in_=pbank[:],
                        func=AF.Sigmoid))(h, gb, tg, pbank),
                    waits=[tok_gmm[(h, tg)], tok_gsd.get(h - 1)], inc=True)
                tok_evac[cidx] = tok_gst[h * NTG + tg]
            tok_gsd[h] = sch.op("Q", (lambda h, gb: lambda e: e.dma_start(
                out=gate_dram[h], in_=gate_st[gb][:]))(h, gb),
                waits=[tok_gst[h * NTG + NTG - 1]], inc=True, sem="gsd0")

        def emit_pn_ob(ob):
            prev_cast = tok_wob.get(ob - 1)
            for n in range(NBITS):
                u = ob * NBITS + n
                sl = u % NSLP
                sq = u % NSLQ
                tok_pn[(u, 0)] = sch.op("G",
                    (lambda ob, n, sl: lambda e: e.dma_start(
                        out=pnbg[sl][:].rearrange("p (c o) -> p c o", o=128),
                        in_=pwT_ap[n][:, ob * 128:(ob + 1) * 128].rearrange(
                            "(c p) o -> p c o", p=128)))(ob, n, sl),
                    waits=[tok_comb.get((u - NSLP, 1))],
                    inc=True, sem=f"pn{sl}")
                tok_pn[(u, 1)] = sch.op("Q",
                    (lambda ob, n, sq: lambda e: e.dma_start(
                        out=pnq[sq][:].rearrange("p (c o) -> p c o", o=128),
                        in_=nwT_ap[n][:, ob * 128:(ob + 1) * 128].rearrange(
                            "(c p) o -> p c o", p=128)))(ob, n, sq),
                    waits=[tok_comb.get((u - NSLQ, 0))],
                    inc=True, sem=f"pq{sq}")
                tok_sig[(u, 0)] = sch.op("S",
                    (lambda sl: lambda e: e.activation(
                        out=pnbg[sl][:], in_=pnbg[sl][:], func=AF.Sigmoid))(sl),
                    waits=[tok_pn[(u, 0)]], inc=True)
                tok_sig[(u, 1)] = sch.op("S",
                    (lambda sq: lambda e: e.activation(
                        out=pnq[sq][:], in_=pnq[sq][:], func=AF.Sigmoid))(sq),
                    waits=[tok_pn[(u, 1)]], inc=True)
                tok_comb[(u, 0)] = sch.op("V",
                    (lambda sl, sq: lambda e: e.tensor_sub(
                        out=pnbg[sl][:], in0=pnbg[sl][:], in1=pnq[sq][:]))(sl, sq),
                    waits=[tok_sig[(u, 0)], tok_sig[(u, 1)]], inc=True)
                if n == 0:
                    tok_comb[(u, 1)] = sch.op("V",
                        (lambda sl: lambda e: e.tensor_scalar_mul(
                            out=Wacc[:], in0=pnbg[sl][:],
                            scalar1=cpos[:, 0:1]))(sl),
                        hard=[tok_comb[(u, 0)], tcneg, prev_cast], inc=True)
                else:
                    tok_comb[(u, 1)] = sch.op("V",
                        (lambda n, sl: lambda e: e.scalar_tensor_tensor(
                            out=Wacc[:], in0=pnbg[sl][:],
                            scalar=cpos[:, n:n + 1],
                            in1=Wacc[:],
                            op0=ALU.mult, op1=ALU.add))(n, sl),
                        hard=[tok_comb[(u, 0)], tok_comb.get((u - 1, 1))], inc=True)
            tok_wob[ob] = sch.op("V", (lambda ob: lambda e: e.tensor_copy(
                out=WffnT[:, ob * D:(ob + 1) * D], in_=Wacc[:]))(ob),
                waits=[tok_xn],
                hard=[tok_comb[(ob * NBITS + NBITS - 1, 1)]], inc=True)

        def emit_ffn_ob(fc):
            for tg in range(NTG):
                cidx = fc * 8 + 4 + tg
                gidx = fc * NTG + tg
                pbank = (pfA + pS)[cidx % 4]

                def mmfn(fc, tg, pbank):
                    def f(e):
                        for ic in range(NFC):
                            r = e.matmul(pbank[:],
                                     WffnT[:, fc * D + ic * 128: fc * D + ic * 128 + 128],
                                     xnT[:, ic * T + tg * 512: ic * T + tg * 512 + 512],
                                     start=(ic == 0), stop=(ic == NFC - 1))
                        return r
                    return f
                tmm_ = sch.op("T", mmfn(fc, tg, pbank),
                    waits=[tok_wob[fc], tok_xn, tok_evac.get(cidx - 4)], inc=True)
                bse = gidx % 2
                tok_gse[gidx] = sch.op("V",
                    (lambda fc, tg, pbank, bse: lambda e: e.tensor_scalar(
                        out=g_st[bse][:], in0=pbank[:],
                        scalar1=biasW[:, fc:fc + 1], scalar2=1.0,
                        op0=ALU.add, op1=ALU.mult))(fc, tg, pbank, bse),
                    waits=[tmm_, tok_gev.get(gidx - 2)],
                    hard=[tbias], inc=True)
                tok_evac[cidx] = tok_gse[gidx]
                tok_gev[gidx] = sch.op("Q",
                    (lambda fc, tg, bse: lambda e: e.dma_start(
                        out=g_dram[fc, :, tg * 512:(tg + 1) * 512], in_=g_st[bse][:]))(fc, tg, bse),
                    waits=[tok_gse[gidx]], inc=True, sem=f"gev{bse}")

        emit_pn_ob(0)
        emit_pn_ob(1)
        emit_wg_load(0)
        emit_wg_load(1)
        for step in range(NH):
            emit_gate_head(step)
            if step + 2 < NH:
                emit_wg_load(step + 2)
            if step + 2 < NFC:
                emit_pn_ob(step + 2)
            emit_ffn_ob(step)
        tok_ffn_done = [tok_gev[NFC * NTG - 1], tok_gev[NFC * NTG - 2],
                        tok_gse[NFC * NTG - 1]]

    if upto == "BG":
        fin = sch.op("G", lambda e: e.dma_start(
            out=outT_ap[0:128, 0:512], in_=xnT[:, 0:512]),
            waits=tok_ffn_done + [tok_gsd[NH - 1]],
            inc=True, sem="fin")
        sch.op("V", lambda e: e.nop(), waits=[fin])
        return _finish(nc, sch, es, esABC)

    esPN.close()

    with contextlib.ExitStack() as esC:
        def sbC(name, shape, dtype):
            return esC.enter_context(nc.sbuf_tensor(name, list(shape), dtype))

        wq_sb = [sbC(f"wq_sb{i}", (128, NFC * 384), BF16) for i in range(2)]
        gate_h = [sbC(f"gate_h{i}", (128, T), BF16) for i in range(2)]
        q_dim = [sbC(f"q_dim{i}", (128, T), BF16) for i in range(2)]
        k_dim = [sbC(f"k_dim{i}", (128, T), BF16) for i in range(2)]
        v_dimb = sbC("v_dim", (128, T), BF16)
        ktok_raw = sbC("ktok_raw", (128, T), BF16)
        k_td = [sbC(f"k_td{i}", (128, T), BF16) for i in range(2)]
        v_tok = [sbC(f"v_tok{i}", (128, T), BF16) for i in range(2)]
        qdec_b = sbC("qdec_b", (128, 512), BF16)
        sT_b = sbC("sT_b", (128, 1024), BF16)
        sq_b = sbC("sq_b", (128, 512), BF16)
        kv_sb = sbC("kv_sb", (128, 128), F32)
        kv_bf = sbC("kv_bf", (128, 128), BF16)
        O_st = [sbC(f"O_st{i}", (128, T), BF16) for i in range(2)]
        qdec_all = sbC("qdec_all", (128, NH * BLK), BF16)
        decT_all = sbC("decT_all", (128, NH * 2 * BLK), BF16)
        kdec_all = sbC("kdec_all", (128, 2 * NH), F32)
        bd_all = sbC("bd_all", (128, NH), F32)
        dtmp = sbC("dtmp", (128, 2 * BLK), F32)

        tok_wq = {}
        tok_gld = {}
        tok_pev = {}
        tok_tr = {}
        tok_ktd = {}
        tok_vev = {}
        tok_qdecb = {}
        tok_og = {}
        tok_kvstt = {}
        tok_kvcast = {}
        tok_ssacc = {}
        tok_sq = {}
        tok_ost = {}
        tok_prod = {}
        tok_tst = {}

        tvjoinC = sch.op("V", lambda e: e.tensor_copy(
            out=join_sb[0:1, 0:1], in_=ones1[0:1, 0:1]),
            waits=tok_ffn_done + [tok_gsd[NH - 1]],
            inc=True)

        tok_decay = None
        tv_ = None
        ts3 = None
        for h in range(NH):
            w0 = [t_cst, tvjoinC, ts3,
                  tok_gse[NFC * NTG - 1] if h == 0 else None]
            tv_ = sch.op("V", (lambda h: lambda e: e.tensor_scalar_mul(
                out=dtmp[:, 0:BLK], in0=rampq_sb, scalar1=slope_sb[:, h:h + 1]))(h),
                waits=w0, hard=[tv_], inc=True)
            ts1 = sch.op("S", (lambda h: lambda e: e.activation(
                out=qdec_all[:, h * BLK:(h + 1) * BLK], in_=dtmp[:, 0:BLK],
                func=AF.Exp, scale=-1.0))(h), waits=[tv_], inc=True)
            tv_ = sch.op("V", (lambda h: lambda e: e.tensor_scalar_mul(
                out=dtmp[:, 0:2 * BLK], in0=rampmn_sb,
                scalar1=slope_sb[:, h:h + 1]))(h), waits=[ts1], inc=True)
            ts2 = sch.op("S", (lambda h: lambda e: e.activation(
                out=decT_all[:, h * 2 * BLK:(h + 1) * 2 * BLK], in_=dtmp[:, 0:2 * BLK],
                func=AF.Exp, scale=-1.0))(h), waits=[tv_], inc=True)
            tv_ = sch.op("V", (lambda h: lambda e: e.tensor_scalar_mul(
                out=dtmp[:, 0:2], in0=rampk_sb, scalar1=slope_sb[:, h:h + 1]))(h),
                waits=[ts2], inc=True)
            ts3 = sch.op("S", (lambda h: lambda e: e.activation(
                out=kdec_all[:, 2 * h:2 * h + 2], in_=dtmp[:, 0:2],
                func=AF.Exp, scale=-1.0))(h), waits=[tv_], inc=True)
            tok_decay = sch.op("S", (lambda h: lambda e: e.activation(
                out=bd_all[:, h:h + 1], in_=slope_sb[:, h:h + 1],
                func=AF.Exp, scale=-256.0))(h), inc=True)

        tok_wq[0] = sch.op("G", lambda e: e.dma_start(
            out=wq_sb[0][:].rearrange("p (c j) -> p c j", j=384),
            in_=wqkvT_ap[0].rearrange("(c p) j -> p c j", p=128)),
            waits=tok_ffn_done + [tok_gsd[NH - 1]], inc=True, sem="wq0")
        tok_gld[0] = sch.op("Q", lambda e: e.dma_start(
            out=gate_h[0][:], in_=gate_dram[0]),
            waits=tok_ffn_done + [tok_gsd[0], tok_gsd[NH - 1]],
            inc=True, sem="gld0")
        if c_heads > 1:
            tok_gld[1] = sch.op("Q", lambda e: e.dma_start(
                out=gate_h[1][:], in_=gate_dram[1]),
                waits=tok_ffn_done + [tok_gsd[1], tok_gsd[NH - 1]],
                inc=True, sem="gld1")
        tkv0 = {}

        def emit_prod_group(h, sel, tg):
            hb = h % 2
            dests = (q_dim[hb], k_dim[hb], v_dimb)
            gidx = (h * 3 + sel) * NTG + tg
            par = gidx % 2

            def mmfn(hb, sel, tg, par):
                def f(e):
                    r = None
                    for fc in range(NFC):
                        r = e.matmul(pfA[par][:],
                                     wq_sb[hb][:, fc * 384 + sel * 128: fc * 384 + (sel + 1) * 128],
                                     xnT[:, fc * T + tg * 512: fc * T + tg * 512 + 512],
                                     start=(fc == 0), stop=(fc == NFC - 1))
                    return r
                return f
            wmm = [tok_wq[h], tok_pev.get((gidx - 2, "g"))]
            wmm.extend(tok_ffn_done)
            if gidx < 2:
                wmm.append(tok_gse.get(NFC * NTG - 1))
            if sel == 0:
                wmm.append(tok_qdecb.get((h - 2, NB - 1)))
            if sel == 2:
                wmm.append(tok_tr.get(h - 1))
            tmm_ = sch.op("T", mmfn(hb, sel, tg, par), waits=wmm, inc=True)
            tok_pev[(gidx, "g")] = sch.op("S",
                (lambda d, tg, par: lambda e: e.activation(
                    out=d[:, tg * 512:(tg + 1) * 512], in_=pfA[par][:],
                    func=AF.Silu))(dests[sel], tg, par),
                waits=[tmm_], inc=True)
            if tg == NTG - 1:
                tok_pev[(h, sel)] = tok_pev[(gidx, "g")]
                if sel == 2:
                    tok_prod[h] = tok_pev[(h, 2)]

        def emit_transposes(h):
            hb = h % 2

            def trfn(srcbuf, b, base):
                def f(e):
                    r = None
                    for s in range(8):
                        r = e.transpose(pT[b][:, s * 128:(s + 1) * 128],
                                        srcbuf[:, (base + s) * 128:(base + s + 1) * 128],
                                        id_bf[:])
                    return r
                return f
            tev_prev = None
            for half in range(2):
                tk = sch.op("T", trfn(k_dim[hb], half, half * 8),
                    waits=[tok_pev[(h, 1)], t_idbf, tok_ktd.get((h - 1, half)),
                           tok_vev.get((h - 1, half)), tev_prev], inc=True)
                tev_prev = sch.op("V", (lambda half: lambda e: e.tensor_copy(
                    out=ktok_raw[:, half * 1024:(half + 1) * 1024], in_=pT[half][:]))(half),
                    waits=[tk, tvjoinC], inc=True)
            for half in range(2):
                tk = sch.op("T", trfn(v_dimb, half, half * 8),
                    waits=[tok_pev[(h, 2)], tev_prev], inc=True)
                tok_vev[(h, half)] = sch.op("V",
                    (lambda half, hb: lambda e: e.tensor_copy(
                        out=v_tok[hb][:, half * 1024:(half + 1) * 1024], in_=pT[half][:]))(half, hb),
                    waits=[tk], hard=[tok_kvstt.get((h - 2, NB - 1))], inc=True)
                tev_prev = tok_vev[(h, half)]
            tok_tr[h] = tk

            kr = ktok_raw[:].rearrange("p (t e) -> p t e", e=128)
            kdr = k_td[hb][:].rearrange("p (t e) -> p t e", e=128)
            for par_t in range(2):
                vi = kr[:, par_t:NTT:2, :]
                vo = kdr[:, par_t:NTT:2, :]
                tok_ktd[(h, par_t)] = sch.op("V",
                    (lambda vi, vo, h, par_t: lambda e: e.tensor_scalar_mul(
                        out=vo, in0=vi,
                        scalar1=kdec_all[:, 2 * h + par_t:2 * h + par_t + 1]))(vi, vo, h, par_t),
                    waits=[tok_decay],
                    hard=[tev_prev, tok_kvstt.get((h - 2, NB - 1))], inc=True)

        def emit_scan_a(h, j):
            hb = h % 2
            jp = j % 2
            if j == 0:
                tkv0[h] = sch.op("V", lambda e: e.memset(kv_sb[:], 0.0),
                    hard=[tok_kvcast.get((h - 1, NB - 1))], inc=True)
            tok_qdecb[(h, j)] = sch.op("V",
                (lambda h, j, jp, hb: lambda e: e.tensor_mul(
                    out=qdec_b[:, jp * 256:(jp + 1) * 256],
                    in0=q_dim[hb][:, j * 256:(j + 1) * 256],
                    in1=qdec_all[:, h * BLK:(h + 1) * BLK]))(h, j, jp, hb),
                waits=[tok_pev[(h, 0)], tok_decay], inc=True)

            def scfn(hb, j, jp):
                def f(e):
                    r = None
                    for nt in range(2):
                        r = e.matmul(pS[jp][:, nt * 256:(nt + 1) * 256],
                                     k_dim[hb][:, (2 * j + nt) * 128:(2 * j + nt + 1) * 128],
                                     q_dim[hb][:, j * 256:(j + 1) * 256],
                                     start=True, stop=True)
                    return r
                return f
            tsc = sch.op("T", scfn(hb, j, jp),
                waits=[tok_pev[(h, 1)], tok_pev[(h, 0)],
                       tok_kvstt.get((h, j - 2)), tok_ssacc.get((h, j - 2)),
                       tok_kvstt.get((h - 1, NB - 1)) if j < 2 else None,
                       tok_ssacc.get((h - 1, NB - 1)) if j < 2 else None],
                inc=True)
            tst = sch.op("V", (lambda h, jp: lambda e: e.tensor_tensor(
                out=sT_b[:, jp * 512:(jp + 1) * 512], in0=pS[jp][:],
                in1=decT_all[:, h * 512:(h + 1) * 512], op=ALU.mult))(h, jp),
                waits=[tsc], inc=True)
            tok_tst[(h, j)] = tst

        def emit_scan_b(h, j):
            hb = h % 2
            jp = j % 2
            tst = tok_tst[(h, j)]

            def ofn(h, j, jp, hb):
                def f(e):
                    r = None
                    first = True
                    if j > 0:
                        r = e.matmul(pO[jp][:, 0:256], kv_bf[:],
                                     qdec_b[:, jp * 256:(jp + 1) * 256],
                                     start=True, stop=False)
                        first = False
                    for nt in range(2):
                        r = e.matmul(pO[jp][:, 0:256],
                                     v_tok[hb][:, (2 * j + nt) * 128:(2 * j + nt + 1) * 128],
                                     sT_b[:, jp * 512 + nt * 256: jp * 512 + (nt + 1) * 256],
                                     start=first, stop=(nt == 1))
                        first = False
                    return r
                return f
            tog = sch.op("T", ofn(h, j, jp, hb),
                waits=[tst, tok_qdecb[(h, j)],
                       tok_kvcast.get((h, j - 1)),
                       tok_vev.get((h, 1)),
                       tok_sq.get((h, j - 2)), tok_og.get((h, j - 2)),
                       tok_sq.get((h - 1, NB - 1)) if j < 2 else None,
                       tok_og.get((h - 1, NB - 1)) if j < 2 else None],
                inc=True)

            def kvfn(h, j, jp, hb):
                def f(e):
                    r = None
                    for nt in range(2):
                        r = e.matmul(pS[jp][:, 0:128],
                                     k_td[hb][:, (2 * j + nt) * 128:(2 * j + nt + 1) * 128],
                                     v_tok[hb][:, (2 * j + nt) * 128:(2 * j + nt + 1) * 128],
                                     start=(nt == 0), stop=(nt == 1))
                    return r
                return f
            tkv = sch.op("T", kvfn(h, j, jp, hb),
                waits=[tok_ktd.get((h, 1)), tok_ktd.get((h, 0))], inc=True)
            tok_sq[(h, j)] = sch.op("S", (lambda jp: lambda e: e.activation(
                out=sq_b[:, jp * 256:(jp + 1) * 256], in_=pO[jp][:, 0:256],
                func=AF.Square))(jp), waits=[tog], inc=True)
            tss = sch.op("T", (lambda jp: lambda e: e.matmul(
                pS[jp][0:1, 128:384], ones_bf[:],
                sq_b[:, jp * 256:(jp + 1) * 256], start=True, stop=True))(jp),
                waits=[tok_sq[(h, j)], tkv], inc=True)
            tok_og[(h, j)] = sch.op("V",
                (lambda h, j, jp, hb: lambda e: e.scalar_tensor_tensor(
                    out=O_st[hb][:, j * 256:(j + 1) * 256], in0=pO[jp][:, 0:256],
                    scalar=normw_sb[:, h:h + 1],
                    in1=gate_h[hb][:, j * 256:(j + 1) * 256],
                    op0=ALU.mult, op1=ALU.mult))(h, j, jp, hb),
                waits=[tog, t_cst, tok_ost.get(h - 2), tok_sq[(h, j)],
                       tok_gld[h]], inc=True)
            tok_kvstt[(h, j)] = sch.op("V",
                (lambda h, jp: lambda e: e.scalar_tensor_tensor(
                    out=kv_sb[:], in0=kv_sb[:], scalar=bd_all[:, h:h + 1],
                    in1=pS[jp][:, 0:128], op0=ALU.mult, op1=ALU.add))(h, jp),
                waits=[tss], hard=[tkv0[h]], inc=True)
            tok_kvcast[(h, j)] = sch.op("V", lambda e: e.tensor_copy(
                out=kv_bf[:], in_=kv_sb[:]),
                hard=[tok_kvstt[(h, j)]], inc=True)
            tok_ssacc[(h, j)] = sch.op("V",
                (lambda j, jp: lambda e: e.tensor_tensor(
                    out=sumsq[0:1, j * 256:(j + 1) * 256],
                    in0=sumsq[0:1, j * 256:(j + 1) * 256],
                    in1=pS[jp][0:1, 128:384], op=ALU.add))(j, jp),
                waits=[tss], inc=True)

        for it in range(c_heads + 1):
            ph = it if it < c_heads else None
            sh = it - 1
            if ph is not None and ph + 1 < c_heads:
                tok_wq[ph + 1] = sch.op("G", (lambda ph: lambda e: e.dma_start(
                    out=wq_sb[(ph + 1) % 2][:].rearrange("p (c j) -> p c j", j=384),
                    in_=wqkvT_ap[ph + 1].rearrange("(c p) j -> p c j", p=128)))(ph),
                    waits=[tok_prod.get(ph - 1),
                           tok_ffn_done[0] if ph == 0 else None],
                    inc=True, sem=f"wq{(ph + 1) % 2}")
            gitems = ([("p", s, t) for s in range(3) for t in range(NTG)]
                      if ph is not None else [])
            bitems = ([x for j in range(NB) for x in (("a", j), ("b", j))]
                      if sh >= 0 else [])
            merged = []
            gi = bi = 0
            while gi < len(gitems) or bi < len(bitems):
                if bi >= len(bitems) or (gi < len(gitems) and
                                         gi * (len(bitems) + 1) <= bi * (len(gitems) + 1)):
                    merged.append(gitems[gi]); gi += 1
                else:
                    merged.append(bitems[bi]); bi += 1
            for item in merged:
                if item[0] == "p":
                    emit_prod_group(ph, item[1], item[2])
                elif item[0] == "a":
                    emit_scan_a(sh, item[1])
                else:
                    emit_scan_b(sh, item[1])
            if sh >= 0:
                hb = sh % 2
                tok_ost[sh] = sch.op("Q", (lambda sh, hb: lambda e: e.dma_start(
                    out=o_dram[sh], in_=O_st[hb][:]))(sh, hb),
                    waits=[tok_og[(sh, NB - 1)]], inc=True, sem=f"ost{hb}")
                if sh + 2 < c_heads:
                    tok_gld[sh + 2] = sch.op("Q", (lambda sh: lambda e: e.dma_start(
                        out=gate_h[sh % 2][:], in_=gate_dram[sh + 2]))(sh),
                        waits=[tok_gsd[sh + 2], tok_og[(sh, NB - 1)]],
                        inc=True, sem=f"gld{sh % 2}")
            if ph is not None:
                emit_transposes(ph)

        tok_c_done = [tok_ost.get(c_heads - 1), tok_ost.get(c_heads - 2),
                      tok_decay]
        tok_c_v = tok_ssacc.get((NH - 1, NB - 1))
        tok_c_s = tok_sq.get((NH - 1, NB - 1))

    if upto == "C":
        fin = sch.op("G", lambda e: e.dma_start(
            out=outT_ap[0:128, 0:512], in_=xnT[:, 0:512]),
            waits=tok_c_done + [tok_c_v], inc=True, sem="fin")
        sch.op("V", lambda e: e.nop(), waits=[fin])
        return _finish(nc, sch, es, esABC)

    esABC.close()

    with contextlib.ExitStack() as esD:
        def sbD(name, shape, dtype):
            return esD.enter_context(nc.sbuf_tensor(name, list(shape), dtype))

        wout_sb = sbD("wout_sb", (128, NFC * D), BF16)
        O_sb = sbD("O_sb", (128, NH * T), BF16)
        rsq_rep = sbD("rsq_rep", (128, T), F32)
        g_fc = [sbD(f"g_fc{i}", (128, T), BF16) for i in range(2)]
        xT_fc = [sbD(f"xT_fc{i}", (128, T), F32) for i in range(2)]
        s1 = [sbD(f"s1_{i}", (128, 512), F32) for i in range(2)]
        outb = [sbD(f"outb{i}", (128, 512), F32) for i in range(2)]
        rtmp = sbD("rtmp", (1, T), F32)
        rvec = sbD("rvec", (1, T), F32)
        rtmp2 = sumsq

        tjoin = sch.op("V", lambda e: e.tensor_copy(
            out=join_sb[0:1, 1:2], in_=ones1[0:1, 0:1]), waits=tok_c_done, inc=True)
        tsjoin = sch.op("S", lambda e: e.copy(
            out=join_sb[0:1, 2:3], in_=ones1[0:1, 0:1]), waits=[tok_c_s], inc=True)

        tok_wout = {}
        for fc in range(NFC):
            tok_wout[fc] = sch.op("G", (lambda fc: lambda e: e.dma_start(
                out=wout_sb[:, fc * NH * 128:(fc + 1) * NH * 128].rearrange(
                    "p (c f) -> p c f", f=128),
                in_=woutT_ap[:, fc * 128:(fc + 1) * 128].rearrange(
                    "(c p) f -> p c f", p=128)))(fc),
                waits=[tok_prod[NH - 1], tjoin, tsjoin] if fc == 0 else [],
                inc=True, sem=f"wou{fc}")
        tok_oload = sch.op("Q", lambda e: e.dma_start(
            out=O_sb[:].rearrange("p (c t) -> p c t", t=T),
            in_=o_dram.rearrange("c p t -> p c t")),
            waits=tok_c_done + [tjoin, tsjoin], inc=True, sem="olo")

        tsr2 = sch.op("S", lambda e: e.activation(
            out=rtmp[:], in_=sumsq[:], func=AF.Sqrt, scale=1.0 / D,
            bias=eps_sb[0:1, 1:2]), waits=[tok_c_v, tsjoin], inc=True)
        tn = sch.op("V", lambda e: e.reciprocal(out=rvec[:], in_=rtmp[:]),
                    waits=[tsr2], hard=[tjoin], inc=True)
        tn = sch.op("V", lambda e: e.tensor_scalar(
            out=rtmp2[:], in0=sumsq[:], scalar1=1.0 / D, scalar2=1e-6,
            op0=ALU.mult, op1=ALU.add), waits=[tsr2], hard=[tn], inc=True)
        tn = sch.op("V", lambda e: e.tensor_mul(
            out=rtmp[:], in0=rvec[:], in1=rvec[:]), hard=[tn], inc=True)
        tn = sch.op("V", lambda e: e.tensor_mul(
            out=rtmp[:], in0=rtmp[:], in1=rtmp2[:]), hard=[tn], inc=True)
        tn = sch.op("V", lambda e: e.tensor_scalar(
            out=rtmp[:], in0=rtmp[:], scalar1=-0.5, scalar2=1.5,
            op0=ALU.mult, op1=ALU.add), hard=[tn], inc=True)
        trv = sch.op("V", lambda e: e.tensor_mul(
            out=rvec[:], in0=rvec[:], in1=rtmp[:]), hard=[tn], inc=True)
        trep2 = {}
        for g in range(NTG):
            par = g % 2
            ta = sch.op("T", (lambda g, par: lambda e: e.matmul(
                pO[par][:, 0:512], ones1[:], rvec[0:1, g * 512:(g + 1) * 512],
                start=True, stop=True))(g, par),
                waits=[trv, tok_c_v, trep2.get(g - 2)], inc=True)
            trep2[g] = sch.op("V", (lambda g, par: lambda e: e.tensor_copy(
                out=rsq_rep[:, g * 512:(g + 1) * 512], in_=pO[par][:, 0:512]))(g, par),
                waits=[ta], inc=True)
        tok_rsq = trep2[NTG - 1]

        tok_gl = {}
        tok_xl = {}
        tok_out = {}
        tok_ev3 = {}
        for fc in range(NFC):
            fb = fc % 2
            tok_gl[fc] = sch.op("Q", (lambda fc, fb: lambda e: e.dma_start(
                out=g_fc[fb][:], in_=g_dram[fc]))(fc, fb),
                waits=[tok_ev3.get((fc - 2, NTG - 1)), tjoin, tsjoin] + tok_ffn_done,
                inc=True, sem=f"gl{fb}")
            tok_gl[fc] = sch.op("S", (lambda fc, fb: lambda e: e.activation(
                out=g_fc[fb][:], in_=g_fc[fb][:], func=AF.Silu))(fc, fb),
                waits=[tok_gl[fc]], inc=True)
            tok_xl[fc] = sch.op("Q", (lambda fc, fb: lambda e: e.dma_start(
                out=xT_fc[fb][:], in_=xT_ap[fc * 128:(fc + 1) * 128, :]))(fc, fb),
                waits=[tok_ev3.get((fc - 2, NTG - 1)), tjoin, tsjoin],
                inc=True, sem=f"xl{fb}")
            for tg in range(NTG):
                gidx = fc * NTG + tg
                pbank = (pfA + pS)[gidx % 4]
                par = gidx % 2

                def mmfn(fc, tg, pbank):
                    def f(e):
                        base = fc * NH * 128
                        for oc in range(NH):
                            r = e.matmul(pbank[:],
                                     wout_sb[:, base + oc * 128: base + (oc + 1) * 128],
                                     O_sb[:, oc * T + tg * 512: oc * T + tg * 512 + 512],
                                     start=(oc == 0), stop=(oc == NH - 1))
                        return r
                    return f
                tmm_ = sch.op("T", mmfn(fc, tg, pbank),
                    waits=[tok_wout[fc], tok_oload, tok_ev3.get((gidx - 4, "s"))], inc=True)
                tv1 = sch.op("V", (lambda par, tg, pbank: lambda e: e.tensor_tensor(
                    out=s1[par][:], in0=pbank[:],
                    in1=rsq_rep[:, tg * 512:(tg + 1) * 512], op=ALU.mult))(par, tg, pbank),
                    waits=[tmm_],
                    hard=[tok_rsq, tok_ev3.get((gidx - 2, "v3"))], inc=True)
                tok_ev3[(gidx, "s")] = tv1
                tv2 = sch.op("V", (lambda par, fb, tg: lambda e: e.tensor_tensor(
                    out=s1[par][:], in0=s1[par][:],
                    in1=g_fc[fb][:, tg * 512:(tg + 1) * 512], op=ALU.mult))(par, fb, tg),
                    waits=[tok_gl[fc]], hard=[tv1], inc=True)
                tv3 = sch.op("V", (lambda par, fb, tg: lambda e: e.tensor_tensor(
                    out=outb[par][:], in0=s1[par][:],
                    in1=xT_fc[fb][:, tg * 512:(tg + 1) * 512], op=ALU.add))(par, fb, tg),
                    waits=[tok_xl[fc], tok_out.get(gidx - 2)], hard=[tv2], inc=True)
                tok_ev3[(fc, tg)] = tv3
                tok_ev3[(gidx, "v3")] = tv3
                tok_out[gidx] = sch.op("Q",
                    (lambda fc, tg, par: lambda e: e.dma_start(
                        out=outT_ap[fc * 128:(fc + 1) * 128, tg * 512:(tg + 1) * 512],
                        in_=outb[par][:]))(fc, tg, par),
                    waits=[tv3], inc=True, sem=f"out{par}")

        finals = [tok_out[NFC * NTG - 1], tok_out[NFC * NTG - 2]]
        sch.op("Q", lambda e: e.nop(), waits=finals)
        sch.op("V", lambda e: e.nop(), waits=finals)

    return _finish(nc, sch, es, esABC)


def _finish(nc, sch, es, esABC):
    semstack = contextlib.ExitStack()
    sems = {}
    for name in sorted(sch.sem_names):
        sems[name] = semstack.enter_context(nc.semaphore(f"sem_{name}"))
    with nc.Block() as block:
        @block.tensor
        def _(eng):
            sch.replay("T", eng, sems)

        @block.vector
        def _(eng):
            sch.replay("V", eng, sems)

        @block.scalar
        def _(eng):
            sch.replay("S", eng, sems)

        @block.sync
        def _(eng):
            sch.replay("Q", eng, sems)

        @block.gpsimd
        def _(eng):
            sch.replay("G", eng, sems)

    semstack.close()
    try:
        esABC.close()
    except Exception:
        pass
    es.close()
    return nc


def host_inputs(inputs):
    x = np.asarray(inputs["L0_attn_hidden_states"], dtype=np.float32)
    slope = np.asarray(inputs["L0_attn_slope_rate"], dtype=np.float32).reshape(NH)
    w_qkv = np.asarray(inputs["L0_attn_w_qkv"], dtype=np.float32)
    w_gate = np.asarray(inputs["L0_attn_w_gate"], dtype=np.float32)
    w_out = np.asarray(inputs["L0_attn_w_out"], dtype=np.float32)
    norm_w = np.asarray(inputs["L0_attn_norm_w"], dtype=np.float32)
    pweight = np.asarray(inputs["L0_ffn_pweight"], dtype=np.float32)
    nweight = np.asarray(inputs["L0_ffn_nweight"], dtype=np.float32)
    exps = np.asarray(inputs["L0_ffn_exps"], dtype=np.float32)
    bexps = np.asarray(inputs["L0_ffn_bexps"], dtype=np.float32)
    mask_w = np.asarray(inputs["L0_ffn_mask_weight"], dtype=np.float32)
    scale = np.asarray(inputs["L0_ffn_scale"], dtype=np.float32)
    pbias = np.asarray(inputs["L0_ffn_pbias"], dtype=np.float32)
    nbias = np.asarray(inputs["L0_ffn_nbias"], dtype=np.float32)
    bscale = np.asarray(inputs["L0_ffn_biasscale"], dtype=np.float32)

    C = np.ascontiguousarray
    consts = np.zeros((128, CW), dtype=np.float32)

    def setc(name, val):
        a, b = _CO[name]
        consts[:, a:b] = val

    p = np.arange(128, dtype=np.float32)
    m = np.arange(BLK, dtype=np.float32).reshape(1, BLK)
    setc("id", np.eye(128, dtype=np.float32))
    setc("rampq", np.arange(1, BLK + 1, dtype=np.float32).reshape(1, BLK))
    grids = []
    for tile in range(2):
        n = (p + 128 * tile).reshape(128, 1)
        g = m - n
        grids.append(np.where(g >= 0, g, np.float32(1e30)).astype(np.float32))
    setc("rampmn", np.concatenate(grids, axis=1))
    setc("rampk", np.stack([255.0 - p, 127.0 - p], axis=1))
    setc("normw", norm_w.reshape(NH, 128).T)
    setc("slope", slope.reshape(1, NH))
    setc("pb2", pbias.reshape(NH, 128, NBITS).transpose(1, 0, 2).reshape(128, NH * NBITS))
    setc("nb2", nbias.reshape(NH, 128, NBITS).transpose(1, 0, 2).reshape(128, NH * NBITS))
    setc("exps", exps.reshape(1, NBITS))
    setc("mw", mask_w.reshape(1, NBITS))
    setc("bexps", bexps.reshape(1, NBITS))
    setc("scale", scale.reshape(1, 1))
    setc("bscale", bscale.reshape(1, 1))

    shared = {
        "wqkvT": C(w_qkv.reshape(NH, 384, D).transpose(0, 2, 1)),
        "wgateT": C(w_gate.reshape(NH, HD, D).transpose(0, 2, 1)),
        "woutT": C(w_out.T),
        "pwT": C(pweight.transpose(2, 1, 0)),
        "nwT": C(nweight.transpose(2, 1, 0)),
        "consts": consts,
    }
    in_maps = []
    for c in range(NCORES):
        im = dict(shared)
        im["x"] = C(x[c])
        im["xT"] = C(x[c].T)
        in_maps.append(im)
    return in_maps


def kernel(**inputs):
    in_maps = host_inputs(inputs)
    nc = build_program()
    res = run_bass_kernel_spmd(nc, in_maps, core_ids=list(range(NCORES)))
    out = np.stack([np.ascontiguousarray(r["outT"].T) for r in res.results])
    return out


# version 24
# speedup vs baseline: 1.0985x; 1.0603x over previous
import contextlib
import numpy as np
import concourse.bass as bass
import concourse.mybir as mybir
from concourse.bass_utils import run_bass_kernel_spmd

F32 = mybir.dt.float32
BF16 = mybir.dt.bfloat16
AF = mybir.ActivationFunctionType
ALU = mybir.AluOpType
AX = mybir.AxisListType

NCORES = 8
T = 2048
D = 2048
NH = 16
HD = 128
BLK = 256
NB = T // BLK
NTT = T // 128
NFC = D // 128
NTG = T // 512
NBITS = 4

ENGS = ("T", "V", "S", "Q", "G")

_CO = {}
_off = 0
for _name, _w in [("id", 128), ("rampq", BLK), ("rampmn", 2 * BLK), ("rampk", 2),
                  ("normw", NH), ("slope", NH), ("pb2", NH * NBITS),
                  ("nb2", NH * NBITS), ("exps", NBITS), ("mw", NBITS),
                  ("bexps", NBITS), ("scale", 1), ("bscale", 1)]:
    _CO[_name] = (_off, _off + _w)
    _off += _w
CW = _off


class Sched:

    def __init__(self):
        self.ops = {e: [] for e in ENGS}
        self.cnt = {}
        self.sem_names = set(["T", "V", "S"])

    def op(self, eng, fn, waits=(), inc=False, sem=None, hard=()):
        if eng in ("Q", "G"):
            amt = 16
            assert sem is not None or not inc
        else:
            amt = 1
            sem = eng
        ws = []
        for w in waits:
            if w is None:
                continue
            pse, val = w
            if eng not in ("Q", "G") and pse == eng:
                continue
            ws.append((pse, val))
        for w in hard:
            if w is None:
                continue
            ws.append(tuple(w))
        self.ops[eng].append((ws, fn, inc, sem))
        if inc:
            self.sem_names.add(sem)
            self.cnt[sem] = self.cnt.get(sem, 0) + amt
            return (sem, self.cnt[sem])
        return None

    def replay(self, eng_name, eng_obj, sems):
        maxw = {}
        for ws, fn, inc, sem in self.ops[eng_name]:
            for pse, val in ws:
                if maxw.get(pse, 0) >= val:
                    continue
                maxw[pse] = val
                eng_obj.wait_ge(sems[pse], val)
            inst = fn(eng_obj)
            if inc:
                amt = 16 if eng_name in ("Q", "G") else 1
                if inst is None:
                    inst = eng_obj.nop()
                inst.then_inc(sems[sem], amt)


def build_program(upto="D", c_heads=NH):
    nc = bass.Bass("TRN2", target_bir_lowering=False)

    def din(name, shape):
        return nc.dram_tensor(name, list(shape), F32, kind="ExternalInput").ap()

    x_ap = din("x", (T, D))
    xT_ap = din("xT", (D, T))
    wqkvT_ap = din("wqkvT", (NH, D, 384))
    wgateT_ap = din("wgateT", (NH, D, HD))
    woutT_ap = din("woutT", (D, D))
    pwT_ap = din("pwT", (NBITS, D, D))
    nwT_ap = din("nwT", (NBITS, D, D))
    consts_ap = din("consts", (128, CW))

    outT_ap = nc.dram_tensor("outT", [D, T], F32, kind="ExternalOutput").ap()

    g_dram = nc.dram_tensor("g_scr", [NFC, 128, T], BF16, kind="Internal").ap()
    gate_dram = nc.dram_tensor("gate_scr", [NH, 128, T], BF16, kind="Internal").ap()
    o_dram = nc.dram_tensor("o_scr", [NH, 128, T], BF16, kind="Internal").ap()

    sch = Sched()
    es = contextlib.ExitStack()

    def sb(name, shape, dtype):
        return es.enter_context(nc.sbuf_tensor(name, list(shape), dtype))

    def ps(name, shape, dtype):
        return es.enter_context(nc.psum_tensor(name, list(shape), dtype))

    csb = sb("csb", (128, CW), F32)

    def cc(name):
        a, b = _CO[name]
        return csb[:, a:b]

    id_f32 = cc("id")
    rampq_sb = cc("rampq")
    rampmn_sb = cc("rampmn")
    rampk_sb = cc("rampk")
    normw_sb = cc("normw")
    slope_sb = cc("slope")
    pb2_sb = cc("pb2")
    nb2_sb = cc("nb2")

    id_bf = sb("id_bf", (128, 128), BF16)
    ones_bf = sb("ones_bf", (128, 1), BF16)
    ones1 = sb("ones1", (1, 128), F32)
    biasW = sb("biasW", (128, NFC), F32)
    sumsq = sb("sumsq", (1, T), F32)
    eps_sb = sb("eps_sb", (128, 2), F32)
    cpos = sb("cpos", (128, NBITS), F32)
    cneg = sb("cneg", (128, NBITS), F32)
    sgm = sb("sgm", (128, NBITS), F32)
    join_sb = sb("join_sb", (1, 8), F32)

    pfA = [ps(f"pfA{i}", (128, 512), F32) for i in range(2)]
    pS = [ps(f"pS{i}", (128, 512), F32) for i in range(2)]
    pO = [ps(f"pO{i}", (128, 512), F32) for i in range(2)]
    pT = [ps(f"pT{i}", (128, 1024), BF16) for i in range(2)]

    t_cst = sch.op("Q", lambda e: e.dma_start(out=csb[:], in_=consts_ap),
                   inc=True, sem="cst")
    tok_wq0 = sch.op("G", lambda e: e.dma_start(
        out=wq_sb0[:].rearrange("p (c j) -> p c j", j=384),
        in_=wqkvT_ap[0].rearrange("(c p) j -> p c j", p=128)),
        inc=True, sem="wq0")
    t_idbf = sch.op("V", lambda e: e.tensor_copy(out=id_bf[:], in_=id_f32),
                    waits=[t_cst], inc=True)
    sch.op("V", lambda e: e.memset(ones_bf[:], 1.0))
    sch.op("V", lambda e: e.memset(ones1[:], 1.0))
    sch.op("V", lambda e: e.memset(eps_sb[:, 0:1], 1e-5))
    sch.op("V", lambda e: e.memset(eps_sb[:, 1:2], 1e-6))
    sch.op("V", lambda e: e.memset(sumsq[:], 0.0))

    esABC = contextlib.ExitStack()
    xnT = esABC.enter_context(nc.sbuf_tensor("xnT", [128, NFC * T], BF16))
    wq_sb0 = esABC.enter_context(nc.sbuf_tensor("wq_sb0", [128, NFC * 384], BF16))
    esPN = contextlib.ExitStack()
    pnbg = [esPN.enter_context(nc.sbuf_tensor(f"pnbg{i}", [128, D // 2], BF16))
            for i in range(8)]
    sgp = [esPN.enter_context(nc.sbuf_tensor(f"sgp{i}", [128, D // 2], BF16))
           for i in range(2)]
    sgn = [esPN.enter_context(nc.sbuf_tensor(f"sgn{i}", [128, D // 2], BF16))
           for i in range(2)]
    Wacc = esPN.enter_context(nc.sbuf_tensor("Wacc", [128, D], F32))
    with contextlib.ExitStack() as esA:
        def sbA(name, shape, dtype):
            return esA.enter_context(nc.sbuf_tensor(name, list(shape), dtype))

        xtile = [sbA(f"xtile{i}", (128, D), F32) for i in range(3)]
        xntok = [sbA(f"xntok{i}", (128, D), BF16) for i in range(2)]
        junkA = sbA("junkA", (128, D), BF16)
        mneg_c = sbA("mneg_c", (128, NTT), F32)
        vss_c = sbA("vss_c", (128, NTT), F32)
        y_c = sbA("y_c", (128, NTT), F32)
        s_c = sbA("s_c", (128, NTT), F32)
        r_c = sbA("r_c", (128, NTT), F32)
        t_c = sbA("t_c", (128, NTT), F32)

        tload = [None, None, None]
        tsq = {}
        txn = {}
        tev = {}

        def a_stage1(tt):
            b = tt % 3
            tload[b] = sch.op("Q", (lambda tt, b: lambda e: e.dma_start(
                out=xtile[b][:], in_=x_ap[tt * 128:(tt + 1) * 128, :]))(tt, b),
                waits=[txn.get(tt - 3)], inc=True, sem=f"xa{b}")
            tred = sch.op("V", (lambda tt, b: lambda e: e.tensor_reduce(
                out=mneg_c[:, tt:tt + 1], in_=xtile[b][:], axis=AX.X, op=ALU.add))(tt, b),
                waits=[tload[b]], inc=True)
            tr2 = sch.op("V", (lambda tt: lambda e: e.tensor_scalar_mul(
                out=mneg_c[:, tt:tt + 1], in0=mneg_c[:, tt:tt + 1],
                scalar1=-1.0 / D))(tt), hard=[tred], inc=True)
            tsq[tt] = sch.op("S", (lambda tt, b: lambda e: e.activation(
                out=junkA[:], in_=xtile[b][:], func=AF.Square,
                bias=mneg_c[:, tt:tt + 1], accum_out=vss_c[:, tt:tt + 1]))(tt, b),
                waits=[tr2], hard=[tsq.get(tt - 1)], inc=True)

        def a_stage2(tt):
            b = tt % 3
            b2 = tt % 2
            tyv = sch.op("V", (lambda tt: lambda e: e.tensor_scalar(
                out=y_c[:, tt:tt + 1], in0=vss_c[:, tt:tt + 1],
                scalar1=1.0 / D, scalar2=1e-5,
                op0=ALU.mult, op1=ALU.add))(tt), waits=[tsq[tt]], inc=True)
            tsr = sch.op("S", (lambda tt: lambda e: e.activation(
                out=s_c[:, tt:tt + 1], in_=y_c[:, tt:tt + 1],
                func=AF.Sqrt))(tt), waits=[tyv], inc=True)
            tn = sch.op("V", (lambda tt: lambda e: e.reciprocal(
                out=r_c[:, tt:tt + 1], in_=s_c[:, tt:tt + 1]))(tt),
                waits=[tsr], inc=True)
            tn = sch.op("V", (lambda tt: lambda e: e.tensor_mul(
                out=t_c[:, tt:tt + 1], in0=r_c[:, tt:tt + 1],
                in1=r_c[:, tt:tt + 1]))(tt), hard=[tn], inc=True)
            tn = sch.op("V", (lambda tt: lambda e: e.tensor_mul(
                out=t_c[:, tt:tt + 1], in0=t_c[:, tt:tt + 1],
                in1=y_c[:, tt:tt + 1]))(tt), hard=[tn], inc=True)
            tn = sch.op("V", (lambda tt: lambda e: e.tensor_scalar(
                out=t_c[:, tt:tt + 1], in0=t_c[:, tt:tt + 1],
                scalar1=-0.5, scalar2=1.5,
                op0=ALU.mult, op1=ALU.add))(tt), hard=[tn], inc=True)
            tn = sch.op("V", (lambda tt: lambda e: e.tensor_mul(
                out=r_c[:, tt:tt + 1], in0=r_c[:, tt:tt + 1],
                in1=t_c[:, tt:tt + 1]))(tt), hard=[tn], inc=True)
            txn[tt] = sch.op("V", (lambda tt, b, b2: lambda e: e.tensor_scalar(
                out=xntok[b2][:], in0=xtile[b][:],
                scalar1=mneg_c[:, tt:tt + 1], scalar2=r_c[:, tt:tt + 1],
                op0=ALU.add, op1=ALU.mult))(tt, b, b2),
                hard=[tn, tev.get((tt - 2, 1))], inc=True)
            for half in range(2):
                def trfnA(b2, half, tt):
                    def f(e):
                        r = None
                        for s in range(8):
                            fc = half * 8 + s
                            r = e.transpose(pT[half][:, s * 128:(s + 1) * 128],
                                            xntok[b2][:, fc * 128:(fc + 1) * 128],
                                            id_bf[:])
                        return r
                    return f
                tk = sch.op("T", trfnA(b2, half, tt),
                    waits=[txn[tt], t_idbf, tev.get((tt - 1, half))], inc=True)
                xv = xnT[:].rearrange("p (c t) -> p c t", t=T)
                tev[(tt, half)] = sch.op("V",
                    (lambda tt, half, xv: lambda e: e.tensor_copy(
                        out=xv[:, half * 8:(half + 1) * 8, tt * 128:(tt + 1) * 128],
                        in_=pT[half][:].rearrange("p (s t) -> p s t", t=128)))(tt, half, xv),
                    waits=[tk], inc=True)

        for tt in range(NTT + 1):
            if tt < NTT:
                a_stage1(tt)
            if tt >= 1:
                a_stage2(tt - 1)
        tok_xn = sch.op("V", lambda e: e.tensor_copy(
            out=join_sb[0:1, 3:4], in_=ones1[0:1, 0:1]),
            hard=[tev[(NTT - 1, 0)], tev[(NTT - 1, 1)]], inc=True)

    tok_gsd = {}
    with contextlib.ExitStack() as esB:
        def sbB(name, shape, dtype):
            return esB.enter_context(nc.sbuf_tensor(name, list(shape), dtype))

        WffnT = sbB("WffnT", (128, NFC * D), BF16)
        wg_sb = [sbB(f"wg_sb{i}", (128, NFC * HD), BF16) for i in range(2)]
        wg_f32 = sbB("wg_f32", (128, NFC * HD), F32)
        gate_st = [sbB("gate_stA", (128, T), BF16)]
        g_st = [sbB(f"g_st{i}", (128, 512), BF16) for i in range(2)]
        btmp = sbB("btmp", (128, NH * NBITS), F32)
        bacc = sbB("bacc", (128, NFC), F32)
        br = sbB("br", (128, NFC), F32)

        tcs = sch.op("S", lambda e: e.activation(
            out=sgm[:], in_=cc("mw"), func=AF.Sigmoid), waits=[t_cst], inc=True)
        tc1 = sch.op("V", lambda e: e.tensor_mul(
            out=cpos[:], in0=sgm[:], in1=cc("exps")), waits=[tcs], inc=True)
        tc2 = sch.op("V", lambda e: e.tensor_scalar_mul(
            out=cpos[:], in0=cpos[:], scalar1=cc("scale")), hard=[tc1], inc=True)
        tcneg = sch.op("V", lambda e: e.tensor_scalar_mul(
            out=cneg[:], in0=cpos[:], scalar1=-1.0), hard=[tc2], inc=True)

        tb0 = sch.op("V", lambda e: e.tensor_sub(
            out=btmp[:], in0=pb2_sb, in1=nb2_sb),
            waits=[t_cst, tok_xn], inc=True)
        btr = btmp[:].rearrange("p (g n) -> p n g", n=NBITS)
        tb1 = tb0
        for n in range(NBITS):
            bx = cc("bexps")
            if n == 0:
                tb1 = sch.op("V", (lambda n, bx: lambda e: e.tensor_scalar_mul(
                    out=bacc[:], in0=btr[:, n, :], scalar1=bx[:, n:n + 1]))(n, bx),
                    hard=[tb1], inc=True)
            else:
                tb1 = sch.op("V", (lambda n, bx: lambda e: e.scalar_tensor_tensor(
                    out=bacc[:], in0=btr[:, n, :], scalar=bx[:, n:n + 1],
                    in1=bacc[:], op0=ALU.mult, op1=ALU.add))(n, bx),
                    hard=[tb1], inc=True)
        tb1 = sch.op("V", lambda e: e.tensor_scalar(
            out=bacc[:], in0=bacc[:], scalar1=1.0, scalar2=-1.0,
            op0=ALU.min, op1=ALU.max), hard=[tb1], inc=True)
        tb2 = sch.op("S", lambda e: e.activation(
            out=br[:], in_=bacc[:], func=AF.Abs, scale=15.0), waits=[tb1], inc=True)
        tb3 = sch.op("V", lambda e: e.tensor_single_scalar(
            out=btmp[:, 0:NFC], in_=br[:], scalar=0.5, op=ALU.is_ge),
            waits=[tb2], inc=True)
        for k in range(1, 15):
            tb3 = sch.op("V", (lambda k: lambda e: e.scalar_tensor_tensor(
                out=btmp[:, 0:NFC], in0=br[:], scalar=k + 0.5, in1=btmp[:, 0:NFC],
                op0=ALU.is_ge, op1=ALU.add))(k), hard=[tb3], inc=True)
        tb4 = sch.op("S", lambda e: e.activation(
            out=br[:], in_=bacc[:], func=AF.Sign), waits=[tb3], hard=[tb2], inc=True)
        tbm = sch.op("V", lambda e: e.tensor_mul(
            out=btmp[:, 0:NFC], in0=btmp[:, 0:NFC], in1=br[:]),
            waits=[tb4], hard=[tb3], inc=True)
        tbias = sch.op("V", lambda e: e.tensor_scalar(
            out=biasW[:], in0=btmp[:, 0:NFC], scalar1=1.0 / 15.0,
            scalar2=cc("bscale"), op0=ALU.mult, op1=ALU.mult),
            hard=[tbm], inc=True)

        tok_wg = {}
        tok_gmm = {}
        tok_gst = {}
        tok_pn = {}
        tok_sig = {}
        tok_comb = {}
        tok_wchunk = {}

        tok_wgl = {}

        def emit_wg_load(h):
            b = h % 2
            tok_wgl[h] = sch.op("Q", (lambda h: lambda e: e.dma_start(
                out=wg_f32[:].rearrange("p (c j) -> p c j", j=HD),
                in_=wgateT_ap[h].rearrange("(c p) j -> p c j", p=128)))(h),
                waits=[tok_wg.get(h - 1),
                       tok_xn if h < 2 else None], inc=True, sem="wgf")
            tok_wg[h] = sch.op("V", (lambda h, b: lambda e: e.tensor_copy(
                out=wg_sb[b][:], in_=wg_f32[:]))(h, b),
                waits=[tok_wgl[h], tok_gmm.get((h - 2, NTG - 1)),
                       tok_xn if h < 2 else None], inc=True)

        def emit_gate_head(h):
            b = h % 2
            gb = 0
            for tg in range(NTG):
                gi4 = h * NTG + tg
                pbank = (pfA + pS)[gi4 % 4]

                def mmfn(h, b, tg, pbank):
                    def f(e):
                        for fc in range(NFC):
                            r = e.matmul(pbank[:],
                                     wg_sb[b][:, fc * HD:(fc + 1) * HD],
                                     xnT[:, fc * T + tg * 512: fc * T + tg * 512 + 512],
                                     start=(fc == 0), stop=(fc == NFC - 1))
                        return r
                    return f
                tok_gmm[(h, tg)] = sch.op("T", mmfn(h, b, tg, pbank),
                    waits=[tok_wg[h], tok_xn,
                           tok_gst.get(gi4 - 4)], inc=True)
                tok_gst[h * NTG + tg] = sch.op("S",
                    (lambda h, gb, tg, pbank: lambda e: e.activation(
                        out=gate_st[gb][:, tg * 512:(tg + 1) * 512], in_=pbank[:],
                        func=AF.Sigmoid))(h, gb, tg, pbank),
                    waits=[tok_gmm[(h, tg)], tok_gsd.get(h - 1)], inc=True)
            tok_gsd[h] = sch.op("Q", (lambda h, gb: lambda e: e.dma_start(
                out=gate_dram[h], in_=gate_st[gb][:]))(h, gb),
                waits=[tok_gst[h * NTG + NTG - 1]], inc=True, sem="gsd0")

        HD2 = D // 2

        def emit_pn_chunk(ic):
            prev_cast = tok_wchunk.get(ic - 1)
            for n in range(NBITS):
                for hf in range(2):
                    ui = ic * 8 + n * 2 + hf
                    bg = ui % 4
                    bq = 4 + ui % 4
                    sp = ui % 2
                    tok_pn[(ui, 0)] = sch.op("G",
                        (lambda ic, n, hf, bg: lambda e: e.dma_start(
                            out=pnbg[bg][:],
                            in_=pwT_ap[n, ic * 128:(ic + 1) * 128,
                                       hf * HD2:(hf + 1) * HD2]))(ic, n, hf, bg),
                        waits=[tok_sig.get((ui - 4, 0))],
                        inc=True, sem=f"pn{bg}")
                    tok_pn[(ui, 1)] = sch.op("G",
                        (lambda ic, n, hf, bq: lambda e: e.dma_start(
                            out=pnbg[bq][:],
                            in_=nwT_ap[n, ic * 128:(ic + 1) * 128,
                                       hf * HD2:(hf + 1) * HD2]))(ic, n, hf, bq),
                        waits=[tok_sig.get((ui - 4, 1))],
                        inc=True, sem=f"pn{bq}")
                    tok_sig[(ui, 0)] = sch.op("S",
                        (lambda ui, bg, sp: lambda e: e.activation(
                            out=sgp[sp][:], in_=pnbg[bg][:], func=AF.Sigmoid))(ui, bg, sp),
                        waits=[tok_pn[(ui, 0)], tok_comb.get((ui - 2, 1))], inc=True)
                    ws = hf * HD2
                    tok_sig[(ui, 1)] = sch.op("S",
                        (lambda ui, bq, sp: lambda e: e.activation(
                            out=sgn[sp][:], in_=pnbg[bq][:], func=AF.Sigmoid))(ui, bq, sp),
                        waits=[tok_pn[(ui, 1)], tok_comb.get((ui - 2, 0))], inc=True)
                    tok_comb[(ui, 0)] = sch.op("V",
                        (lambda ui, sp: lambda e: e.tensor_sub(
                            out=sgp[sp][:], in0=sgp[sp][:], in1=sgn[sp][:]))(ui, sp),
                        waits=[tok_sig[(ui, 0)], tok_sig[(ui, 1)]],
                        hard=[tok_comb.get((ui - 2, 1))], inc=True)
                    if n == 0:
                        tok_comb[(ui, 1)] = sch.op("V",
                            (lambda ui, ws, sp: lambda e: e.tensor_scalar_mul(
                                out=Wacc[:, ws:ws + HD2], in0=sgp[sp][:],
                                scalar1=cpos[:, 0:1]))(ui, ws, sp),
                            hard=[tok_comb[(ui, 0)], tcneg, prev_cast], inc=True)
                    else:
                        tok_comb[(ui, 1)] = sch.op("V",
                            (lambda ui, n, ws, sp: lambda e: e.scalar_tensor_tensor(
                                out=Wacc[:, ws:ws + HD2], in0=sgp[sp][:],
                                scalar=cpos[:, n:n + 1],
                                in1=Wacc[:, ws:ws + HD2],
                                op0=ALU.mult, op1=ALU.add))(ui, n, ws, sp),
                            hard=[tok_comb[(ui, 0)], tok_comb.get((ui - 1, 1))], inc=True)
            tok_wchunk[ic] = sch.op("V", (lambda ic: lambda e: e.tensor_copy(
                out=WffnT[:, ic * D:(ic + 1) * D], in_=Wacc[:]))(ic),
                waits=[tok_xn],
                hard=[tok_comb[(ic * 8 + 7, 1)]], inc=True)

        emit_pn_chunk(0)
        emit_pn_chunk(1)
        emit_wg_load(0)
        emit_wg_load(1)
        for step in range(NH):
            emit_gate_head(step)
            if step + 2 < NH:
                emit_wg_load(step + 2)
            if step + 2 < NFC:
                emit_pn_chunk(step + 2)
        tok_wffn = tok_wchunk[NFC - 1]
        tok_last_gate_ev = tok_gst[NH * NTG - 1]

        tok_gev = {}
        tok_gse = {}
        for fc in range(NFC):
            for tg in range(NTG):
                gidx = fc * NTG + tg
                pbank = (pfA + pS)[gidx % 4]

                def mmfn(fc, tg, pbank):
                    def f(e):
                        for ic in range(NFC):
                            r = e.matmul(pbank[:],
                                     WffnT[:, ic * D + fc * 128: ic * D + fc * 128 + 128],
                                     xnT[:, ic * T + tg * 512: ic * T + tg * 512 + 512],
                                     start=(ic == 0), stop=(ic == NFC - 1))
                        return r
                    return f
                tmm_ = sch.op("T", mmfn(fc, tg, pbank),
                    waits=[tok_wffn, tok_gse.get(gidx - 4),
                           tok_last_gate_ev if gidx < 4 else None], inc=True)
                bse = gidx % 2
                tok_gse[gidx] = sch.op("S",
                    (lambda fc, tg, pbank, bse: lambda e: e.activation(
                        out=g_st[bse][:], in_=pbank[:], func=AF.Silu,
                        bias=biasW[:, fc:fc + 1]))(fc, tg, pbank, bse),
                    waits=[tmm_, tbias, tok_gev.get(gidx - 2)], inc=True)
                tok_gev[gidx] = sch.op("Q",
                    (lambda fc, tg, bse: lambda e: e.dma_start(
                        out=g_dram[fc, :, tg * 512:(tg + 1) * 512], in_=g_st[bse][:]))(fc, tg, bse),
                    waits=[tok_gse[gidx]], inc=True, sem=f"gev{bse}")
        tok_ffn_done = [tok_gev[NFC * NTG - 1], tok_gev[NFC * NTG - 2],
                        tok_gse[NFC * NTG - 1]]

    if upto == "BG":
        fin = sch.op("G", lambda e: e.dma_start(
            out=outT_ap[0:128, 0:512], in_=xnT[:, 0:512]),
            waits=tok_ffn_done + [tok_gsd[NH - 1]],
            inc=True, sem="fin")
        sch.op("V", lambda e: e.nop(), waits=[fin])
        return _finish(nc, sch, es, esABC)

    esPN.close()

    with contextlib.ExitStack() as esC:
        def sbC(name, shape, dtype):
            return esC.enter_context(nc.sbuf_tensor(name, list(shape), dtype))

        wq_sb = [wq_sb0, sbC("wq_sb1", (128, NFC * 384), BF16)]
        gate_h = [sbC(f"gate_h{i}", (128, T), BF16) for i in range(2)]
        q_dim = [sbC(f"q_dim{i}", (128, T), BF16) for i in range(2)]
        k_dim = [sbC(f"k_dim{i}", (128, T), BF16) for i in range(2)]
        v_dimb = sbC("v_dim", (128, T), BF16)
        ktok_raw = sbC("ktok_raw", (128, T), BF16)
        k_td = [sbC(f"k_td{i}", (128, T), BF16) for i in range(2)]
        v_tok = [sbC(f"v_tok{i}", (128, T), BF16) for i in range(2)]
        qdec_b = sbC("qdec_b", (128, 512), BF16)
        sT_b = sbC("sT_b", (128, 1024), BF16)
        sq_b = sbC("sq_b", (128, 512), BF16)
        kv_sb = sbC("kv_sb", (128, 128), F32)
        kv_bf = sbC("kv_bf", (128, 128), BF16)
        O_st = [sbC(f"O_st{i}", (128, T), BF16) for i in range(2)]
        qdec_all = sbC("qdec_all", (128, NH * BLK), BF16)
        decT_all = sbC("decT_all", (128, NH * 2 * BLK), BF16)
        kdec_all = sbC("kdec_all", (128, 2 * NH), F32)
        bd_all = sbC("bd_all", (128, NH), F32)
        dtmp = sbC("dtmp", (128, 2 * BLK), F32)

        tok_wq = {}
        tok_gld = {}
        tok_pev = {}
        tok_tr = {}
        tok_ktd = {}
        tok_vev = {}
        tok_qdecb = {}
        tok_og = {}
        tok_kvstt = {}
        tok_kvcast = {}
        tok_ssacc = {}
        tok_sq = {}
        tok_ost = {}
        tok_prod = {}
        tok_tst = {}

        tvjoinC = sch.op("V", lambda e: e.tensor_copy(
            out=join_sb[0:1, 0:1], in_=ones1[0:1, 0:1]),
            waits=tok_ffn_done + [tok_gsd[NH - 1]],
            inc=True)

        tok_decay = None
        tv_ = None
        ts3 = None
        for h in range(NH):
            w0 = [t_cst, tvjoinC, ts3,
                  tok_gse[NFC * NTG - 1] if h == 0 else None]
            tv_ = sch.op("V", (lambda h: lambda e: e.tensor_scalar_mul(
                out=dtmp[:, 0:BLK], in0=rampq_sb, scalar1=slope_sb[:, h:h + 1]))(h),
                waits=w0, hard=[tv_], inc=True)
            ts1 = sch.op("S", (lambda h: lambda e: e.activation(
                out=qdec_all[:, h * BLK:(h + 1) * BLK], in_=dtmp[:, 0:BLK],
                func=AF.Exp, scale=-1.0))(h), waits=[tv_], inc=True)
            tv_ = sch.op("V", (lambda h: lambda e: e.tensor_scalar_mul(
                out=dtmp[:, 0:2 * BLK], in0=rampmn_sb,
                scalar1=slope_sb[:, h:h + 1]))(h), waits=[ts1], inc=True)
            ts2 = sch.op("S", (lambda h: lambda e: e.activation(
                out=decT_all[:, h * 2 * BLK:(h + 1) * 2 * BLK], in_=dtmp[:, 0:2 * BLK],
                func=AF.Exp, scale=-1.0))(h), waits=[tv_], inc=True)
            tv_ = sch.op("V", (lambda h: lambda e: e.tensor_scalar_mul(
                out=dtmp[:, 0:2], in0=rampk_sb, scalar1=slope_sb[:, h:h + 1]))(h),
                waits=[ts2], inc=True)
            ts3 = sch.op("S", (lambda h: lambda e: e.activation(
                out=kdec_all[:, 2 * h:2 * h + 2], in_=dtmp[:, 0:2],
                func=AF.Exp, scale=-1.0))(h), waits=[tv_], inc=True)
            tok_decay = sch.op("S", (lambda h: lambda e: e.activation(
                out=bd_all[:, h:h + 1], in_=slope_sb[:, h:h + 1],
                func=AF.Exp, scale=-256.0))(h), inc=True)

        tok_wq[0] = tok_wq0
        tok_gld[0] = sch.op("Q", lambda e: e.dma_start(
            out=gate_h[0][:], in_=gate_dram[0]),
            waits=tok_ffn_done + [tok_gsd[0], tok_gsd[NH - 1]],
            inc=True, sem="gld0")
        if c_heads > 1:
            tok_gld[1] = sch.op("Q", lambda e: e.dma_start(
                out=gate_h[1][:], in_=gate_dram[1]),
                waits=tok_ffn_done + [tok_gsd[1], tok_gsd[NH - 1]],
                inc=True, sem="gld1")
        tkv0 = {}

        def emit_prod_group(h, sel, tg):
            hb = h % 2
            dests = (q_dim[hb], k_dim[hb], v_dimb)
            gidx = (h * 3 + sel) * NTG + tg
            par = gidx % 2

            def mmfn(hb, sel, tg, par):
                def f(e):
                    r = None
                    for fc in range(NFC):
                        r = e.matmul(pfA[par][:],
                                     wq_sb[hb][:, fc * 384 + sel * 128: fc * 384 + (sel + 1) * 128],
                                     xnT[:, fc * T + tg * 512: fc * T + tg * 512 + 512],
                                     start=(fc == 0), stop=(fc == NFC - 1))
                    return r
                return f
            wmm = [tok_wq[h], tok_pev.get((gidx - 2, "g"))]
            wmm.extend(tok_ffn_done)
            if gidx < 2:
                wmm.append(tok_gse.get(NFC * NTG - 1))
            if sel == 0:
                wmm.append(tok_qdecb.get((h - 2, NB - 1)))
            if sel == 2:
                wmm.append(tok_tr.get(h - 1))
            tmm_ = sch.op("T", mmfn(hb, sel, tg, par), waits=wmm, inc=True)
            tok_pev[(gidx, "g")] = sch.op("S",
                (lambda d, tg, par: lambda e: e.activation(
                    out=d[:, tg * 512:(tg + 1) * 512], in_=pfA[par][:],
                    func=AF.Silu))(dests[sel], tg, par),
                waits=[tmm_], inc=True)
            if tg == NTG - 1:
                tok_pev[(h, sel)] = tok_pev[(gidx, "g")]
                if sel == 2:
                    tok_prod[h] = tok_pev[(h, 2)]

        def emit_transposes(h):
            hb = h % 2

            def trfn(srcbuf, b, base):
                def f(e):
                    r = None
                    for s in range(8):
                        r = e.transpose(pT[b][:, s * 128:(s + 1) * 128],
                                        srcbuf[:, (base + s) * 128:(base + s + 1) * 128],
                                        id_bf[:])
                    return r
                return f
            tev_prev = None
            for half in range(2):
                tk = sch.op("T", trfn(k_dim[hb], half, half * 8),
                    waits=[tok_pev[(h, 1)], t_idbf, tok_ktd.get((h - 1, half)),
                           tok_vev.get((h - 1, half)), tev_prev], inc=True)
                tev_prev = sch.op("V", (lambda half: lambda e: e.tensor_copy(
                    out=ktok_raw[:, half * 1024:(half + 1) * 1024], in_=pT[half][:]))(half),
                    waits=[tk, tvjoinC], inc=True)
            for half in range(2):
                tk = sch.op("T", trfn(v_dimb, half, half * 8),
                    waits=[tok_pev[(h, 2)], tev_prev], inc=True)
                tok_vev[(h, half)] = sch.op("V",
                    (lambda half, hb: lambda e: e.tensor_copy(
                        out=v_tok[hb][:, half * 1024:(half + 1) * 1024], in_=pT[half][:]))(half, hb),
                    waits=[tk], hard=[tok_kvstt.get((h - 2, NB - 1))], inc=True)
                tev_prev = tok_vev[(h, half)]
            tok_tr[h] = tk

            kr = ktok_raw[:].rearrange("p (t e) -> p t e", e=128)
            kdr = k_td[hb][:].rearrange("p (t e) -> p t e", e=128)
            for par_t in range(2):
                vi = kr[:, par_t:NTT:2, :]
                vo = kdr[:, par_t:NTT:2, :]
                tok_ktd[(h, par_t)] = sch.op("V",
                    (lambda vi, vo, h, par_t: lambda e: e.tensor_scalar_mul(
                        out=vo, in0=vi,
                        scalar1=kdec_all[:, 2 * h + par_t:2 * h + par_t + 1]))(vi, vo, h, par_t),
                    waits=[tok_decay],
                    hard=[tev_prev, tok_kvstt.get((h - 2, NB - 1))], inc=True)

        def emit_scan_a(h, j):
            hb = h % 2
            jp = j % 2
            if j == 0:
                tkv0[h] = sch.op("V", lambda e: e.memset(kv_sb[:], 0.0),
                    hard=[tok_kvcast.get((h - 1, NB - 1))], inc=True)
            tok_qdecb[(h, j)] = sch.op("V",
                (lambda h, j, jp, hb: lambda e: e.tensor_mul(
                    out=qdec_b[:, jp * 256:(jp + 1) * 256],
                    in0=q_dim[hb][:, j * 256:(j + 1) * 256],
                    in1=qdec_all[:, h * BLK:(h + 1) * BLK]))(h, j, jp, hb),
                waits=[tok_pev[(h, 0)], tok_decay], inc=True)

            def scfn(hb, j, jp):
                def f(e):
                    r = None
                    for nt in range(2):
                        r = e.matmul(pS[jp][:, nt * 256:(nt + 1) * 256],
                                     k_dim[hb][:, (2 * j + nt) * 128:(2 * j + nt + 1) * 128],
                                     q_dim[hb][:, j * 256:(j + 1) * 256],
                                     start=True, stop=True)
                    return r
                return f
            tsc = sch.op("T", scfn(hb, j, jp),
                waits=[tok_pev[(h, 1)], tok_pev[(h, 0)],
                       tok_kvstt.get((h, j - 2)), tok_ssacc.get((h, j - 2)),
                       tok_kvstt.get((h - 1, NB - 1)) if j < 2 else None,
                       tok_ssacc.get((h - 1, NB - 1)) if j < 2 else None],
                inc=True)
            tst = sch.op("V", (lambda h, jp: lambda e: e.tensor_tensor(
                out=sT_b[:, jp * 512:(jp + 1) * 512], in0=pS[jp][:],
                in1=decT_all[:, h * 512:(h + 1) * 512], op=ALU.mult))(h, jp),
                waits=[tsc], inc=True)
            tok_tst[(h, j)] = tst

        def emit_scan_b(h, j):
            hb = h % 2
            jp = j % 2
            tst = tok_tst[(h, j)]

            def ofn(h, j, jp, hb):
                def f(e):
                    r = None
                    first = True
                    if j > 0:
                        r = e.matmul(pO[jp][:, 0:256], kv_bf[:],
                                     qdec_b[:, jp * 256:(jp + 1) * 256],
                                     start=True, stop=False)
                        first = False
                    for nt in range(2):
                        r = e.matmul(pO[jp][:, 0:256],
                                     v_tok[hb][:, (2 * j + nt) * 128:(2 * j + nt + 1) * 128],
                                     sT_b[:, jp * 512 + nt * 256: jp * 512 + (nt + 1) * 256],
                                     start=first, stop=(nt == 1))
                        first = False
                    return r
                return f
            tog = sch.op("T", ofn(h, j, jp, hb),
                waits=[tst, tok_qdecb[(h, j)],
                       tok_kvcast.get((h, j - 1)),
                       tok_vev.get((h, 1)),
                       tok_sq.get((h, j - 2)), tok_og.get((h, j - 2)),
                       tok_sq.get((h - 1, NB - 1)) if j < 2 else None,
                       tok_og.get((h - 1, NB - 1)) if j < 2 else None],
                inc=True)

            def kvfn(h, j, jp, hb):
                def f(e):
                    r = None
                    for nt in range(2):
                        r = e.matmul(pS[jp][:, 0:128],
                                     k_td[hb][:, (2 * j + nt) * 128:(2 * j + nt + 1) * 128],
                                     v_tok[hb][:, (2 * j + nt) * 128:(2 * j + nt + 1) * 128],
                                     start=(nt == 0), stop=(nt == 1))
                    return r
                return f
            tkv = sch.op("T", kvfn(h, j, jp, hb),
                waits=[tok_ktd.get((h, 1)), tok_ktd.get((h, 0))], inc=True)
            tok_sq[(h, j)] = sch.op("S", (lambda jp: lambda e: e.activation(
                out=sq_b[:, jp * 256:(jp + 1) * 256], in_=pO[jp][:, 0:256],
                func=AF.Square))(jp), waits=[tog], inc=True)
            tss = sch.op("T", (lambda jp: lambda e: e.matmul(
                pS[jp][0:1, 128:384], ones_bf[:],
                sq_b[:, jp * 256:(jp + 1) * 256], start=True, stop=True))(jp),
                waits=[tok_sq[(h, j)], tkv], inc=True)
            tok_og[(h, j)] = sch.op("V",
                (lambda h, j, jp, hb: lambda e: e.scalar_tensor_tensor(
                    out=O_st[hb][:, j * 256:(j + 1) * 256], in0=pO[jp][:, 0:256],
                    scalar=normw_sb[:, h:h + 1],
                    in1=gate_h[hb][:, j * 256:(j + 1) * 256],
                    op0=ALU.mult, op1=ALU.mult))(h, j, jp, hb),
                waits=[tog, t_cst, tok_ost.get(h - 2), tok_sq[(h, j)],
                       tok_gld[h]], inc=True)
            tok_kvstt[(h, j)] = sch.op("V",
                (lambda h, jp: lambda e: e.scalar_tensor_tensor(
                    out=kv_sb[:], in0=kv_sb[:], scalar=bd_all[:, h:h + 1],
                    in1=pS[jp][:, 0:128], op0=ALU.mult, op1=ALU.add))(h, jp),
                waits=[tss], hard=[tkv0[h]], inc=True)
            tok_kvcast[(h, j)] = sch.op("V", lambda e: e.tensor_copy(
                out=kv_bf[:], in_=kv_sb[:]),
                hard=[tok_kvstt[(h, j)]], inc=True)
            tok_ssacc[(h, j)] = sch.op("V",
                (lambda j, jp: lambda e: e.tensor_tensor(
                    out=sumsq[0:1, j * 256:(j + 1) * 256],
                    in0=sumsq[0:1, j * 256:(j + 1) * 256],
                    in1=pS[jp][0:1, 128:384], op=ALU.add))(j, jp),
                waits=[tss], inc=True)

        for it in range(c_heads + 1):
            ph = it if it < c_heads else None
            sh = it - 1
            if ph is not None and ph + 1 < c_heads:
                tok_wq[ph + 1] = sch.op("G", (lambda ph: lambda e: e.dma_start(
                    out=wq_sb[(ph + 1) % 2][:].rearrange("p (c j) -> p c j", j=384),
                    in_=wqkvT_ap[ph + 1].rearrange("(c p) j -> p c j", p=128)))(ph),
                    waits=[tok_prod.get(ph - 1),
                           tok_ffn_done[0] if ph == 0 else None],
                    inc=True, sem=f"wq{(ph + 1) % 2}")
            gitems = ([("p", s, t) for s in range(3) for t in range(NTG)]
                      if ph is not None else [])
            bitems = ([x for j in range(NB) for x in (("a", j), ("b", j))]
                      if sh >= 0 else [])
            merged = []
            gi = bi = 0
            while gi < len(gitems) or bi < len(bitems):
                if bi >= len(bitems) or (gi < len(gitems) and
                                         gi * (len(bitems) + 1) <= bi * (len(gitems) + 1)):
                    merged.append(gitems[gi]); gi += 1
                else:
                    merged.append(bitems[bi]); bi += 1
            for item in merged:
                if item[0] == "p":
                    emit_prod_group(ph, item[1], item[2])
                elif item[0] == "a":
                    emit_scan_a(sh, item[1])
                else:
                    emit_scan_b(sh, item[1])
            if sh >= 0:
                hb = sh % 2
                tok_ost[sh] = sch.op("Q", (lambda sh, hb: lambda e: e.dma_start(
                    out=o_dram[sh], in_=O_st[hb][:]))(sh, hb),
                    waits=[tok_og[(sh, NB - 1)]], inc=True, sem=f"ost{hb}")
                if sh + 2 < c_heads:
                    tok_gld[sh + 2] = sch.op("Q", (lambda sh: lambda e: e.dma_start(
                        out=gate_h[sh % 2][:], in_=gate_dram[sh + 2]))(sh),
                        waits=[tok_gsd[sh + 2], tok_og[(sh, NB - 1)]],
                        inc=True, sem=f"gld{sh % 2}")
            if ph is not None:
                emit_transposes(ph)

        tok_c_done = [tok_ost.get(c_heads - 1), tok_ost.get(c_heads - 2),
                      tok_decay]
        tok_c_v = tok_ssacc.get((NH - 1, NB - 1))
        tok_c_s = tok_sq.get((NH - 1, NB - 1))

    if upto == "C":
        fin = sch.op("G", lambda e: e.dma_start(
            out=outT_ap[0:128, 0:512], in_=xnT[:, 0:512]),
            waits=tok_c_done + [tok_c_v], inc=True, sem="fin")
        sch.op("V", lambda e: e.nop(), waits=[fin])
        return _finish(nc, sch, es, esABC)

    esABC.close()

    with contextlib.ExitStack() as esD:
        def sbD(name, shape, dtype):
            return esD.enter_context(nc.sbuf_tensor(name, list(shape), dtype))

        wout_sb = sbD("wout_sb", (128, NFC * D), BF16)
        O_sb = sbD("O_sb", (128, NH * T), BF16)
        rsq_rep = sbD("rsq_rep", (128, T), F32)
        g_fc = [sbD(f"g_fc{i}", (128, T), BF16) for i in range(2)]
        xT_fc = [sbD(f"xT_fc{i}", (128, T), F32) for i in range(2)]
        s1 = [sbD(f"s1_{i}", (128, 512), F32) for i in range(2)]
        outb = [sbD(f"outb{i}", (128, 512), F32) for i in range(2)]
        rtmp = sbD("rtmp", (1, T), F32)
        rvec = sbD("rvec", (1, T), F32)
        rtmp2 = sumsq

        tjoin = sch.op("V", lambda e: e.tensor_copy(
            out=join_sb[0:1, 1:2], in_=ones1[0:1, 0:1]), waits=tok_c_done, inc=True)
        tsjoin = sch.op("S", lambda e: e.copy(
            out=join_sb[0:1, 2:3], in_=ones1[0:1, 0:1]), waits=[tok_c_s], inc=True)

        tok_wout = {}
        for fc in range(NFC):
            tok_wout[fc] = sch.op("G", (lambda fc: lambda e: e.dma_start(
                out=wout_sb[:, fc * NH * 128:(fc + 1) * NH * 128].rearrange(
                    "p (c f) -> p c f", f=128),
                in_=woutT_ap[:, fc * 128:(fc + 1) * 128].rearrange(
                    "(c p) f -> p c f", p=128)))(fc),
                waits=[tok_prod[NH - 1], tjoin, tsjoin] if fc == 0 else [],
                inc=True, sem=f"wou{fc}")
        tok_oload = sch.op("Q", lambda e: e.dma_start(
            out=O_sb[:].rearrange("p (c t) -> p c t", t=T),
            in_=o_dram.rearrange("c p t -> p c t")),
            waits=tok_c_done + [tjoin, tsjoin], inc=True, sem="olo")

        tsr2 = sch.op("S", lambda e: e.activation(
            out=rtmp[:], in_=sumsq[:], func=AF.Sqrt, scale=1.0 / D,
            bias=eps_sb[0:1, 1:2]), waits=[tok_c_v, tsjoin], inc=True)
        tn = sch.op("V", lambda e: e.reciprocal(out=rvec[:], in_=rtmp[:]),
                    waits=[tsr2], hard=[tjoin], inc=True)
        tn = sch.op("V", lambda e: e.tensor_scalar(
            out=rtmp2[:], in0=sumsq[:], scalar1=1.0 / D, scalar2=1e-6,
            op0=ALU.mult, op1=ALU.add), waits=[tsr2], hard=[tn], inc=True)
        tn = sch.op("V", lambda e: e.tensor_mul(
            out=rtmp[:], in0=rvec[:], in1=rvec[:]), hard=[tn], inc=True)
        tn = sch.op("V", lambda e: e.tensor_mul(
            out=rtmp[:], in0=rtmp[:], in1=rtmp2[:]), hard=[tn], inc=True)
        tn = sch.op("V", lambda e: e.tensor_scalar(
            out=rtmp[:], in0=rtmp[:], scalar1=-0.5, scalar2=1.5,
            op0=ALU.mult, op1=ALU.add), hard=[tn], inc=True)
        trv = sch.op("V", lambda e: e.tensor_mul(
            out=rvec[:], in0=rvec[:], in1=rtmp[:]), hard=[tn], inc=True)
        trep2 = {}
        for g in range(NTG):
            par = g % 2
            ta = sch.op("T", (lambda g, par: lambda e: e.matmul(
                pO[par][:, 0:512], ones1[:], rvec[0:1, g * 512:(g + 1) * 512],
                start=True, stop=True))(g, par),
                waits=[trv, tok_c_v, trep2.get(g - 2)], inc=True)
            trep2[g] = sch.op("V", (lambda g, par: lambda e: e.tensor_copy(
                out=rsq_rep[:, g * 512:(g + 1) * 512], in_=pO[par][:, 0:512]))(g, par),
                waits=[ta], inc=True)
        tok_rsq = trep2[NTG - 1]

        tok_gl = {}
        tok_xl = {}
        tok_out = {}
        tok_ev3 = {}
        for fc in range(NFC):
            fb = fc % 2
            tok_gl[fc] = sch.op("Q", (lambda fc, fb: lambda e: e.dma_start(
                out=g_fc[fb][:], in_=g_dram[fc]))(fc, fb),
                waits=[tok_ev3.get((fc - 2, NTG - 1)), tjoin, tsjoin] + tok_ffn_done,
                inc=True, sem=f"gl{fb}")
            tok_xl[fc] = sch.op("Q", (lambda fc, fb: lambda e: e.dma_start(
                out=xT_fc[fb][:], in_=xT_ap[fc * 128:(fc + 1) * 128, :]))(fc, fb),
                waits=[tok_ev3.get((fc - 2, NTG - 1)), tjoin, tsjoin],
                inc=True, sem=f"xl{fb}")
            for tg in range(NTG):
                gidx = fc * NTG + tg
                pbank = (pfA + pS)[gidx % 4]
                par = gidx % 2

                def mmfn(fc, tg, pbank):
                    def f(e):
                        base = fc * NH * 128
                        for oc in range(NH):
                            r = e.matmul(pbank[:],
                                     wout_sb[:, base + oc * 128: base + (oc + 1) * 128],
                                     O_sb[:, oc * T + tg * 512: oc * T + tg * 512 + 512],
                                     start=(oc == 0), stop=(oc == NH - 1))
                        return r
                    return f
                tmm_ = sch.op("T", mmfn(fc, tg, pbank),
                    waits=[tok_wout[fc], tok_oload, tok_ev3.get((gidx - 4, "s"))], inc=True)
                tv1 = sch.op("V", (lambda par, tg, pbank: lambda e: e.tensor_tensor(
                    out=s1[par][:], in0=pbank[:],
                    in1=rsq_rep[:, tg * 512:(tg + 1) * 512], op=ALU.mult))(par, tg, pbank),
                    waits=[tmm_],
                    hard=[tok_rsq, tok_ev3.get((gidx - 2, "v3"))], inc=True)
                tok_ev3[(gidx, "s")] = tv1
                tv2 = sch.op("V", (lambda par, fb, tg: lambda e: e.tensor_tensor(
                    out=s1[par][:], in0=s1[par][:],
                    in1=g_fc[fb][:, tg * 512:(tg + 1) * 512], op=ALU.mult))(par, fb, tg),
                    waits=[tok_gl[fc]], hard=[tv1], inc=True)
                tv3 = sch.op("V", (lambda par, fb, tg: lambda e: e.tensor_tensor(
                    out=outb[par][:], in0=s1[par][:],
                    in1=xT_fc[fb][:, tg * 512:(tg + 1) * 512], op=ALU.add))(par, fb, tg),
                    waits=[tok_xl[fc], tok_out.get(gidx - 2)], hard=[tv2], inc=True)
                tok_ev3[(fc, tg)] = tv3
                tok_ev3[(gidx, "v3")] = tv3
                tok_out[gidx] = sch.op("Q",
                    (lambda fc, tg, par: lambda e: e.dma_start(
                        out=outT_ap[fc * 128:(fc + 1) * 128, tg * 512:(tg + 1) * 512],
                        in_=outb[par][:]))(fc, tg, par),
                    waits=[tv3], inc=True, sem=f"out{par}")

        finals = [tok_out[NFC * NTG - 1], tok_out[NFC * NTG - 2]]
        sch.op("Q", lambda e: e.nop(), waits=finals)
        sch.op("V", lambda e: e.nop(), waits=finals)

    return _finish(nc, sch, es, esABC)


def _finish(nc, sch, es, esABC):
    semstack = contextlib.ExitStack()
    sems = {}
    for name in sorted(sch.sem_names):
        sems[name] = semstack.enter_context(nc.semaphore(f"sem_{name}"))
    with nc.Block() as block:
        @block.tensor
        def _(eng):
            sch.replay("T", eng, sems)

        @block.vector
        def _(eng):
            sch.replay("V", eng, sems)

        @block.scalar
        def _(eng):
            sch.replay("S", eng, sems)

        @block.sync
        def _(eng):
            sch.replay("Q", eng, sems)

        @block.gpsimd
        def _(eng):
            sch.replay("G", eng, sems)

    semstack.close()
    try:
        esABC.close()
    except Exception:
        pass
    es.close()
    return nc


def host_inputs(inputs):
    x = np.asarray(inputs["L0_attn_hidden_states"], dtype=np.float32)
    slope = np.asarray(inputs["L0_attn_slope_rate"], dtype=np.float32).reshape(NH)
    w_qkv = np.asarray(inputs["L0_attn_w_qkv"], dtype=np.float32)
    w_gate = np.asarray(inputs["L0_attn_w_gate"], dtype=np.float32)
    w_out = np.asarray(inputs["L0_attn_w_out"], dtype=np.float32)
    norm_w = np.asarray(inputs["L0_attn_norm_w"], dtype=np.float32)
    pweight = np.asarray(inputs["L0_ffn_pweight"], dtype=np.float32)
    nweight = np.asarray(inputs["L0_ffn_nweight"], dtype=np.float32)
    exps = np.asarray(inputs["L0_ffn_exps"], dtype=np.float32)
    bexps = np.asarray(inputs["L0_ffn_bexps"], dtype=np.float32)
    mask_w = np.asarray(inputs["L0_ffn_mask_weight"], dtype=np.float32)
    scale = np.asarray(inputs["L0_ffn_scale"], dtype=np.float32)
    pbias = np.asarray(inputs["L0_ffn_pbias"], dtype=np.float32)
    nbias = np.asarray(inputs["L0_ffn_nbias"], dtype=np.float32)
    bscale = np.asarray(inputs["L0_ffn_biasscale"], dtype=np.float32)

    C = np.ascontiguousarray
    consts = np.zeros((128, CW), dtype=np.float32)

    def setc(name, val):
        a, b = _CO[name]
        consts[:, a:b] = val

    p = np.arange(128, dtype=np.float32)
    m = np.arange(BLK, dtype=np.float32).reshape(1, BLK)
    setc("id", np.eye(128, dtype=np.float32))
    setc("rampq", np.arange(1, BLK + 1, dtype=np.float32).reshape(1, BLK))
    grids = []
    for tile in range(2):
        n = (p + 128 * tile).reshape(128, 1)
        g = m - n
        grids.append(np.where(g >= 0, g, np.float32(1e30)).astype(np.float32))
    setc("rampmn", np.concatenate(grids, axis=1))
    setc("rampk", np.stack([255.0 - p, 127.0 - p], axis=1))
    setc("normw", norm_w.reshape(NH, 128).T)
    setc("slope", slope.reshape(1, NH))
    setc("pb2", pbias.reshape(NH, 128, NBITS).transpose(1, 0, 2).reshape(128, NH * NBITS))
    setc("nb2", nbias.reshape(NH, 128, NBITS).transpose(1, 0, 2).reshape(128, NH * NBITS))
    setc("exps", exps.reshape(1, NBITS))
    setc("mw", mask_w.reshape(1, NBITS))
    setc("bexps", bexps.reshape(1, NBITS))
    setc("scale", scale.reshape(1, 1))
    setc("bscale", bscale.reshape(1, 1))

    shared = {
        "wqkvT": C(w_qkv.reshape(NH, 384, D).transpose(0, 2, 1)),
        "wgateT": C(w_gate.reshape(NH, HD, D).transpose(0, 2, 1)),
        "woutT": C(w_out.T),
        "pwT": C(pweight.transpose(2, 1, 0)),
        "nwT": C(nweight.transpose(2, 1, 0)),
        "consts": consts,
    }
    in_maps = []
    for c in range(NCORES):
        im = dict(shared)
        im["x"] = C(x[c])
        im["xT"] = C(x[c].T)
        in_maps.append(im)
    return in_maps


def kernel(**inputs):
    in_maps = host_inputs(inputs)
    nc = build_program()
    res = run_bass_kernel_spmd(nc, in_maps, core_ids=list(range(NCORES)))
    out = np.stack([np.ascontiguousarray(r["outT"].T) for r in res.results])
    return out


# version 25
# speedup vs baseline: 1.1730x; 1.0678x over previous
import contextlib
import numpy as np
import concourse.bass as bass
import concourse.mybir as mybir
from concourse.bass_utils import run_bass_kernel_spmd

F32 = mybir.dt.float32
BF16 = mybir.dt.bfloat16
AF = mybir.ActivationFunctionType
ALU = mybir.AluOpType
AX = mybir.AxisListType

NCORES = 8
T = 2048
D = 2048
NH = 16
HD = 128
BLK = 256
NB = T // BLK
NTT = T // 128
NFC = D // 128
NTG = T // 512
NBITS = 4

ENGS = ("T", "V", "S", "Q", "G")

_CO = {}
_off = 0
for _name, _w in [("id", 128), ("rampq", BLK), ("rampmn", 2 * BLK), ("rampk", 2),
                  ("normw", NH), ("slope", NH), ("pb2", NH * NBITS),
                  ("nb2", NH * NBITS), ("exps", NBITS), ("mw", NBITS),
                  ("bexps", NBITS), ("scale", 1), ("bscale", 1)]:
    _CO[_name] = (_off, _off + _w)
    _off += _w
CW = _off


class Sched:

    def __init__(self):
        self.ops = {e: [] for e in ENGS}
        self.cnt = {}
        self.sem_names = set(["T", "V", "S"])

    def op(self, eng, fn, waits=(), inc=False, sem=None, hard=()):
        if eng in ("Q", "G"):
            amt = 16
            assert sem is not None or not inc
        else:
            amt = 1
            sem = eng
        ws = []
        for w in waits:
            if w is None:
                continue
            pse, val = w
            if eng not in ("Q", "G") and pse == eng:
                continue
            ws.append((pse, val))
        for w in hard:
            if w is None:
                continue
            ws.append(tuple(w))
        self.ops[eng].append((ws, fn, inc, sem))
        if inc:
            self.sem_names.add(sem)
            self.cnt[sem] = self.cnt.get(sem, 0) + amt
            return (sem, self.cnt[sem])
        return None

    def replay(self, eng_name, eng_obj, sems):
        maxw = {}
        for ws, fn, inc, sem in self.ops[eng_name]:
            for pse, val in ws:
                if maxw.get(pse, 0) >= val:
                    continue
                maxw[pse] = val
                eng_obj.wait_ge(sems[pse], val)
            inst = fn(eng_obj)
            if inc:
                amt = 16 if eng_name in ("Q", "G") else 1
                if inst is None:
                    inst = eng_obj.nop()
                inst.then_inc(sems[sem], amt)


def build_program(upto="D", c_heads=NH):
    nc = bass.Bass("TRN2", target_bir_lowering=False)

    def din(name, shape):
        return nc.dram_tensor(name, list(shape), F32, kind="ExternalInput").ap()

    x_ap = din("x", (T, D))
    xT_ap = din("xT", (D, T))
    wqkvT_ap = din("wqkvT", (NH, D, 384))
    wgateT_ap = din("wgateT", (NH, D, HD))
    woutT_ap = din("woutT", (D, D))
    pwT_ap = din("pwT", (NBITS, D, D))
    nwT_ap = din("nwT", (NBITS, D, D))
    consts_ap = din("consts", (128, CW))

    outT_ap = nc.dram_tensor("outT", [D, T], F32, kind="ExternalOutput").ap()

    g_dram = nc.dram_tensor("g_scr", [NFC, 128, T], BF16, kind="Internal").ap()
    gate_dram = nc.dram_tensor("gate_scr", [NH, 128, T], BF16, kind="Internal").ap()
    o_dram = nc.dram_tensor("o_scr", [NH, 128, T], BF16, kind="Internal").ap()

    sch = Sched()
    es = contextlib.ExitStack()

    def sb(name, shape, dtype):
        return es.enter_context(nc.sbuf_tensor(name, list(shape), dtype))

    def ps(name, shape, dtype):
        return es.enter_context(nc.psum_tensor(name, list(shape), dtype))

    csb = sb("csb", (128, CW), F32)

    def cc(name):
        a, b = _CO[name]
        return csb[:, a:b]

    id_f32 = cc("id")
    rampq_sb = cc("rampq")
    rampmn_sb = cc("rampmn")
    rampk_sb = cc("rampk")
    normw_sb = cc("normw")
    slope_sb = cc("slope")
    pb2_sb = cc("pb2")
    nb2_sb = cc("nb2")

    id_bf = sb("id_bf", (128, 128), BF16)
    ones_bf = sb("ones_bf", (128, 1), BF16)
    ones1 = sb("ones1", (1, 128), F32)
    biasW = sb("biasW", (128, NFC), F32)
    sumsq = sb("sumsq", (1, T), F32)
    eps_sb = sb("eps_sb", (128, 2), F32)
    cpos = sb("cpos", (128, NBITS), F32)
    cneg = sb("cneg", (128, NBITS), F32)
    sgm = sb("sgm", (128, NBITS), F32)
    join_sb = sb("join_sb", (1, 8), F32)

    pfA = [ps(f"pfA{i}", (128, 512), F32) for i in range(2)]
    pS = [ps(f"pS{i}", (128, 512), F32) for i in range(2)]
    pO = [ps(f"pO{i}", (128, 512), F32) for i in range(2)]
    pT = [ps(f"pT{i}", (128, 1024), BF16) for i in range(2)]

    t_cst = sch.op("Q", lambda e: e.dma_start(out=csb[:], in_=consts_ap),
                   inc=True, sem="cst")
    tok_wq0 = sch.op("G", lambda e: e.dma_start(
        out=wq_sb0[:].rearrange("p (c j) -> p c j", j=384),
        in_=wqkvT_ap[0].rearrange("(c p) j -> p c j", p=128)),
        inc=True, sem="wq0")
    t_idbf = sch.op("V", lambda e: e.tensor_copy(out=id_bf[:], in_=id_f32),
                    waits=[t_cst], inc=True)
    sch.op("V", lambda e: e.memset(ones_bf[:], 1.0))
    sch.op("V", lambda e: e.memset(ones1[:], 1.0))
    sch.op("V", lambda e: e.memset(eps_sb[:, 0:1], 1e-5))
    sch.op("V", lambda e: e.memset(eps_sb[:, 1:2], 1e-6))
    sch.op("V", lambda e: e.memset(sumsq[:], 0.0))

    esABC = contextlib.ExitStack()
    xnT = esABC.enter_context(nc.sbuf_tensor("xnT", [128, NFC * T], BF16))
    wq_sb0 = esABC.enter_context(nc.sbuf_tensor("wq_sb0", [128, NFC * 384], BF16))
    esPN = contextlib.ExitStack()
    pnbg = [esPN.enter_context(nc.sbuf_tensor(f"pnbg{i}", [128, D // 2], BF16))
            for i in range(8)]
    sgp = [esPN.enter_context(nc.sbuf_tensor(f"sgp{i}", [128, D // 2], BF16))
           for i in range(2)]
    sgn = [esPN.enter_context(nc.sbuf_tensor(f"sgn{i}", [128, D // 2], BF16))
           for i in range(2)]
    Wacc = esPN.enter_context(nc.sbuf_tensor("Wacc", [128, D], F32))
    with contextlib.ExitStack() as esA:
        def sbA(name, shape, dtype):
            return esA.enter_context(nc.sbuf_tensor(name, list(shape), dtype))

        xtile = [sbA(f"xtile{i}", (128, D), F32) for i in range(3)]
        xntok = [sbA(f"xntok{i}", (128, D), BF16) for i in range(2)]
        junkA = sbA("junkA", (128, D), BF16)
        mneg_c = sbA("mneg_c", (128, NTT), F32)
        vss_c = sbA("vss_c", (128, NTT), F32)
        y_c = sbA("y_c", (128, NTT), F32)
        s_c = sbA("s_c", (128, NTT), F32)
        r_c = sbA("r_c", (128, NTT), F32)
        t_c = sbA("t_c", (128, NTT), F32)

        tload = [None, None, None]
        tsq = {}
        txn = {}
        tev = {}

        def a_stage1(tt):
            b = tt % 3
            tload[b] = sch.op("Q", (lambda tt, b: lambda e: e.dma_start(
                out=xtile[b][:], in_=x_ap[tt * 128:(tt + 1) * 128, :]))(tt, b),
                waits=[txn.get(tt - 3)], inc=True, sem=f"xa{b}")
            tred = sch.op("V", (lambda tt, b: lambda e: e.tensor_reduce(
                out=mneg_c[:, tt:tt + 1], in_=xtile[b][:], axis=AX.X, op=ALU.add))(tt, b),
                waits=[tload[b]], inc=True)
            tr2 = sch.op("V", (lambda tt: lambda e: e.tensor_scalar_mul(
                out=mneg_c[:, tt:tt + 1], in0=mneg_c[:, tt:tt + 1],
                scalar1=-1.0 / D))(tt), hard=[tred], inc=True)
            tsq[tt] = sch.op("S", (lambda tt, b: lambda e: e.activation(
                out=junkA[:], in_=xtile[b][:], func=AF.Square,
                bias=mneg_c[:, tt:tt + 1], accum_out=vss_c[:, tt:tt + 1]))(tt, b),
                waits=[tr2], hard=[tsq.get(tt - 1)], inc=True)

        def a_stage2(tt):
            b = tt % 3
            b2 = tt % 2
            tyv = sch.op("V", (lambda tt: lambda e: e.tensor_scalar(
                out=y_c[:, tt:tt + 1], in0=vss_c[:, tt:tt + 1],
                scalar1=1.0 / D, scalar2=1e-5,
                op0=ALU.mult, op1=ALU.add))(tt), waits=[tsq[tt]], inc=True)
            tsr = sch.op("S", (lambda tt: lambda e: e.activation(
                out=s_c[:, tt:tt + 1], in_=y_c[:, tt:tt + 1],
                func=AF.Sqrt))(tt), waits=[tyv], inc=True)
            tn = sch.op("V", (lambda tt: lambda e: e.reciprocal(
                out=r_c[:, tt:tt + 1], in_=s_c[:, tt:tt + 1]))(tt),
                waits=[tsr], inc=True)
            tn = sch.op("V", (lambda tt: lambda e: e.tensor_mul(
                out=t_c[:, tt:tt + 1], in0=r_c[:, tt:tt + 1],
                in1=r_c[:, tt:tt + 1]))(tt), hard=[tn], inc=True)
            tn = sch.op("V", (lambda tt: lambda e: e.tensor_mul(
                out=t_c[:, tt:tt + 1], in0=t_c[:, tt:tt + 1],
                in1=y_c[:, tt:tt + 1]))(tt), hard=[tn], inc=True)
            tn = sch.op("V", (lambda tt: lambda e: e.tensor_scalar(
                out=t_c[:, tt:tt + 1], in0=t_c[:, tt:tt + 1],
                scalar1=-0.5, scalar2=1.5,
                op0=ALU.mult, op1=ALU.add))(tt), hard=[tn], inc=True)
            tn = sch.op("V", (lambda tt: lambda e: e.tensor_mul(
                out=r_c[:, tt:tt + 1], in0=r_c[:, tt:tt + 1],
                in1=t_c[:, tt:tt + 1]))(tt), hard=[tn], inc=True)
            txn[tt] = sch.op("V", (lambda tt, b, b2: lambda e: e.tensor_scalar(
                out=xntok[b2][:], in0=xtile[b][:],
                scalar1=mneg_c[:, tt:tt + 1], scalar2=r_c[:, tt:tt + 1],
                op0=ALU.add, op1=ALU.mult))(tt, b, b2),
                hard=[tn, tev.get((tt - 2, 1))], inc=True)
            for half in range(2):
                def trfnA(b2, half, tt):
                    def f(e):
                        r = None
                        for s in range(8):
                            fc = half * 8 + s
                            r = e.transpose(pT[half][:, s * 128:(s + 1) * 128],
                                            xntok[b2][:, fc * 128:(fc + 1) * 128],
                                            id_bf[:])
                        return r
                    return f
                tk = sch.op("T", trfnA(b2, half, tt),
                    waits=[txn[tt], t_idbf, tev.get((tt - 1, half))], inc=True)
                xv = xnT[:].rearrange("p (c t) -> p c t", t=T)
                tev[(tt, half)] = sch.op("V",
                    (lambda tt, half, xv: lambda e: e.tensor_copy(
                        out=xv[:, half * 8:(half + 1) * 8, tt * 128:(tt + 1) * 128],
                        in_=pT[half][:].rearrange("p (s t) -> p s t", t=128)))(tt, half, xv),
                    waits=[tk], inc=True)

        for tt in range(NTT + 1):
            if tt < NTT:
                a_stage1(tt)
            if tt >= 1:
                a_stage2(tt - 1)
        tok_xn = sch.op("V", lambda e: e.tensor_copy(
            out=join_sb[0:1, 3:4], in_=ones1[0:1, 0:1]),
            hard=[tev[(NTT - 1, 0)], tev[(NTT - 1, 1)]], inc=True)

    tok_gsd = {}
    with contextlib.ExitStack() as esB:
        def sbB(name, shape, dtype):
            return esB.enter_context(nc.sbuf_tensor(name, list(shape), dtype))

        WffnT = sbB("WffnT", (128, NFC * D), BF16)
        wg_sb = [sbB(f"wg_sb{i}", (128, NFC * HD), BF16) for i in range(2)]
        wg_f32 = sbB("wg_f32", (128, NFC * HD), F32)
        gate_st = [sbB("gate_stA", (128, T), BF16)]
        g_st = [sbB(f"g_st{i}", (128, 512), BF16) for i in range(2)]
        btmp = sbB("btmp", (128, NH * NBITS), F32)
        bacc = sbB("bacc", (128, NFC), F32)
        br = sbB("br", (128, NFC), F32)

        tcs = sch.op("S", lambda e: e.activation(
            out=sgm[:], in_=cc("mw"), func=AF.Sigmoid), waits=[t_cst], inc=True)
        tc1 = sch.op("V", lambda e: e.tensor_mul(
            out=cpos[:], in0=sgm[:], in1=cc("exps")), waits=[tcs], inc=True)
        tc2 = sch.op("V", lambda e: e.tensor_scalar_mul(
            out=cpos[:], in0=cpos[:], scalar1=cc("scale")), hard=[tc1], inc=True)
        tcneg = sch.op("V", lambda e: e.tensor_scalar_mul(
            out=cneg[:], in0=cpos[:], scalar1=-1.0), hard=[tc2], inc=True)

        tb0 = sch.op("V", lambda e: e.tensor_sub(
            out=btmp[:], in0=pb2_sb, in1=nb2_sb),
            waits=[t_cst, tok_xn], inc=True)
        btr = btmp[:].rearrange("p (g n) -> p n g", n=NBITS)
        tb1 = tb0
        for n in range(NBITS):
            bx = cc("bexps")
            if n == 0:
                tb1 = sch.op("V", (lambda n, bx: lambda e: e.tensor_scalar_mul(
                    out=bacc[:], in0=btr[:, n, :], scalar1=bx[:, n:n + 1]))(n, bx),
                    hard=[tb1], inc=True)
            else:
                tb1 = sch.op("V", (lambda n, bx: lambda e: e.scalar_tensor_tensor(
                    out=bacc[:], in0=btr[:, n, :], scalar=bx[:, n:n + 1],
                    in1=bacc[:], op0=ALU.mult, op1=ALU.add))(n, bx),
                    hard=[tb1], inc=True)
        tb1 = sch.op("V", lambda e: e.tensor_scalar(
            out=bacc[:], in0=bacc[:], scalar1=1.0, scalar2=-1.0,
            op0=ALU.min, op1=ALU.max), hard=[tb1], inc=True)
        tb2 = sch.op("S", lambda e: e.activation(
            out=br[:], in_=bacc[:], func=AF.Abs, scale=15.0), waits=[tb1], inc=True)
        tb3 = sch.op("V", lambda e: e.tensor_single_scalar(
            out=btmp[:, 0:NFC], in_=br[:], scalar=0.5, op=ALU.is_ge),
            waits=[tb2], inc=True)
        for k in range(1, 15):
            tb3 = sch.op("V", (lambda k: lambda e: e.scalar_tensor_tensor(
                out=btmp[:, 0:NFC], in0=br[:], scalar=k + 0.5, in1=btmp[:, 0:NFC],
                op0=ALU.is_ge, op1=ALU.add))(k), hard=[tb3], inc=True)
        tb4 = sch.op("S", lambda e: e.activation(
            out=br[:], in_=bacc[:], func=AF.Sign), waits=[tb3], hard=[tb2], inc=True)
        tbm = sch.op("V", lambda e: e.tensor_mul(
            out=btmp[:, 0:NFC], in0=btmp[:, 0:NFC], in1=br[:]),
            waits=[tb4], hard=[tb3], inc=True)
        tbias = sch.op("V", lambda e: e.tensor_scalar(
            out=biasW[:], in0=btmp[:, 0:NFC], scalar1=1.0 / 15.0,
            scalar2=cc("bscale"), op0=ALU.mult, op1=ALU.mult),
            hard=[tbm], inc=True)

        tok_wg = {}
        tok_gmm = {}
        tok_gst = {}
        tok_pn = {}
        tok_sig = {}
        tok_comb = {}
        tok_wchunk = {}

        tok_wgl = {}

        def emit_wg_load(h):
            b = h % 2
            if h % 2 == 0:
                tok_wg[h] = sch.op("G", (lambda h, b: lambda e: e.dma_start(
                    out=wg_sb[b][:].rearrange("p (c j) -> p c j", j=HD),
                    in_=wgateT_ap[h].rearrange("(c p) j -> p c j", p=128)))(h, b),
                    waits=[tok_gmm.get((h - 2, NTG - 1)),
                           tok_xn if h < 2 else None], inc=True, sem=f"wg{b}")
            else:
                tok_wgl[h] = sch.op("Q", (lambda h: lambda e: e.dma_start(
                    out=wg_f32[:].rearrange("p (c j) -> p c j", j=HD),
                    in_=wgateT_ap[h].rearrange("(c p) j -> p c j", p=128)))(h),
                    waits=[tok_wg.get(h - 2),
                           tok_xn if h < 2 else None], inc=True, sem="wgf")
                tok_wg[h] = sch.op("V", (lambda h, b: lambda e: e.tensor_copy(
                    out=wg_sb[b][:], in_=wg_f32[:]))(h, b),
                    waits=[tok_wgl[h], tok_gmm.get((h - 2, NTG - 1)),
                           tok_xn if h < 2 else None], inc=True)

        def emit_gate_head(h):
            b = h % 2
            gb = 0
            for tg in range(NTG):
                gi4 = h * NTG + tg
                pbank = (pfA + pS)[gi4 % 4]

                def mmfn(h, b, tg, pbank):
                    def f(e):
                        for fc in range(NFC):
                            r = e.matmul(pbank[:],
                                     wg_sb[b][:, fc * HD:(fc + 1) * HD],
                                     xnT[:, fc * T + tg * 512: fc * T + tg * 512 + 512],
                                     start=(fc == 0), stop=(fc == NFC - 1))
                        return r
                    return f
                tok_gmm[(h, tg)] = sch.op("T", mmfn(h, b, tg, pbank),
                    waits=[tok_wg[h], tok_xn,
                           tok_gst.get(gi4 - 4)], inc=True)
                tok_gst[h * NTG + tg] = sch.op("S",
                    (lambda h, gb, tg, pbank: lambda e: e.activation(
                        out=gate_st[gb][:, tg * 512:(tg + 1) * 512], in_=pbank[:],
                        func=AF.Sigmoid))(h, gb, tg, pbank),
                    waits=[tok_gmm[(h, tg)], tok_gsd.get(h - 1)], inc=True)
            tok_gsd[h] = sch.op("Q", (lambda h, gb: lambda e: e.dma_start(
                out=gate_dram[h], in_=gate_st[gb][:]))(h, gb),
                waits=[tok_gst[h * NTG + NTG - 1]], inc=True, sem="gsd0")

        HD2 = D // 2

        def emit_pn_chunk(ic):
            prev_cast = tok_wchunk.get(ic - 1)
            for n in range(NBITS):
                for hf in range(2):
                    ui = ic * 8 + n * 2 + hf
                    bg = ui % 4
                    bq = 4 + ui % 4
                    sp = ui % 2
                    tok_pn[(ui, 0)] = sch.op("G",
                        (lambda ic, n, hf, bg: lambda e: e.dma_start(
                            out=pnbg[bg][:],
                            in_=pwT_ap[n, ic * 128:(ic + 1) * 128,
                                       hf * HD2:(hf + 1) * HD2]))(ic, n, hf, bg),
                        waits=[tok_sig.get((ui - 4, 0))],
                        inc=True, sem=f"pn{bg}")
                    tok_pn[(ui, 1)] = sch.op("G",
                        (lambda ic, n, hf, bq: lambda e: e.dma_start(
                            out=pnbg[bq][:],
                            in_=nwT_ap[n, ic * 128:(ic + 1) * 128,
                                       hf * HD2:(hf + 1) * HD2]))(ic, n, hf, bq),
                        waits=[tok_sig.get((ui - 4, 1))],
                        inc=True, sem=f"pn{bq}")
                    tok_sig[(ui, 0)] = sch.op("S",
                        (lambda ui, bg, sp: lambda e: e.activation(
                            out=sgp[sp][:], in_=pnbg[bg][:], func=AF.Sigmoid))(ui, bg, sp),
                        waits=[tok_pn[(ui, 0)], tok_comb.get((ui - 2, 1))], inc=True)
                    ws = hf * HD2
                    tok_sig[(ui, 1)] = sch.op("S",
                        (lambda ui, bq, sp: lambda e: e.activation(
                            out=sgn[sp][:], in_=pnbg[bq][:], func=AF.Sigmoid))(ui, bq, sp),
                        waits=[tok_pn[(ui, 1)], tok_comb.get((ui - 2, 0))], inc=True)
                    tok_comb[(ui, 0)] = sch.op("V",
                        (lambda ui, sp: lambda e: e.tensor_sub(
                            out=sgp[sp][:], in0=sgp[sp][:], in1=sgn[sp][:]))(ui, sp),
                        waits=[tok_sig[(ui, 0)], tok_sig[(ui, 1)]],
                        hard=[tok_comb.get((ui - 2, 1))], inc=True)
                    if n == 0:
                        tok_comb[(ui, 1)] = sch.op("V",
                            (lambda ui, ws, sp: lambda e: e.tensor_scalar_mul(
                                out=Wacc[:, ws:ws + HD2], in0=sgp[sp][:],
                                scalar1=cpos[:, 0:1]))(ui, ws, sp),
                            hard=[tok_comb[(ui, 0)], tcneg, prev_cast], inc=True)
                    else:
                        tok_comb[(ui, 1)] = sch.op("V",
                            (lambda ui, n, ws, sp: lambda e: e.scalar_tensor_tensor(
                                out=Wacc[:, ws:ws + HD2], in0=sgp[sp][:],
                                scalar=cpos[:, n:n + 1],
                                in1=Wacc[:, ws:ws + HD2],
                                op0=ALU.mult, op1=ALU.add))(ui, n, ws, sp),
                            hard=[tok_comb[(ui, 0)], tok_comb.get((ui - 1, 1))], inc=True)
            tok_wchunk[ic] = sch.op("V", (lambda ic: lambda e: e.tensor_copy(
                out=WffnT[:, ic * D:(ic + 1) * D], in_=Wacc[:]))(ic),
                waits=[tok_xn],
                hard=[tok_comb[(ic * 8 + 7, 1)]], inc=True)

        emit_pn_chunk(0)
        emit_pn_chunk(1)
        emit_wg_load(0)
        emit_wg_load(1)
        for step in range(NH):
            emit_gate_head(step)
            if step + 2 < NH:
                emit_wg_load(step + 2)
            if step + 2 < NFC:
                emit_pn_chunk(step + 2)
        tok_wffn = tok_wchunk[NFC - 1]
        tok_last_gate_ev = tok_gst[NH * NTG - 1]

        tok_gev = {}
        tok_gse = {}
        for fc in range(NFC):
            for tg in range(NTG):
                gidx = fc * NTG + tg
                pbank = (pfA + pS)[gidx % 4]

                def mmfn(fc, tg, pbank):
                    def f(e):
                        for ic in range(NFC):
                            r = e.matmul(pbank[:],
                                     WffnT[:, ic * D + fc * 128: ic * D + fc * 128 + 128],
                                     xnT[:, ic * T + tg * 512: ic * T + tg * 512 + 512],
                                     start=(ic == 0), stop=(ic == NFC - 1))
                        return r
                    return f
                tmm_ = sch.op("T", mmfn(fc, tg, pbank),
                    waits=[tok_wffn, tok_gse.get(gidx - 4),
                           tok_last_gate_ev if gidx < 4 else None], inc=True)
                bse = gidx % 2
                tok_gse[gidx] = sch.op("S",
                    (lambda fc, tg, pbank, bse: lambda e: e.activation(
                        out=g_st[bse][:], in_=pbank[:], func=AF.Silu,
                        bias=biasW[:, fc:fc + 1]))(fc, tg, pbank, bse),
                    waits=[tmm_, tbias, tok_gev.get(gidx - 2)], inc=True)
                tok_gev[gidx] = sch.op("Q",
                    (lambda fc, tg, bse: lambda e: e.dma_start(
                        out=g_dram[fc, :, tg * 512:(tg + 1) * 512], in_=g_st[bse][:]))(fc, tg, bse),
                    waits=[tok_gse[gidx]], inc=True, sem=f"gev{bse}")
        tok_ffn_done = [tok_gev[NFC * NTG - 1], tok_gev[NFC * NTG - 2],
                        tok_gse[NFC * NTG - 1]]

    if upto == "BG":
        fin = sch.op("G", lambda e: e.dma_start(
            out=outT_ap[0:128, 0:512], in_=xnT[:, 0:512]),
            waits=tok_ffn_done + [tok_gsd[NH - 1]],
            inc=True, sem="fin")
        sch.op("V", lambda e: e.nop(), waits=[fin])
        return _finish(nc, sch, es, esABC)

    esPN.close()

    with contextlib.ExitStack() as esC:
        def sbC(name, shape, dtype):
            return esC.enter_context(nc.sbuf_tensor(name, list(shape), dtype))

        wq_sb = [wq_sb0, sbC("wq_sb1", (128, NFC * 384), BF16)]
        gate_h = [sbC(f"gate_h{i}", (128, T), BF16) for i in range(2)]
        q_dim = [sbC(f"q_dim{i}", (128, T), BF16) for i in range(2)]
        k_dim = [sbC(f"k_dim{i}", (128, T), BF16) for i in range(2)]
        v_dimb = sbC("v_dim", (128, T), BF16)
        ktok_raw = sbC("ktok_raw", (128, T), BF16)
        k_td = [sbC(f"k_td{i}", (128, T), BF16) for i in range(2)]
        v_tok = [sbC(f"v_tok{i}", (128, T), BF16) for i in range(2)]
        qdec_b = sbC("qdec_b", (128, 512), BF16)
        sT_b = sbC("sT_b", (128, 1024), BF16)
        sq_b = sbC("sq_b", (128, 512), BF16)
        kv_sb = sbC("kv_sb", (128, 128), F32)
        kv_bf = sbC("kv_bf", (128, 128), BF16)
        O_st = [sbC(f"O_st{i}", (128, T), BF16) for i in range(2)]
        qdec_all = sbC("qdec_all", (128, NH * BLK), BF16)
        decT_all = sbC("decT_all", (128, NH * 2 * BLK), BF16)
        kdec_all = sbC("kdec_all", (128, 2 * NH), F32)
        bd_all = sbC("bd_all", (128, NH), F32)
        dtmp = sbC("dtmp", (128, 2 * BLK), F32)

        tok_wq = {}
        tok_gld = {}
        tok_pev = {}
        tok_tr = {}
        tok_ktd = {}
        tok_vev = {}
        tok_qdecb = {}
        tok_og = {}
        tok_kvstt = {}
        tok_kvcast = {}
        tok_ssacc = {}
        tok_sq = {}
        tok_ost = {}
        tok_prod = {}
        tok_tst = {}

        tvjoinC = sch.op("V", lambda e: e.tensor_copy(
            out=join_sb[0:1, 0:1], in_=ones1[0:1, 0:1]),
            waits=tok_ffn_done + [tok_gsd[NH - 1]],
            inc=True)

        tok_decay = None
        tv_ = None
        ts3 = None
        for h in range(NH):
            w0 = [t_cst, tvjoinC, ts3,
                  tok_gse[NFC * NTG - 1] if h == 0 else None]
            tv_ = sch.op("V", (lambda h: lambda e: e.tensor_scalar_mul(
                out=dtmp[:, 0:BLK], in0=rampq_sb, scalar1=slope_sb[:, h:h + 1]))(h),
                waits=w0, hard=[tv_], inc=True)
            ts1 = sch.op("S", (lambda h: lambda e: e.activation(
                out=qdec_all[:, h * BLK:(h + 1) * BLK], in_=dtmp[:, 0:BLK],
                func=AF.Exp, scale=-1.0))(h), waits=[tv_], inc=True)
            tv_ = sch.op("V", (lambda h: lambda e: e.tensor_scalar_mul(
                out=dtmp[:, 0:2 * BLK], in0=rampmn_sb,
                scalar1=slope_sb[:, h:h + 1]))(h), waits=[ts1], inc=True)
            ts2 = sch.op("S", (lambda h: lambda e: e.activation(
                out=decT_all[:, h * 2 * BLK:(h + 1) * 2 * BLK], in_=dtmp[:, 0:2 * BLK],
                func=AF.Exp, scale=-1.0))(h), waits=[tv_], inc=True)
            tv_ = sch.op("V", (lambda h: lambda e: e.tensor_scalar_mul(
                out=dtmp[:, 0:2], in0=rampk_sb, scalar1=slope_sb[:, h:h + 1]))(h),
                waits=[ts2], inc=True)
            ts3 = sch.op("S", (lambda h: lambda e: e.activation(
                out=kdec_all[:, 2 * h:2 * h + 2], in_=dtmp[:, 0:2],
                func=AF.Exp, scale=-1.0))(h), waits=[tv_], inc=True)
            tok_decay = sch.op("S", (lambda h: lambda e: e.activation(
                out=bd_all[:, h:h + 1], in_=slope_sb[:, h:h + 1],
                func=AF.Exp, scale=-256.0))(h), inc=True)

        tok_wq[0] = tok_wq0
        tok_gld[0] = sch.op("Q", lambda e: e.dma_start(
            out=gate_h[0][:], in_=gate_dram[0]),
            waits=tok_ffn_done + [tok_gsd[0], tok_gsd[NH - 1]],
            inc=True, sem="gld0")
        if c_heads > 1:
            tok_gld[1] = sch.op("Q", lambda e: e.dma_start(
                out=gate_h[1][:], in_=gate_dram[1]),
                waits=tok_ffn_done + [tok_gsd[1], tok_gsd[NH - 1]],
                inc=True, sem="gld1")
        tkv0 = {}

        def emit_prod_group(h, sel, tg):
            hb = h % 2
            dests = (q_dim[hb], k_dim[hb], v_dimb)
            gidx = (h * 3 + sel) * NTG + tg
            par = gidx % 2

            def mmfn(hb, sel, tg, par):
                def f(e):
                    r = None
                    for fc in range(NFC):
                        r = e.matmul(pfA[par][:],
                                     wq_sb[hb][:, fc * 384 + sel * 128: fc * 384 + (sel + 1) * 128],
                                     xnT[:, fc * T + tg * 512: fc * T + tg * 512 + 512],
                                     start=(fc == 0), stop=(fc == NFC - 1))
                    return r
                return f
            wmm = [tok_wq[h], tok_pev.get((gidx - 2, "g"))]
            wmm.extend(tok_ffn_done)
            if gidx < 2:
                wmm.append(tok_gse.get(NFC * NTG - 1))
            if sel == 0:
                wmm.append(tok_qdecb.get((h - 2, NB - 1)))
            if sel == 2:
                wmm.append(tok_tr.get(h - 1))
            tmm_ = sch.op("T", mmfn(hb, sel, tg, par), waits=wmm, inc=True)
            tok_pev[(gidx, "g")] = sch.op("S",
                (lambda d, tg, par: lambda e: e.activation(
                    out=d[:, tg * 512:(tg + 1) * 512], in_=pfA[par][:],
                    func=AF.Silu))(dests[sel], tg, par),
                waits=[tmm_], inc=True)
            if tg == NTG - 1:
                tok_pev[(h, sel)] = tok_pev[(gidx, "g")]
                if sel == 2:
                    tok_prod[h] = tok_pev[(h, 2)]

        def emit_transposes(h):
            hb = h % 2

            def trfn(srcbuf, b, base):
                def f(e):
                    r = None
                    for s in range(8):
                        r = e.transpose(pT[b][:, s * 128:(s + 1) * 128],
                                        srcbuf[:, (base + s) * 128:(base + s + 1) * 128],
                                        id_bf[:])
                    return r
                return f
            tev_prev = None
            for half in range(2):
                tk = sch.op("T", trfn(k_dim[hb], half, half * 8),
                    waits=[tok_pev[(h, 1)], t_idbf, tok_ktd.get((h - 1, half)),
                           tok_vev.get((h - 1, half)), tev_prev], inc=True)
                tev_prev = sch.op("V", (lambda half: lambda e: e.tensor_copy(
                    out=ktok_raw[:, half * 1024:(half + 1) * 1024], in_=pT[half][:]))(half),
                    waits=[tk, tvjoinC], inc=True)
            for half in range(2):
                tk = sch.op("T", trfn(v_dimb, half, half * 8),
                    waits=[tok_pev[(h, 2)], tev_prev], inc=True)
                tok_vev[(h, half)] = sch.op("V",
                    (lambda half, hb: lambda e: e.tensor_copy(
                        out=v_tok[hb][:, half * 1024:(half + 1) * 1024], in_=pT[half][:]))(half, hb),
                    waits=[tk], hard=[tok_kvstt.get((h - 2, NB - 1))], inc=True)
                tev_prev = tok_vev[(h, half)]
            tok_tr[h] = tk

            kr = ktok_raw[:].rearrange("p (t e) -> p t e", e=128)
            kdr = k_td[hb][:].rearrange("p (t e) -> p t e", e=128)
            for par_t in range(2):
                vi = kr[:, par_t:NTT:2, :]
                vo = kdr[:, par_t:NTT:2, :]
                tok_ktd[(h, par_t)] = sch.op("V",
                    (lambda vi, vo, h, par_t: lambda e: e.tensor_scalar_mul(
                        out=vo, in0=vi,
                        scalar1=kdec_all[:, 2 * h + par_t:2 * h + par_t + 1]))(vi, vo, h, par_t),
                    waits=[tok_decay],
                    hard=[tev_prev, tok_kvstt.get((h - 2, NB - 1))], inc=True)

        def emit_scan_a(h, j):
            hb = h % 2
            jp = j % 2
            if j == 0:
                tkv0[h] = sch.op("V", lambda e: e.memset(kv_sb[:], 0.0),
                    hard=[tok_kvcast.get((h - 1, NB - 1))], inc=True)
            tok_qdecb[(h, j)] = sch.op("V",
                (lambda h, j, jp, hb: lambda e: e.tensor_mul(
                    out=qdec_b[:, jp * 256:(jp + 1) * 256],
                    in0=q_dim[hb][:, j * 256:(j + 1) * 256],
                    in1=qdec_all[:, h * BLK:(h + 1) * BLK]))(h, j, jp, hb),
                waits=[tok_pev[(h, 0)], tok_decay], inc=True)

            def scfn(hb, j, jp):
                def f(e):
                    r = None
                    for nt in range(2):
                        r = e.matmul(pS[jp][:, nt * 256:(nt + 1) * 256],
                                     k_dim[hb][:, (2 * j + nt) * 128:(2 * j + nt + 1) * 128],
                                     q_dim[hb][:, j * 256:(j + 1) * 256],
                                     start=True, stop=True)
                    return r
                return f
            tsc = sch.op("T", scfn(hb, j, jp),
                waits=[tok_pev[(h, 1)], tok_pev[(h, 0)],
                       tok_kvstt.get((h, j - 2)), tok_ssacc.get((h, j - 2)),
                       tok_kvstt.get((h - 1, NB - 1)) if j < 2 else None,
                       tok_ssacc.get((h - 1, NB - 1)) if j < 2 else None],
                inc=True)
            tst = sch.op("V", (lambda h, jp: lambda e: e.tensor_tensor(
                out=sT_b[:, jp * 512:(jp + 1) * 512], in0=pS[jp][:],
                in1=decT_all[:, h * 512:(h + 1) * 512], op=ALU.mult))(h, jp),
                waits=[tsc], inc=True)
            tok_tst[(h, j)] = tst

        def emit_scan_b(h, j):
            hb = h % 2
            jp = j % 2
            tst = tok_tst[(h, j)]

            def ofn(h, j, jp, hb):
                def f(e):
                    r = None
                    first = True
                    if j > 0:
                        r = e.matmul(pO[jp][:, 0:256], kv_bf[:],
                                     qdec_b[:, jp * 256:(jp + 1) * 256],
                                     start=True, stop=False)
                        first = False
                    for nt in range(2):
                        r = e.matmul(pO[jp][:, 0:256],
                                     v_tok[hb][:, (2 * j + nt) * 128:(2 * j + nt + 1) * 128],
                                     sT_b[:, jp * 512 + nt * 256: jp * 512 + (nt + 1) * 256],
                                     start=first, stop=(nt == 1))
                        first = False
                    return r
                return f
            tog = sch.op("T", ofn(h, j, jp, hb),
                waits=[tst, tok_qdecb[(h, j)],
                       tok_kvcast.get((h, j - 1)),
                       tok_vev.get((h, 1)),
                       tok_sq.get((h, j - 2)), tok_og.get((h, j - 2)),
                       tok_sq.get((h - 1, NB - 1)) if j < 2 else None,
                       tok_og.get((h - 1, NB - 1)) if j < 2 else None],
                inc=True)

            def kvfn(h, j, jp, hb):
                def f(e):
                    r = None
                    for nt in range(2):
                        r = e.matmul(pS[jp][:, 0:128],
                                     k_td[hb][:, (2 * j + nt) * 128:(2 * j + nt + 1) * 128],
                                     v_tok[hb][:, (2 * j + nt) * 128:(2 * j + nt + 1) * 128],
                                     start=(nt == 0), stop=(nt == 1))
                    return r
                return f
            tkv = sch.op("T", kvfn(h, j, jp, hb),
                waits=[tok_ktd.get((h, 1)), tok_ktd.get((h, 0))], inc=True)
            tok_sq[(h, j)] = sch.op("S", (lambda jp: lambda e: e.activation(
                out=sq_b[:, jp * 256:(jp + 1) * 256], in_=pO[jp][:, 0:256],
                func=AF.Square))(jp), waits=[tog], inc=True)
            tss = sch.op("T", (lambda jp: lambda e: e.matmul(
                pS[jp][0:1, 128:384], ones_bf[:],
                sq_b[:, jp * 256:(jp + 1) * 256], start=True, stop=True))(jp),
                waits=[tok_sq[(h, j)], tkv], inc=True)
            tok_og[(h, j)] = sch.op("V",
                (lambda h, j, jp, hb: lambda e: e.scalar_tensor_tensor(
                    out=O_st[hb][:, j * 256:(j + 1) * 256], in0=pO[jp][:, 0:256],
                    scalar=normw_sb[:, h:h + 1],
                    in1=gate_h[hb][:, j * 256:(j + 1) * 256],
                    op0=ALU.mult, op1=ALU.mult))(h, j, jp, hb),
                waits=[tog, t_cst, tok_ost.get(h - 2), tok_sq[(h, j)],
                       tok_gld[h]], inc=True)
            tok_kvstt[(h, j)] = sch.op("V",
                (lambda h, jp: lambda e: e.scalar_tensor_tensor(
                    out=kv_sb[:], in0=kv_sb[:], scalar=bd_all[:, h:h + 1],
                    in1=pS[jp][:, 0:128], op0=ALU.mult, op1=ALU.add))(h, jp),
                waits=[tss], hard=[tkv0[h]], inc=True)
            tok_kvcast[(h, j)] = sch.op("V", lambda e: e.tensor_copy(
                out=kv_bf[:], in_=kv_sb[:]),
                hard=[tok_kvstt[(h, j)]], inc=True)
            tok_ssacc[(h, j)] = sch.op("V",
                (lambda j, jp: lambda e: e.tensor_tensor(
                    out=sumsq[0:1, j * 256:(j + 1) * 256],
                    in0=sumsq[0:1, j * 256:(j + 1) * 256],
                    in1=pS[jp][0:1, 128:384], op=ALU.add))(j, jp),
                waits=[tss], inc=True)

        for it in range(c_heads + 1):
            ph = it if it < c_heads else None
            sh = it - 1
            if ph is not None and ph + 1 < c_heads:
                tok_wq[ph + 1] = sch.op("G", (lambda ph: lambda e: e.dma_start(
                    out=wq_sb[(ph + 1) % 2][:].rearrange("p (c j) -> p c j", j=384),
                    in_=wqkvT_ap[ph + 1].rearrange("(c p) j -> p c j", p=128)))(ph),
                    waits=[tok_prod.get(ph - 1),
                           tok_ffn_done[0] if ph == 0 else None],
                    inc=True, sem=f"wq{(ph + 1) % 2}")
            gitems = ([("p", s, t) for s in range(3) for t in range(NTG)]
                      if ph is not None else [])
            bitems = ([x for j in range(NB) for x in (("a", j), ("b", j))]
                      if sh >= 0 else [])
            merged = []
            gi = bi = 0
            while gi < len(gitems) or bi < len(bitems):
                if bi >= len(bitems) or (gi < len(gitems) and
                                         gi * (len(bitems) + 1) <= bi * (len(gitems) + 1)):
                    merged.append(gitems[gi]); gi += 1
                else:
                    merged.append(bitems[bi]); bi += 1
            for item in merged:
                if item[0] == "p":
                    emit_prod_group(ph, item[1], item[2])
                elif item[0] == "a":
                    emit_scan_a(sh, item[1])
                else:
                    emit_scan_b(sh, item[1])
            if sh >= 0:
                hb = sh % 2
                tok_ost[sh] = sch.op("Q", (lambda sh, hb: lambda e: e.dma_start(
                    out=o_dram[sh], in_=O_st[hb][:]))(sh, hb),
                    waits=[tok_og[(sh, NB - 1)]], inc=True, sem=f"ost{hb}")
                if sh + 2 < c_heads:
                    tok_gld[sh + 2] = sch.op("Q", (lambda sh: lambda e: e.dma_start(
                        out=gate_h[sh % 2][:], in_=gate_dram[sh + 2]))(sh),
                        waits=[tok_gsd[sh + 2], tok_og[(sh, NB - 1)]],
                        inc=True, sem=f"gld{sh % 2}")
            if ph is not None:
                emit_transposes(ph)

        tok_c_done = [tok_ost.get(c_heads - 1), tok_ost.get(c_heads - 2),
                      tok_decay]
        tok_c_v = tok_ssacc.get((NH - 1, NB - 1))
        tok_c_s = tok_sq.get((NH - 1, NB - 1))

    if upto == "C":
        fin = sch.op("G", lambda e: e.dma_start(
            out=outT_ap[0:128, 0:512], in_=xnT[:, 0:512]),
            waits=tok_c_done + [tok_c_v], inc=True, sem="fin")
        sch.op("V", lambda e: e.nop(), waits=[fin])
        return _finish(nc, sch, es, esABC)

    esABC.close()

    with contextlib.ExitStack() as esD:
        def sbD(name, shape, dtype):
            return esD.enter_context(nc.sbuf_tensor(name, list(shape), dtype))

        wout_sb = sbD("wout_sb", (128, NFC * D), BF16)
        O_sb = sbD("O_sb", (128, NH * T), BF16)
        rsq_rep = sbD("rsq_rep", (128, T), F32)
        g_fc = [sbD(f"g_fc{i}", (128, T), BF16) for i in range(2)]
        xT_fc = [sbD(f"xT_fc{i}", (128, T), F32) for i in range(2)]
        s1 = [sbD(f"s1_{i}", (128, 512), F32) for i in range(2)]
        outb = [sbD(f"outb{i}", (128, 512), F32) for i in range(2)]
        rtmp = sbD("rtmp", (1, T), F32)
        rvec = sbD("rvec", (1, T), F32)
        rtmp2 = sumsq

        tjoin = sch.op("V", lambda e: e.tensor_copy(
            out=join_sb[0:1, 1:2], in_=ones1[0:1, 0:1]), waits=tok_c_done, inc=True)
        tsjoin = sch.op("S", lambda e: e.copy(
            out=join_sb[0:1, 2:3], in_=ones1[0:1, 0:1]), waits=[tok_c_s], inc=True)

        tok_wout = {}
        for fc in range(NFC):
            tok_wout[fc] = sch.op("G", (lambda fc: lambda e: e.dma_start(
                out=wout_sb[:, fc * NH * 128:(fc + 1) * NH * 128].rearrange(
                    "p (c f) -> p c f", f=128),
                in_=woutT_ap[:, fc * 128:(fc + 1) * 128].rearrange(
                    "(c p) f -> p c f", p=128)))(fc),
                waits=[tok_prod[NH - 1], tjoin, tsjoin] if fc == 0 else [],
                inc=True, sem=f"wou{fc}")
        tok_oload = sch.op("Q", lambda e: e.dma_start(
            out=O_sb[:].rearrange("p (c t) -> p c t", t=T),
            in_=o_dram.rearrange("c p t -> p c t")),
            waits=tok_c_done + [tjoin, tsjoin], inc=True, sem="olo")

        tsr2 = sch.op("S", lambda e: e.activation(
            out=rtmp[:], in_=sumsq[:], func=AF.Sqrt, scale=1.0 / D,
            bias=eps_sb[0:1, 1:2]), waits=[tok_c_v, tsjoin], inc=True)
        tn = sch.op("V", lambda e: e.reciprocal(out=rvec[:], in_=rtmp[:]),
                    waits=[tsr2], hard=[tjoin], inc=True)
        tn = sch.op("V", lambda e: e.tensor_scalar(
            out=rtmp2[:], in0=sumsq[:], scalar1=1.0 / D, scalar2=1e-6,
            op0=ALU.mult, op1=ALU.add), waits=[tsr2], hard=[tn], inc=True)
        tn = sch.op("V", lambda e: e.tensor_mul(
            out=rtmp[:], in0=rvec[:], in1=rvec[:]), hard=[tn], inc=True)
        tn = sch.op("V", lambda e: e.tensor_mul(
            out=rtmp[:], in0=rtmp[:], in1=rtmp2[:]), hard=[tn], inc=True)
        tn = sch.op("V", lambda e: e.tensor_scalar(
            out=rtmp[:], in0=rtmp[:], scalar1=-0.5, scalar2=1.5,
            op0=ALU.mult, op1=ALU.add), hard=[tn], inc=True)
        trv = sch.op("V", lambda e: e.tensor_mul(
            out=rvec[:], in0=rvec[:], in1=rtmp[:]), hard=[tn], inc=True)
        trep2 = {}
        for g in range(NTG):
            par = g % 2
            ta = sch.op("T", (lambda g, par: lambda e: e.matmul(
                pO[par][:, 0:512], ones1[:], rvec[0:1, g * 512:(g + 1) * 512],
                start=True, stop=True))(g, par),
                waits=[trv, tok_c_v, trep2.get(g - 2)], inc=True)
            trep2[g] = sch.op("V", (lambda g, par: lambda e: e.tensor_copy(
                out=rsq_rep[:, g * 512:(g + 1) * 512], in_=pO[par][:, 0:512]))(g, par),
                waits=[ta], inc=True)
        tok_rsq = trep2[NTG - 1]

        tok_gl = {}
        tok_xl = {}
        tok_out = {}
        tok_ev3 = {}
        for fc in range(NFC):
            fb = fc % 2
            tok_gl[fc] = sch.op("Q", (lambda fc, fb: lambda e: e.dma_start(
                out=g_fc[fb][:], in_=g_dram[fc]))(fc, fb),
                waits=[tok_ev3.get((fc - 2, NTG - 1)), tjoin, tsjoin] + tok_ffn_done,
                inc=True, sem=f"gl{fb}")
            tok_xl[fc] = sch.op("Q", (lambda fc, fb: lambda e: e.dma_start(
                out=xT_fc[fb][:], in_=xT_ap[fc * 128:(fc + 1) * 128, :]))(fc, fb),
                waits=[tok_ev3.get((fc - 2, NTG - 1)), tjoin, tsjoin],
                inc=True, sem=f"xl{fb}")
            for tg in range(NTG):
                gidx = fc * NTG + tg
                pbank = (pfA + pS)[gidx % 4]
                par = gidx % 2

                def mmfn(fc, tg, pbank):
                    def f(e):
                        base = fc * NH * 128
                        for oc in range(NH):
                            r = e.matmul(pbank[:],
                                     wout_sb[:, base + oc * 128: base + (oc + 1) * 128],
                                     O_sb[:, oc * T + tg * 512: oc * T + tg * 512 + 512],
                                     start=(oc == 0), stop=(oc == NH - 1))
                        return r
                    return f
                tmm_ = sch.op("T", mmfn(fc, tg, pbank),
                    waits=[tok_wout[fc], tok_oload, tok_ev3.get((gidx - 4, "s"))], inc=True)
                tv1 = sch.op("V", (lambda par, tg, pbank: lambda e: e.tensor_tensor(
                    out=s1[par][:], in0=pbank[:],
                    in1=rsq_rep[:, tg * 512:(tg + 1) * 512], op=ALU.mult))(par, tg, pbank),
                    waits=[tmm_],
                    hard=[tok_rsq, tok_ev3.get((gidx - 2, "v3"))], inc=True)
                tok_ev3[(gidx, "s")] = tv1
                tv2 = sch.op("V", (lambda par, fb, tg: lambda e: e.tensor_tensor(
                    out=s1[par][:], in0=s1[par][:],
                    in1=g_fc[fb][:, tg * 512:(tg + 1) * 512], op=ALU.mult))(par, fb, tg),
                    waits=[tok_gl[fc]], hard=[tv1], inc=True)
                tv3 = sch.op("V", (lambda par, fb, tg: lambda e: e.tensor_tensor(
                    out=outb[par][:], in0=s1[par][:],
                    in1=xT_fc[fb][:, tg * 512:(tg + 1) * 512], op=ALU.add))(par, fb, tg),
                    waits=[tok_xl[fc], tok_out.get(gidx - 2)], hard=[tv2], inc=True)
                tok_ev3[(fc, tg)] = tv3
                tok_ev3[(gidx, "v3")] = tv3
                tok_out[gidx] = sch.op("Q",
                    (lambda fc, tg, par: lambda e: e.dma_start(
                        out=outT_ap[fc * 128:(fc + 1) * 128, tg * 512:(tg + 1) * 512],
                        in_=outb[par][:]))(fc, tg, par),
                    waits=[tv3], inc=True, sem=f"out{par}")

        finals = [tok_out[NFC * NTG - 1], tok_out[NFC * NTG - 2]]
        sch.op("Q", lambda e: e.nop(), waits=finals)
        sch.op("V", lambda e: e.nop(), waits=finals)

    return _finish(nc, sch, es, esABC)


def _finish(nc, sch, es, esABC):
    semstack = contextlib.ExitStack()
    sems = {}
    for name in sorted(sch.sem_names):
        sems[name] = semstack.enter_context(nc.semaphore(f"sem_{name}"))
    with nc.Block() as block:
        @block.tensor
        def _(eng):
            sch.replay("T", eng, sems)

        @block.vector
        def _(eng):
            sch.replay("V", eng, sems)

        @block.scalar
        def _(eng):
            sch.replay("S", eng, sems)

        @block.sync
        def _(eng):
            sch.replay("Q", eng, sems)

        @block.gpsimd
        def _(eng):
            sch.replay("G", eng, sems)

    semstack.close()
    try:
        esABC.close()
    except Exception:
        pass
    es.close()
    return nc


def host_inputs(inputs):
    x = np.asarray(inputs["L0_attn_hidden_states"], dtype=np.float32)
    slope = np.asarray(inputs["L0_attn_slope_rate"], dtype=np.float32).reshape(NH)
    w_qkv = np.asarray(inputs["L0_attn_w_qkv"], dtype=np.float32)
    w_gate = np.asarray(inputs["L0_attn_w_gate"], dtype=np.float32)
    w_out = np.asarray(inputs["L0_attn_w_out"], dtype=np.float32)
    norm_w = np.asarray(inputs["L0_attn_norm_w"], dtype=np.float32)
    pweight = np.asarray(inputs["L0_ffn_pweight"], dtype=np.float32)
    nweight = np.asarray(inputs["L0_ffn_nweight"], dtype=np.float32)
    exps = np.asarray(inputs["L0_ffn_exps"], dtype=np.float32)
    bexps = np.asarray(inputs["L0_ffn_bexps"], dtype=np.float32)
    mask_w = np.asarray(inputs["L0_ffn_mask_weight"], dtype=np.float32)
    scale = np.asarray(inputs["L0_ffn_scale"], dtype=np.float32)
    pbias = np.asarray(inputs["L0_ffn_pbias"], dtype=np.float32)
    nbias = np.asarray(inputs["L0_ffn_nbias"], dtype=np.float32)
    bscale = np.asarray(inputs["L0_ffn_biasscale"], dtype=np.float32)

    C = np.ascontiguousarray
    consts = np.zeros((128, CW), dtype=np.float32)

    def setc(name, val):
        a, b = _CO[name]
        consts[:, a:b] = val

    p = np.arange(128, dtype=np.float32)
    m = np.arange(BLK, dtype=np.float32).reshape(1, BLK)
    setc("id", np.eye(128, dtype=np.float32))
    setc("rampq", np.arange(1, BLK + 1, dtype=np.float32).reshape(1, BLK))
    grids = []
    for tile in range(2):
        n = (p + 128 * tile).reshape(128, 1)
        g = m - n
        grids.append(np.where(g >= 0, g, np.float32(1e30)).astype(np.float32))
    setc("rampmn", np.concatenate(grids, axis=1))
    setc("rampk", np.stack([255.0 - p, 127.0 - p], axis=1))
    setc("normw", norm_w.reshape(NH, 128).T)
    setc("slope", slope.reshape(1, NH))
    setc("pb2", pbias.reshape(NH, 128, NBITS).transpose(1, 0, 2).reshape(128, NH * NBITS))
    setc("nb2", nbias.reshape(NH, 128, NBITS).transpose(1, 0, 2).reshape(128, NH * NBITS))
    setc("exps", exps.reshape(1, NBITS))
    setc("mw", mask_w.reshape(1, NBITS))
    setc("bexps", bexps.reshape(1, NBITS))
    setc("scale", scale.reshape(1, 1))
    setc("bscale", bscale.reshape(1, 1))

    shared = {
        "wqkvT": C(w_qkv.reshape(NH, 384, D).transpose(0, 2, 1)),
        "wgateT": C(w_gate.reshape(NH, HD, D).transpose(0, 2, 1)),
        "woutT": C(w_out.T),
        "pwT": C(pweight.transpose(2, 1, 0)),
        "nwT": C(nweight.transpose(2, 1, 0)),
        "consts": consts,
    }
    in_maps = []
    for c in range(NCORES):
        im = dict(shared)
        im["x"] = C(x[c])
        im["xT"] = C(x[c].T)
        in_maps.append(im)
    return in_maps


def kernel(**inputs):
    in_maps = host_inputs(inputs)
    nc = build_program()
    res = run_bass_kernel_spmd(nc, in_maps, core_ids=list(range(NCORES)))
    out = np.stack([np.ascontiguousarray(r["outT"].T) for r in res.results])
    return out
